# version 06
# baseline (speedup 1.0000x reference)
import contextlib
import numpy as np
import concourse.bass as bass
import concourse.mybir as mybir
from concourse.bass_utils import run_bass_kernel_spmd

F32 = mybir.dt.float32
BF16 = mybir.dt.bfloat16
U32 = mybir.dt.uint32
AF = mybir.ActivationFunctionType
ALU = mybir.AluOpType
AX = mybir.AxisListType

D = 2048
TC = 2048
KT = 16
TT = 16
NB = 4
BLK = 512
E = 8
IM = 1024
ISH = 4096
EPS = 1e-5


class PS:

    def __init__(self, en, nc, name, ways=2):
        self.s = [en(nc.semaphore(f"{name}_{i}")) for i in range(ways)]
        self.c = [0] * ways
        self.ways = ways
        self.n = 0
        self.mark = {}

    def reserve(self, key):
        assert key not in self.mark
        w = self.n % self.ways
        self.n += 1
        self.c[w] += 16
        self.mark[key] = (w, self.c[w])

    def inc(self, inst, key):
        w, _ = self.mark[key]
        inst.then_inc(self.s[w], 16)

    def wait(self, eng, key):
        w, v = self.mark[key]
        eng.wait_ge(self.s[w], v)

    def wait_all(self, eng):
        for w in range(self.ways):
            if self.c[w]:
                eng.wait_ge(self.s[w], self.c[w])


def build_nc(tc=2048, blk=512, debug=False):
    global TC, TT, NB, BLK
    TC, BLK = tc, blk
    TT, NB = TC // 128, TC // BLK

    nc = bass.Bass("TRN2", target_bir_lowering=False)

    x_in = nc.dram_tensor("x", [TC, D], F32, kind="ExternalInput").ap()
    sel_in = nc.dram_tensor("sel", [3, 8], F32, kind="ExternalInput").ap()
    A_in = nc.dram_tensor("A", [D, 64], F32, kind="ExternalInput").ap()
    B_in = nc.dram_tensor("B", [D, 64], F32, kind="ExternalInput").ap()
    Cw_in = nc.dram_tensor("C_w", [D, D], F32, kind="ExternalInput").ap()
    Cb_in = nc.dram_tensor("C_b", [D], F32, kind="ExternalInput").ap()
    Dw_in = nc.dram_tensor("D_w", [D, D], F32, kind="ExternalInput").ap()
    Db_in = nc.dram_tensor("D_b", [D], F32, kind="ExternalInput").ap()
    ld_in = nc.dram_tensor("log_delta", [D], F32, kind="ExternalInput").ap()
    wg_in = nc.dram_tensor("w_gate", [D, E], F32, kind="ExternalInput").ap()
    wgu_in = nc.dram_tensor("w_gate_up", [E, D, 2 * IM], F32, kind="ExternalInput").ap()
    wdn_in = nc.dram_tensor("w_down", [E, IM, D], F32, kind="ExternalInput").ap()
    wsg_in = nc.dram_tensor("w_sh_gate_up", [D, 2 * ISH], F32, kind="ExternalInput").ap()
    wsd_in = nc.dram_tensor("w_sh_down", [ISH, D], F32, kind="ExternalInput").ap()
    y_out = nc.dram_tensor("y", [TC, D], F32, kind="ExternalOutput").ap()

    cw_bf = nc.dram_tensor("cw_bf", [D, D], BF16).ap()
    dw_bf = nc.dram_tensor("dw_bf", [D, D], BF16).ap()
    wgu_bf = nc.dram_tensor("wgu_bf", [E, D, 2 * IM], BF16).ap()
    wdn_bf = nc.dram_tensor("wdn_bf", [E, IM, D], BF16).ap()
    wsg_bf = nc.dram_tensor("wsg_bf", [D, 2 * ISH], BF16).ap()
    wsd_bf = nc.dram_tensor("wsd_bf", [ISH, D], BF16).ap()
    xln_tok = nc.dram_tensor("xln_tok", [TC, D], BF16).ap()
    xproj_d = nc.dram_tensor("xproj_d", [D, TC], F32).ap()
    ao_bf = nc.dram_tensor("ao_bf", [D, TC], BF16).ap()
    x2_d = nc.dram_tensor("x2_d", [TC, D], F32).ap()
    xln3_tok = nc.dram_tensor("xln3_tok", [TC, D], BF16).ap()
    ffn_bf = nc.dram_tensor("ffn_bf", [D, TC], BF16).ap()
    li_b = nc.dram_tensor("lasts_in", [128, KT], F32).ap()
    lo_b = nc.dram_tensor("lasts_out", [8 * 128, KT], F32).ap()
    if debug:
        dbg = {
            "d_xln": (nc.dram_tensor("d_xln", [TC, D], F32, kind="ExternalOutput").ap(), xln_tok),
            "d_xp": (nc.dram_tensor("d_xp", [D, TC], F32, kind="ExternalOutput").ap(), xproj_d),
            "d_ao": (nc.dram_tensor("d_ao", [D, TC], F32, kind="ExternalOutput").ap(), ao_bf),
            "d_x2": (nc.dram_tensor("d_x2", [TC, D], F32, kind="ExternalOutput").ap(), x2_d),
            "d_x3": (nc.dram_tensor("d_x3", [TC, D], F32, kind="ExternalOutput").ap(), xln3_tok),
            "d_ffn": (nc.dram_tensor("d_ffn", [D, TC], F32, kind="ExternalOutput").ap(), ffn_bf),
        }

    stk = contextlib.ExitStack()
    with stk:
        en = stk.enter_context

        adiag = en(nc.sbuf_tensor("adiag", [128, KT], F32))
        a2048 = en(nc.sbuf_tensor("a2048", [128, KT], F32))
        asq = en(nc.sbuf_tensor("asq", [128, KT], F32))
        ab = en(nc.sbuf_tensor("ab", [128, KT], F32))
        cb_s = en(nc.sbuf_tensor("cb_s", [128, KT], F32))
        db_s = en(nc.sbuf_tensor("db_s", [128, KT], F32))
        ldl = en(nc.sbuf_tensor("ldl", [128, KT], F32))
        abt = en(nc.sbuf_tensor("abt", [128, 2 * 64], F32))
        bbt = en(nc.sbuf_tensor("bbt", [128, 2 * 64], F32))
        lasts = en(nc.sbuf_tensor("lasts", [128, KT], F32))
        lall = en(nc.sbuf_tensor("lall", [128, 8 * KT], F32))
        gcar = en(nc.sbuf_tensor("gcar", [128, KT], F32))
        pred = en(nc.sbuf_tensor("pred", [128, 3 * KT], F32))
        selr = en(nc.sbuf_tensor("selr", [128, 24], F32))
        selb = en(nc.sbuf_tensor("selb", [128, 24], F32))
        onesr = en(nc.sbuf_tensor("onesr", [128, 128], F32))
        stt = en(nc.sbuf_tensor("stt", [128, 16 * 8], F32))

        s_p0 = en(nc.semaphore("s_p0"))
        s_ab0 = en(nc.semaphore("s_ab0"))
        s_ab1 = en(nc.semaphore("s_ab1"))
        s_abp = [s_ab0, s_ab1]
        s_c_cd = en(nc.semaphore("s_c_cd"))
        s_c_sh = en(nc.semaphore("s_c_sh"))
        s_c_gu = en(nc.semaphore("s_c_gu"))
        s_c_dn = en(nc.semaphore("s_c_dn"))
        s_pre = en(nc.semaphore("s_pre"))
        s_sel = en(nc.semaphore("s_sel"))
        s_fx0 = en(nc.semaphore("s_fx0"))

        CAST_CD = 2 * 16
        CAST_SH = 32
        CAST_GU = E * 16
        CAST_DN = E * 16
        P0_PARAMS = 49 * 16

        def p0_bodies(blk):

            @blk.gpsimd
            def _(g):
                g.dma_start(out=cw_bf[:, :], in_=Cw_in[:, :]).then_inc(s_c_cd, 16)
                g.dma_start(out=dw_bf[:, :], in_=Dw_in[:, :]).then_inc(s_c_cd, 16)
                g.dma_start(out=wsg_bf[:, :], in_=wsg_in[:, :]).then_inc(s_c_sh, 16)
                g.dma_start(out=wsd_bf[:, :], in_=wsd_in[:, :]).then_inc(s_c_sh, 16)
                for e in range(E):
                    g.dma_start(out=wgu_bf[e, :, :], in_=wgu_in[e, :, :]).then_inc(
                        s_c_gu, 16
                    )
                for e in range(E):
                    g.dma_start(out=wdn_bf[e, :, :], in_=wdn_in[e, :, :]).then_inc(
                        s_c_dn, 16
                    )

            @blk.sync
            def _(sy):
                for j in range(KT):
                    sl = slice(j * 128, (j + 1) * 128)
                    o1 = lambda t: t.rearrange("(p o) -> p o", o=1)
                    sy.dma_start(out=ldl[:, j : j + 1], in_=o1(ld_in[sl])).then_inc(s_p0, 16)
                    sy.dma_start(out=cb_s[:, j : j + 1], in_=o1(Cb_in[sl])).then_inc(s_p0, 16)
                    sy.dma_start(out=db_s[:, j : j + 1], in_=o1(Db_in[sl])).then_inc(s_p0, 16)
                sy.dma_start(
                    out=selr[0:1, 0:24],
                    in_=sel_in.rearrange("a b -> (a b)").rearrange("(o f) -> o f", o=1),
                ).then_inc(s_p0, 16)
                for j in range(KT):
                    if j >= 2:
                        sy.wait_ge(s_pre, 3 + j - 1)
                    p = j % 2
                    sy.dma_start(out=abt[:, p * 64 : (p + 1) * 64], in_=A_in[j * 128 : (j + 1) * 128, :]).then_inc(s_abp[p], 16)
                    sy.dma_start(out=bbt[:, p * 64 : (p + 1) * 64], in_=B_in[j * 128 : (j + 1) * 128, :]).then_inc(s_abp[p], 16)

            @blk.scalar
            def _(s):
                s.wait_ge(s_p0, P0_PARAMS)
                s.activation(out=adiag[:, :], in_=ldl[:, :], func=AF.Sigmoid,
                             scale=-1.0).then_inc(s_pre, 1)

            @blk.vector
            def _(v):
                fx0 = [0]

                def edge(inst):
                    inst.then_inc(s_fx0, 1)
                    fx0[0] += 1
                    v.wait_ge(s_fx0, fx0[0])

                v.wait_ge(s_pre, 1)
                edge(v.tensor_mul(asq[:, :], adiag[:, :], adiag[:, :]))
                cur, oth = asq, a2048
                for _ in range(TC.bit_length() - 2):
                    edge(v.tensor_mul(oth[:, :], cur[:, :], cur[:, :]))
                    cur, oth = oth, cur
                if cur is not a2048:
                    edge(v.tensor_copy(a2048[:, :], cur[:, :]))
                v.memset(onesr[0:1, :], 1.0).then_inc(s_pre, 2)
                for j in range(KT):
                    p = j % 2
                    v.wait_ge(s_abp[p], 32 * (j // 2 + 1))
                    edge(v.tensor_mul(bbt[:, p * 64 : (p + 1) * 64],
                                      abt[:, p * 64 : (p + 1) * 64],
                                      bbt[:, p * 64 : (p + 1) * 64]))
                    v.tensor_reduce(ab[:, j : j + 1],
                                    bbt[:, p * 64 : (p + 1) * 64], AX.X,
                                    ALU.add).then_inc(s_pre, 1)

            @blk.tensor
            def _(t):
                t.wait_ge(s_pre, 3)
                t.wait_ge(s_p0, P0_PARAMS)
                t.matmul(pss[:, 0:24], onesr[0:1, :], selr[0:1, 0:24],
                         start=True, stop=True).then_inc(s_sel, 1)

        with contextlib.ExitStack() as sc:
            e1 = sc.enter_context
            xb = e1(nc.sbuf_tensor("xb", [128, 2 * D], F32))
            xlnb = e1(nc.sbuf_tensor("xlnb", [128, 2 * D], BF16))
            dump = e1(nc.sbuf_tensor("dump1", [128, D], F32))
            pss = e1(nc.psum_tensor("pss", [128, 32], F32))
            s_ld = PS(en, nc, "s1_ld")
            s_st = PS(en, nc, "s1_st")
            for _j in range(TT):
                s_ld.reserve(_j)
            for _j in range(TT):
                s_st.reserve(_j)
            s_v = en(nc.semaphore("s1_v"))
            s_sc = en(nc.semaphore("s1_sc"))
            s_fx = en(nc.semaphore("s1_fx"))

            with nc.Block() as blk:
                p0_bodies(blk)

                @blk.sync
                def _(sy):
                    for j in range(TT):
                        if j >= 2:
                            sy.wait_ge(s_v, 4 * (j - 1))
                            sy.wait_ge(s_sc, 2 * (j - 1))
                            s_st.wait(sy, j - 2)
                        s_ld.inc(
                            sy.dma_start(
                                out=xb[:, (j % 2) * D : (j % 2 + 1) * D],
                                in_=x_in[j * 128 : (j + 1) * 128, :],
                            ),
                            j,
                        )
                        if j >= 1:
                            sy.wait_ge(s_v, 4 * j)
                            s_st.inc(
                                sy.dma_start(
                                    out=xln_tok[(j - 1) * 128 : j * 128, :],
                                    in_=xlnb[:, ((j - 1) % 2) * D : ((j - 1) % 2 + 1) * D],
                                ),
                                j - 1,
                            )
                    sy.wait_ge(s_v, 4 * TT)
                    s_st.inc(
                        sy.dma_start(
                            out=xln_tok[(TT - 1) * 128 : TT * 128, :],
                            in_=xlnb[:, ((TT - 1) % 2) * D : ((TT - 1) % 2 + 1) * D],
                        ),
                        TT - 1,
                    )
                    s_st.wait_all(sy)

                @blk.scalar
                def _(s):
                    for j in range(TT):
                        xj = xb[:, (j % 2) * D : (j % 2 + 1) * D]
                        c = j * 8
                        s_ld.wait(s, j)
                        s.activation(out=dump[:, :], in_=xj, func=AF.Square,
                                     accum_out=stt[:, c + 1 : c + 2]).then_inc(s_sc, 1)
                        s.wait_ge(s_sc, 2 * j + 1)
                        s.wait_ge(s_v, 4 * j + 2)
                        s.activation(out=stt[:, c + 4 : c + 5],
                                     in_=stt[:, c + 1 : c + 2], func=AF.Sqrt,
                                     scale=1.0 / D,
                                     bias=stt[:, c + 3 : c + 4]).then_inc(s_sc, 1)

                @blk.vector
                def _(v):
                    fx = 0
                    for j in range(TT):
                        xj = xb[:, (j % 2) * D : (j % 2 + 1) * D]
                        c = j * 8
                        m = stt[:, c + 2 : c + 3]
                        s_ld.wait(v, j)
                        v.tensor_reduce(stt[:, c : c + 1], xj, AX.X,
                                        ALU.add).then_inc(s_fx, 1)
                        fx += 1
                        v.wait_ge(s_fx, fx)
                        v.tensor_scalar(m, stt[:, c : c + 1], 1.0 / D, None,
                                        ALU.mult).then_inc(s_v, 1)
                        v.wait_ge(s_v, 4 * j + 1)
                        v.tensor_mul(stt[:, c + 3 : c + 4], m, m).then_inc(s_fx, 1)
                        fx += 1
                        v.wait_ge(s_fx, fx)
                        v.tensor_scalar(stt[:, c + 3 : c + 4],
                                        stt[:, c + 3 : c + 4], -1.0, EPS,
                                        ALU.mult, ALU.add).then_inc(s_v, 1)
                        v.wait_ge(s_sc, 2 * (j + 1))
                        v.reciprocal(stt[:, c + 5 : c + 6],
                                     stt[:, c + 4 : c + 5]).then_inc(s_fx, 1)
                        fx += 1
                        v.wait_ge(s_fx, fx)
                        v.tensor_scalar(xlnb[:, (j % 2) * D : (j % 2 + 1) * D],
                                        xj, m, stt[:, c + 5 : c + 6],
                                        ALU.subtract, ALU.mult).then_inc(s_v, 2)

        
                @blk.vector
                def _(v):
                    v.wait_ge(s_sel, 1)
                    v.tensor_copy(selb[:, :], pss[:, 0:24]).then_inc(s_sel, 1)

        with contextlib.ExitStack() as sc:
            e2 = sc.enter_context
            xlnT = e2(nc.sbuf_tensor("xlnT", [128, KT * TC], BF16))
            cwT = e2(nc.sbuf_tensor("cwT", [128, KT * 1024], BF16))
            dwT = e2(nc.sbuf_tensor("dwT", [128, KT * 1024], BF16))
            sdb = e2(nc.sbuf_tensor("sdb", [128, NB * BLK], F32))
            xpb = e2(nc.sbuf_tensor("xpb", [128, 2 * BLK], F32))
            psC = [e2(nc.psum_tensor(f"psC{i}", [128, BLK], F32)) for i in range(NB)]
            psD = [e2(nc.psum_tensor(f"psD{i}", [128, BLK], F32)) for i in range(NB)]
            s_tp = en(nc.semaphore("s2_tp"))
            s_w = en(nc.semaphore("s2_w"))
            s_mm = en(nc.semaphore("s2_mm"))
            s_sg = en(nc.semaphore("s2_sg"))
            s_ev = en(nc.semaphore("s2_ev"))
            s_st = PS(en, nc, "s2_st")
            GH = NB * 8
            for _i in range(2 * GH):
                s_st.reserve(_i)

            with nc.Block() as blk:

                @blk.scalar
                def _(s):
                    for ct in range(KT):
                        s.dma_start_transpose(
                            xlnT[:, ct * TC : (ct + 1) * TC],
                            xln_tok[:, ct * 128 : (ct + 1) * 128],
                        ).then_inc(s_tp, 16)
                    s.wait_ge(s_c_cd, CAST_CD)
                    for k in range(KT):
                        s.dma_start_transpose(
                            cwT[:, k * 1024 : (k + 1) * 1024],
                            cw_bf[0:1024, k * 128 : (k + 1) * 128],
                        ).then_inc(s_w, 16)
                        s.dma_start_transpose(
                            dwT[:, k * 1024 : (k + 1) * 1024],
                            dw_bf[0:1024, k * 128 : (k + 1) * 128],
                        ).then_inc(s_w, 16)

                @blk.sync
                def _(sy):
                    for i in range(2 * GH):
                        g, tb = i // NB, i % NB
                        sy.wait_ge(s_ev, i + 1)
                        s_st.inc(
                            sy.dma_start(
                                out=xproj_d[g * 128 : (g + 1) * 128,
                                            tb * BLK : (tb + 1) * BLK],
                                in_=xpb[:, (i % 2) * BLK : (i % 2 + 1) * BLK],
                            ),
                            i,
                        )
                    s_st.wait_all(sy)

                @blk.tensor
                def _(t):
                    t.wait_ge(s_tp, 16 * KT)
                    for oh in range(2):
                        t.wait_ge(s_w, 32 * 16 * (oh + 1))
                        for oc8 in range(8):
                            g = oh * 8 + oc8
                            if g >= 1:
                                t.wait_ge(s_ev, NB * g)
                            for k in range(KT):
                                for tb in range(NB):
                                    t.matmul(
                                        psC[tb][:, :],
                                        cwT[:, k * 1024 + oc8 * 128 : k * 1024 + (oc8 + 1) * 128],
                                        xlnT[:, k * TC + tb * BLK : k * TC + (tb + 1) * BLK],
                                        start=(k == 0), stop=(k == KT - 1),
                                    )
                            for k in range(KT):
                                for tb in range(NB):
                                    inst = t.matmul(
                                        psD[tb][:, :],
                                        dwT[:, k * 1024 + oc8 * 128 : k * 1024 + (oc8 + 1) * 128],
                                        xlnT[:, k * TC + tb * BLK : k * TC + (tb + 1) * BLK],
                                        start=(k == 0), stop=(k == KT - 1),
                                    )
                            inst.then_inc(s_mm, 1)

                @blk.scalar
                def _(s):
                    for g in range(16):
                        if g == 8:
                            s.wait_ge(s_mm, 8)
                            for k in range(KT):
                                s.dma_start_transpose(
                                    cwT[:, k * 1024 : (k + 1) * 1024],
                                    cw_bf[1024:2048, k * 128 : (k + 1) * 128],
                                ).then_inc(s_w, 16)
                                s.dma_start_transpose(
                                    dwT[:, k * 1024 : (k + 1) * 1024],
                                    dw_bf[1024:2048, k * 128 : (k + 1) * 128],
                                ).then_inc(s_w, 16)
                        oc = g
                        s.wait_ge(s_mm, g + 1)
                        for tb in range(NB):
                            if g >= 1:
                                s.wait_ge(s_ev, NB * g)
                            s.activation(
                                out=sdb[:, tb * BLK : (tb + 1) * BLK],
                                in_=psD[tb][:, :], func=AF.Sigmoid,
                                bias=db_s[:, oc : oc + 1], scale=1.0,
                            ).then_inc(s_sg, 1)

                @blk.vector
                def _(v):
                    for g in range(16):
                        oc = g
                        for tb in range(NB):
                            i = g * NB + tb
                            v.wait_ge(s_sg, i + 1)
                            if i >= 2:
                                s_st.wait(v, i - 2)
                            v.scalar_tensor_tensor(
                                xpb[:, (i % 2) * BLK : (i % 2 + 1) * BLK],
                                psC[tb][:, :], cb_s[:, oc : oc + 1],
                                sdb[:, tb * BLK : (tb + 1) * BLK],
                                ALU.add, ALU.mult,
                            ).then_inc(s_ev, 1)

        with contextlib.ExitStack() as sc:
            e3 = sc.enter_context
            gs = e3(nc.sbuf_tensor("gs", [128, KT * TC], BF16))
            xpl = e3(nc.sbuf_tensor("xpl", [128, 2 * TC], F32))
            aob = e3(nc.sbuf_tensor("aob", [128, 2 * TC], BF16))
            ztile = e3(nc.sbuf_tensor("ztile", [128, TC], F32))
            zeros = e3(nc.sbuf_tensor("zeros", [128, TC], F32))
            abc = e3(nc.sbuf_tensor("abc", [128, TC], F32))
            s_ld = PS(en, nc, "s3_ld")
            s_gx = en(nc.semaphore("s3_gx"))
            s_v = en(nc.semaphore("s3_v"))
            s_cc = en(nc.semaphore("s3_cc"))
            s_st = PS(en, nc, "s3_st")
            s_fx = en(nc.semaphore("s3_fx"))
            for _ct in range(KT):
                s_ld.reserve(_ct)
            for _ct in range(KT):
                s_st.reserve(_ct)

            with nc.Block() as blk:

                @blk.sync
                def _(sy):
                    for ct in range(KT):
                        if ct >= 2:
                            sy.wait_ge(s_v, 2 * (ct - 1))
                        s_ld.inc(
                            sy.dma_start(
                                out=xpl[:, (ct % 2) * TC : (ct % 2 + 1) * TC],
                                in_=xproj_d[ct * 128 : (ct + 1) * 128, :],
                            ),
                            ct,
                        )
                    sy.wait_ge(s_v, 2 * KT)
                    sy.dma_start(out=li_b[:, :], in_=lasts[:, :]).then_inc(s_gx, 16)
                    sy.wait_ge(s_cc, 1)
                    for r in range(8):
                        sy.dma_start(
                            out=lall[:, r * KT : (r + 1) * KT],
                            in_=lo_b[r * 128 : (r + 1) * 128, :],
                        ).then_inc(s_gx, 16)

                @blk.vector
                def _(v):
                    fx4 = 0
                    v.memset(zeros[:, :], 0.0).then_inc(s_fx, 1)
                    fx4 += 1
                    v.wait_ge(s_fx, fx4)
                    for ct in range(KT):
                        s_ld.wait(v, ct)
                        v.tensor_single_scalar(abc[:, :], zeros[:, :],
                                               adiag[:, ct : ct + 1],
                                               ALU.add).then_inc(s_fx, 1)
                        fx4 += 1
                        v.wait_ge(s_fx, fx4)
                        v.tensor_tensor_scan(
                            gs[:, ct * TC : (ct + 1) * TC], abc[:, :],
                            xpl[:, (ct % 2) * TC : (ct % 2 + 1) * TC],
                            0.0, ALU.mult, ALU.add,
                        ).then_inc(s_v, 1)
                        v.wait_ge(s_v, 2 * ct + 1)
                        v.tensor_copy(
                            lasts[:, ct : ct + 1],
                            gs[:, ct * TC + TC - 1 : ct * TC + TC],
                        ).then_inc(s_v, 1)

                @blk.gpsimd
                def _(g):
                    g.wait_ge(s_gx, 16)
                    g.collective_compute(
                        "AllGather", ALU.bypass,
                        replica_groups=[list(range(8))],
                        ins=[li_b.opt()], outs=[lo_b.opt()],
                    ).then_inc(s_cc, 1)

                @blk.vector
                def _(v):
                    fx6 = [KT + 1]

                    def edge(inst):
                        inst.then_inc(s_fx, 1)
                        fx6[0] += 1
                        v.wait_ge(s_fx, fx6[0])

                    v.wait_ge(s_gx, 16 * 9)
                    for s_ in range(3):
                        edge(v.tensor_scalar_mul(pred[:, s_ * KT : (s_ + 1) * KT],
                                                 lall[:, 0:KT],
                                                 selb[:, s_ * 8 : s_ * 8 + 1]))
                        for r in range(1, 8):
                            edge(v.scalar_tensor_tensor(
                                pred[:, s_ * KT : (s_ + 1) * KT],
                                lall[:, r * KT : (r + 1) * KT],
                                selb[:, s_ * 8 + r : s_ * 8 + r + 1],
                                pred[:, s_ * KT : (s_ + 1) * KT],
                                ALU.mult, ALU.add,
                            ))
                    edge(v.tensor_mul(gcar[:, :], a2048[:, :],
                                      pred[:, 2 * KT : 3 * KT]))
                    edge(v.tensor_add(gcar[:, :], gcar[:, :],
                                      pred[:, KT : 2 * KT]))
                    edge(v.tensor_mul(gcar[:, :], a2048[:, :], gcar[:, :]))
                    v.tensor_add(gcar[:, :], gcar[:, :],
                                 pred[:, 0:KT]).then_inc(s_v, 1)
                    v.wait_ge(s_v, 2 * KT + 1)
                    for ct in range(KT):
                        if ct >= 2:
                            s_st.wait(v, ct - 2)
                        edge(v.tensor_single_scalar(abc[:, :], zeros[:, :],
                                                    adiag[:, ct : ct + 1],
                                                    ALU.add))
                        edge(v.tensor_tensor_scan(
                            ztile[:, :], abc[:, :], zeros[:, :],
                            gcar[:, ct : ct + 1], ALU.mult, ALU.add,
                        ))
                        edge(v.tensor_add(ztile[:, :],
                                          gs[:, ct * TC : (ct + 1) * TC],
                                          ztile[:, :]))
                        v.tensor_scalar_mul(
                            aob[:, (ct % 2) * TC : (ct % 2 + 1) * TC],
                            ztile[:, :], ab[:, ct : ct + 1],
                        ).then_inc(s_v, 1)

                @blk.sync
                def _(sy):
                    for ct in range(KT):
                        sy.wait_ge(s_v, 2 * KT + 1 + ct + 1)
                        s_st.inc(
                            sy.dma_start(
                                out=ao_bf[ct * 128 : (ct + 1) * 128, :],
                                in_=aob[:, (ct % 2) * TC : (ct % 2 + 1) * TC],
                            ),
                            ct,
                        )
                    s_st.wait_all(sy)

        with contextlib.ExitStack() as sc:
            e4 = sc.enter_context
            aot = e4(nc.sbuf_tensor("aot", [128, 2 * D], BF16))
            dump = e4(nc.sbuf_tensor("dump4", [128, D], F32))
            xb2 = e4(nc.sbuf_tensor("xb2", [128, 2 * D], F32))
            x2b = e4(nc.sbuf_tensor("x2b", [128, 2 * D], F32))
            x3b = e4(nc.sbuf_tensor("x3b", [128, 2 * D], BF16))
            s_tp = PS(en, nc, "s4_tp")
            s_ld = PS(en, nc, "s4_ld")
            s_st = PS(en, nc, "s4_st", ways=4)
            for _j in range(TT):
                s_tp.reserve(_j)
            for _j in range(TT):
                s_ld.reserve(_j)
            for _j in range(TT):
                s_st.reserve((_j, 0))
                s_st.reserve((_j, 1))
            s_v = en(nc.semaphore("s4_v"))
            s_sc = en(nc.semaphore("s4_sc"))
            s_fx = en(nc.semaphore("s4_fx"))

            with nc.Block() as blk:

                @blk.sync
                def _(sy):
                    for j in range(TT):
                        if j >= 2:
                            sy.wait_ge(s_v, 8 * (j - 1))
                            s_st.wait(sy, (j - 2, 0))
                            s_st.wait(sy, (j - 2, 1))
                        s_ld.inc(
                            sy.dma_start(
                                out=xb2[:, (j % 2) * D : (j % 2 + 1) * D],
                                in_=x_in[j * 128 : (j + 1) * 128, :],
                            ),
                            j,
                        )
                        if j >= 1:
                            sy.wait_ge(s_v, 8 * (j - 1) + 4)
                            s_st.inc(
                                sy.dma_start(
                                    out=x2_d[(j - 1) * 128 : j * 128, :],
                                    in_=x2b[:, ((j - 1) % 2) * D : ((j - 1) % 2 + 1) * D],
                                ),
                                (j - 1, 0),
                            )
                            sy.wait_ge(s_v, 8 * j)
                            s_st.inc(
                                sy.dma_start(
                                    out=xln3_tok[(j - 1) * 128 : j * 128, :],
                                    in_=x3b[:, ((j - 1) % 2) * D : ((j - 1) % 2 + 1) * D],
                                ),
                                (j - 1, 1),
                            )
                    sy.wait_ge(s_v, 8 * (TT - 1) + 4)
                    s_st.inc(
                        sy.dma_start(
                            out=x2_d[(TT - 1) * 128 : TT * 128, :],
                            in_=x2b[:, ((TT - 1) % 2) * D : ((TT - 1) % 2 + 1) * D],
                        ),
                        (TT - 1, 0),
                    )
                    sy.wait_ge(s_v, 8 * TT)
                    s_st.inc(
                        sy.dma_start(
                            out=xln3_tok[(TT - 1) * 128 : TT * 128, :],
                            in_=x3b[:, ((TT - 1) % 2) * D : ((TT - 1) % 2 + 1) * D],
                        ),
                        (TT - 1, 1),
                    )
                    s_st.wait_all(sy)

                @blk.scalar
                def _(s):
                    s_tp.inc(
                        s.dma_start_transpose(
                            aot[:, 0:D], ao_bf[:, 0:128]
                        ),
                        0,
                    )
                    s_tp.inc(
                        s.dma_start_transpose(
                            aot[:, D : 2 * D], ao_bf[:, 128:256]
                        ),
                        1,
                    )
                    for j in range(TT):
                        c = j * 8
                        a_ = aot[:, (j % 2) * D : (j % 2 + 1) * D]
                        x2_ = x2b[:, (j % 2) * D : (j % 2 + 1) * D]
                        s_tp.wait(s, j)
                        s.activation(out=dump[:, :], in_=a_, func=AF.Square,
                                     accum_out=stt[:, c + 1 : c + 2]).then_inc(s_sc, 1)
                        s.wait_ge(s_sc, 4 * j + 1)
                        s.wait_ge(s_v, 8 * j + 2)
                        s.activation(out=stt[:, c + 4 : c + 5],
                                     in_=stt[:, c + 1 : c + 2], func=AF.Sqrt,
                                     scale=1.0 / D,
                                     bias=stt[:, c + 3 : c + 4]).then_inc(s_sc, 1)
                        s.wait_ge(s_v, 8 * j + 4)
                        s.activation(out=dump[:, :], in_=x2_, func=AF.Square,
                                     accum_out=stt[:, c + 6 : c + 7]).then_inc(s_sc, 1)
                        s.wait_ge(s_sc, 4 * j + 3)
                        s.wait_ge(s_v, 8 * j + 6)
                        s.activation(out=stt[:, c + 4 : c + 5],
                                     in_=stt[:, c + 6 : c + 7], func=AF.Sqrt,
                                     scale=1.0 / D,
                                     bias=stt[:, c + 3 : c + 4]).then_inc(s_sc, 1)
                        if j + 2 < TT:
                            s_tp.inc(
                                s.dma_start_transpose(
                                    aot[:, (j % 2) * D : (j % 2 + 1) * D],
                                    ao_bf[:, (j + 2) * 128 : (j + 3) * 128],
                                ),
                                j + 2,
                            )

                @blk.vector
                def _(v):
                    fx = 0
                    for j in range(TT):
                        c = j * 8
                        a_ = aot[:, (j % 2) * D : (j % 2 + 1) * D]
                        x_ = xb2[:, (j % 2) * D : (j % 2 + 1) * D]
                        x2_ = x2b[:, (j % 2) * D : (j % 2 + 1) * D]
                        m = stt[:, c + 2 : c + 3]
                        s_tp.wait(v, j)
                        v.tensor_reduce(stt[:, c : c + 1], a_, AX.X,
                                        ALU.add).then_inc(s_fx, 1)
                        fx += 1
                        v.wait_ge(s_fx, fx)
                        v.tensor_scalar(m, stt[:, c : c + 1], 1.0 / D, None,
                                        ALU.mult).then_inc(s_v, 1)
                        v.wait_ge(s_v, 8 * j + 1)
                        v.tensor_mul(stt[:, c + 3 : c + 4], m, m).then_inc(s_fx, 1)
                        fx += 1
                        v.wait_ge(s_fx, fx)
                        v.tensor_scalar(stt[:, c + 3 : c + 4],
                                        stt[:, c + 3 : c + 4], -1.0, EPS,
                                        ALU.mult, ALU.add).then_inc(s_v, 1)
                        v.wait_ge(s_sc, 4 * j + 2)
                        v.reciprocal(stt[:, c + 5 : c + 6],
                                     stt[:, c + 4 : c + 5]).then_inc(s_fx, 1)
                        fx += 1
                        v.wait_ge(s_fx, fx)
                        s_ld.wait(v, j)
                        v.tensor_scalar(x2_, a_, m, stt[:, c + 5 : c + 6],
                                        ALU.subtract, ALU.mult).then_inc(s_v, 1)
                        v.wait_ge(s_v, 8 * j + 3)
                        v.tensor_add(x2_, x2_, x_).then_inc(s_v, 1)
                        v.wait_ge(s_v, 8 * j + 4)
                        v.tensor_reduce(stt[:, c : c + 1], x2_, AX.X,
                                        ALU.add).then_inc(s_fx, 1)
                        fx += 1
                        v.wait_ge(s_fx, fx)
                        v.tensor_scalar(m, stt[:, c : c + 1], 1.0 / D, None,
                                        ALU.mult).then_inc(s_v, 1)
                        v.wait_ge(s_v, 8 * j + 5)
                        v.tensor_mul(stt[:, c + 3 : c + 4], m, m).then_inc(s_fx, 1)
                        fx += 1
                        v.wait_ge(s_fx, fx)
                        v.tensor_scalar(stt[:, c + 3 : c + 4],
                                        stt[:, c + 3 : c + 4], -1.0, EPS,
                                        ALU.mult, ALU.add).then_inc(s_v, 1)
                        v.wait_ge(s_sc, 4 * (j + 1))
                        v.reciprocal(stt[:, c + 5 : c + 6],
                                     stt[:, c + 4 : c + 5]).then_inc(s_fx, 1)
                        fx += 1
                        v.wait_ge(s_fx, fx)
                        v.tensor_scalar(x3b[:, (j % 2) * D : (j % 2 + 1) * D],
                                        x2_, m, stt[:, c + 5 : c + 6],
                                        ALU.subtract, ALU.mult).then_inc(s_v, 2)

        with contextlib.ExitStack() as sc:
            e5 = sc.enter_context
            x3T = e5(nc.sbuf_tensor("x3T", [128, KT * TC], BF16))
            wgf = e5(nc.sbuf_tensor("wgf", [128, KT * E], F32))
            wgs = e5(nc.sbuf_tensor("wgs", [128, KT * E], BF16))
            iden = e5(nc.sbuf_tensor("iden", [128, 128], F32))
            icol = e5(nc.sbuf_tensor("icol", [128, 128], F32))
            ipart = e5(nc.sbuf_tensor("ipart", [128, 1], F32))
            i8 = e5(nc.sbuf_tensor("i8", [128, E], F32))
            ohm = e5(nc.sbuf_tensor("ohm", [128, E * 128], BF16))
            pbc = e5(nc.sbuf_tensor("pbc", [128, 128], F32))
            lgt = e5(nc.sbuf_tensor("lgt", [128, E], F32))
            top8 = e5(nc.sbuf_tensor("top8", [128, 8], F32))
            idx8 = e5(nc.sbuf_tensor("idx8", [128, 8], U32))
            idxf = e5(nc.sbuf_tensor("idxf", [128, 8], F32))
            oh1 = e5(nc.sbuf_tensor("oh1", [128, E], F32))
            oh2 = e5(nc.sbuf_tensor("oh2", [128, E], F32))
            comb = e5(nc.sbuf_tensor("comb", [128, E], F32))
            comb_cm = e5(nc.sbuf_tensor("comb_cm", [128, TC], BF16))
            cbb = e5(nc.sbuf_tensor("cbb", [128, TC], BF16))
            sgb = e5(nc.sbuf_tensor("sgb", [128, TC], BF16))
            ucb = e5(nc.sbuf_tensor("ucb", [128, BLK], F32))
            actb = e5(nc.sbuf_tensor("actb", [128, 8 * TC], BF16))
            wgu_s = e5(nc.sbuf_tensor("wgu_s", [128, 4 * KT * 128], BF16))
            wdn_s = e5(nc.sbuf_tensor("wdn_s", [128, 4 * 8 * 128], BF16))
            racc = e5(nc.sbuf_tensor("racc", [128, KT * TC], BF16))
            rp_stack = contextlib.ExitStack()
            ps_r = rp_stack.enter_context(nc.psum_tensor("ps_r", [128, E], F32))
            ps_t = rp_stack.enter_context(nc.psum_tensor("ps_t", [128, 128], F32))
            s_tp = en(nc.semaphore("s5_tp"))
            s_wg = en(nc.semaphore("s5_wg"))
            s_w4 = PS(en, nc, "s5_w", ways=4)
            s_mm = en(nc.semaphore("s5_mm"))
            s_sil = en(nc.semaphore("s5_sil"))
            s_act = en(nc.semaphore("s5_act"))
            s_eo = en(nc.semaphore("s5_eo"))
            s_rt = en(nc.semaphore("s5_rt"))
            s_sg = en(nc.semaphore("s5_sg"))
            s_cb = en(nc.semaphore("s5_cb"))
            s_fx8 = en(nc.semaphore("s5_fx8"))
            s_ux = en(nc.semaphore("s5_ux"))
            uxv = [0]

            mmv = [0]
            silv = [0]
            actv = [0]
            eov = [0]
            cbv = [0]
            stage_id = [0]

            with nc.Block() as blk:

                @blk.scalar
                def _(s):
                    for ct in range(KT):
                        s.dma_start_transpose(
                            x3T[:, ct * TC : (ct + 1) * TC],
                            xln3_tok[:, ct * 128 : (ct + 1) * 128],
                        ).then_inc(s_tp, 16)

                @blk.sync
                def _(sy):
                    for k in range(KT):
                        sy.dma_start(out=wgf[:, k * E : (k + 1) * E],
                                     in_=wg_in[k * 128 : (k + 1) * 128, :]).then_inc(
                            s_wg, 16)

                @blk.gpsimd
                def _(g):
                    g.iota(i8[:, :], [[1, E]], base=0, channel_multiplier=0,
                           allow_small_or_imprecise_dtypes=True)
                    g.iota(icol[:, :], [[1, 128]], base=0, channel_multiplier=0,
                           allow_small_or_imprecise_dtypes=True)
                    g.iota(ipart[:, :], [[0, 1]], base=0, channel_multiplier=1,
                           allow_small_or_imprecise_dtypes=True).then_inc(s_rt, 1)

                @blk.vector
                def _(v):
                    v.wait_ge(s_rt, 1)
                    v.tensor_single_scalar(iden[:, :], icol[:, :], ipart[:, 0:1],
                                           ALU.is_equal)
                    v.tensor_scalar_mul(pbc[:, :], icol[:, :], 0.0).then_inc(s_fx8, 1)
                    v.wait_ge(s_fx8, 1)
                    v.tensor_scalar_add(pbc[:, :], pbc[:, :],
                                        ipart[:, 0:1]).then_inc(s_fx8, 1)
                    v.wait_ge(s_fx8, 2)
                    for e in range(E):
                        v.tensor_single_scalar(ohm[:, e * 128 : (e + 1) * 128],
                                               pbc[:, :], float(e), ALU.is_equal)
                    v.memset(racc[:, :], 0.0)
                    v.wait_ge(s_wg, 16 * KT)
                    v.tensor_copy(wgs[:, :], wgf[:, :]).then_inc(s_rt, 1)

            MASKS = 2

            with nc.Block() as blk:
                mm0 = mmv[0]

                @blk.tensor
                def _(t):
                    t.wait_ge(s_tp, 16 * KT)
                    t.wait_ge(s_rt, MASKS)
                    for j in range(TT):
                        if j >= 1:
                            t.wait_ge(s_rt, MASKS + 10 * (j - 1) + 1)
                        for k in range(KT):
                            inst = t.matmul(
                                ps_r[:, :],
                                x3T[:, k * TC + j * 128 : k * TC + (j + 1) * 128],
                                wgs[:, k * E : (k + 1) * E],
                                start=(k == 0), stop=(k == KT - 1),
                            )
                        inst.then_inc(s_mm, 1)
                        t.wait_ge(s_rt, MASKS + 10 * j + 9)
                        t.transpose(ps_t[0:8, 0:128], comb[:, :],
                                    iden[:, :]).then_inc(s_mm, 1)

                @blk.vector
                def _(v):
                    for j in range(TT):
                        base = MASKS + 10 * j
                        v.wait_ge(s_mm, mm0 + 2 * j + 1)
                        v.tensor_copy(lgt[:, :], ps_r[:, :]).then_inc(s_rt, 1)
                        if j >= 1:
                            v.wait_ge(s_sg, j)
                        v.wait_ge(s_rt, base + 1)
                        v.max(top8[:, :], lgt[:, :]).then_inc(s_rt, 1)
                        v.wait_ge(s_rt, base + 2)
                        v.max_index(idx8[:, :], top8[:, :], lgt[:, :]).then_inc(s_rt, 1)
                        v.wait_ge(s_rt, base + 3)
                        v.tensor_copy(idxf[:, :], idx8[:, :]).then_inc(s_rt, 1)
                        v.tensor_sub(stt[:, 0:1], top8[:, 0:1],
                                     top8[:, 1:2]).then_inc(s_rt, 1)
                        v.wait_ge(s_sg, j + 1)
                        v.wait_ge(s_rt, base + 4)
                        v.tensor_single_scalar(oh1[:, :], i8[:, :], idxf[:, 0:1],
                                               ALU.is_equal).then_inc(s_rt, 1)
                        v.tensor_single_scalar(oh2[:, :], i8[:, :], idxf[:, 1:2],
                                               ALU.is_equal).then_inc(s_rt, 1)
                        v.wait_ge(s_rt, base + 7)
                        v.tensor_scalar_mul(comb[:, :], oh1[:, :],
                                            stt[:, 2:3]).then_inc(s_rt, 1)
                        v.wait_ge(s_rt, base + 8)
                        v.scalar_tensor_tensor(comb[:, :], oh2[:, :], stt[:, 3:4],
                                               comb[:, :], ALU.mult,
                                               ALU.add).then_inc(s_rt, 1)
                        v.wait_ge(s_mm, mm0 + 2 * j + 2)
                        v.tensor_copy(comb_cm[0:8, j * 128 : (j + 1) * 128],
                                      ps_t[0:8, 0:128]).then_inc(s_rt, 1)

                @blk.scalar
                def _(s):
                    for j in range(TT):
                        base = MASKS + 10 * j
                        s.wait_ge(s_rt, base + 5)
                        s.activation(out=stt[:, 2:3], in_=stt[:, 0:1],
                                     func=AF.Sigmoid, scale=1.0)
                        s.activation(out=stt[:, 3:4], in_=stt[:, 0:1],
                                     func=AF.Sigmoid, scale=-1.0).then_inc(s_sg, 1)

                mmv[0] = mm0 + 2 * TT

            RT_DONE = MASKS + 10 * TT
            rp_stack.close()
            ps_q = [e5(nc.psum_tensor(f"ps_q{i}", [128, BLK], F32))
                    for i in range(2 * NB)]

            def gu_stage(is_shared, e_or_ic):
                with nc.Block() as blk:
                    mm0 = mmv[0]
                    sil0 = silv[0]
                    act0 = actv[0]
                    cb_need = cbv[0]
                    sid = stage_id[0]
                    stage_id[0] += 1
                    for _li in range(16):
                        s_w4.reserve((sid, _li))

                    @blk.sync
                    def _(sy):
                        if is_shared:
                            sy.wait_ge(s_c_sh, CAST_SH)
                        else:
                            sy.wait_ge(s_c_gu, CAST_GU)
                        if mm0:
                            sy.wait_ge(s_mm, mm0)
                        for fi in range(8):
                            for half in range(2):
                                li = fi * 2 + half
                                if li >= 4:
                                    sy.wait_ge(s_mm, mm0 + (li - 3))
                                if is_shared:
                                    ft = e_or_ic * 8 + fi if half == 0 else 32 + e_or_ic * 8 + fi
                                    src = wsg_bf.rearrange("(k p) f -> p k f", p=128)[
                                        :, :, ft * 128 : (ft + 1) * 128]
                                else:
                                    ft = fi if half == 0 else 8 + fi
                                    src = wgu_bf[e_or_ic].rearrange(
                                        "(k p) f -> p k f", p=128)[
                                        :, :, ft * 128 : (ft + 1) * 128]
                                s_w4.inc(
                                    sy.dma_start(
                                        out=wgu_s[:, (li % 4) * KT * 128 : (li % 4 + 1) * KT * 128],
                                        in_=src,
                                    ),
                                    (sid, li),
                                )

                    @blk.tensor
                    def _(t):
                        if not is_shared:
                            t.wait_ge(s_cb, cb_need)
                        for li in range(16):
                            s_w4.wait(t, (sid, li))
                            q = (li % 2) * NB
                            if li >= 2:
                                pl = li - 2
                                if pl % 2 == 0:
                                    t.wait_ge(s_sil, sil0 + (pl // 2) * NB + NB)
                                else:
                                    t.wait_ge(s_act, act0 + (pl // 2) * NB + NB)
                            for k in range(KT):
                                for tb in range(NB):
                                    inst = t.matmul(
                                        ps_q[q + tb][:, :],
                                        wgu_s[:, (li % 4) * KT * 128 + k * 128 : (li % 4) * KT * 128 + (k + 1) * 128],
                                        x3T[:, k * TC + tb * BLK : k * TC + (tb + 1) * BLK],
                                        start=(k == 0), stop=(k == KT - 1),
                                    )
                            inst.then_inc(s_mm, 1)

                    @blk.scalar
                    def _(s):
                        for fi in range(8):
                            li = fi * 2
                            q = (li % 2) * NB
                            s.wait_ge(s_mm, mm0 + li + 1)
                            for tb in range(NB):
                                if fi >= 1:
                                    s.wait_ge(s_act, act0 + (fi - 1) * NB + tb + 1)
                                s.activation(
                                    out=sgb[:, tb * BLK : (tb + 1) * BLK],
                                    in_=ps_q[q + tb][:, :], func=AF.Silu,
                                ).then_inc(s_sil, 1)

                    @blk.vector
                    def _(v):
                        for fi in range(8):
                            li = fi * 2 + 1
                            q = (li % 2) * NB
                            v.wait_ge(s_mm, mm0 + li + 1)
                            for tb in range(NB):
                                if is_shared:
                                    inst = v.tensor_copy(ucb[:, :], ps_q[q + tb][:, :])
                                else:
                                    inst = v.tensor_tensor(
                                        ucb[:, :], ps_q[q + tb][:, :],
                                        cbb[:, tb * BLK : (tb + 1) * BLK], ALU.mult)
                                inst.then_inc(s_ux, 1)
                                uxv[0] += 1
                                v.wait_ge(s_ux, uxv[0])
                                v.wait_ge(s_sil, sil0 + fi * NB + tb + 1)
                                v.tensor_tensor(
                                    actb[:, fi * TC + tb * BLK : fi * TC + (tb + 1) * BLK],
                                    sgb[:, tb * BLK : (tb + 1) * BLK], ucb[:, :],
                                    ALU.mult,
                                ).then_inc(s_act, 1)

                    mmv[0] = mm0 + 16
                    silv[0] = sil0 + 8 * NB
                    actv[0] = act0 + 8 * NB

            def down_stage(is_shared, e_or_ic):
                with nc.Block() as blk:
                    mm0 = mmv[0]
                    eo0 = eov[0]
                    sid = stage_id[0]
                    stage_id[0] += 1
                    for _ht in range(KT):
                        s_w4.reserve((sid, _ht))

                    @blk.sync
                    def _(sy):
                        if not is_shared:
                            sy.wait_ge(s_c_dn, CAST_DN)
                        sy.wait_ge(s_mm, mm0)
                        for ht in range(KT):
                            if ht >= 4:
                                sy.wait_ge(s_mm, mm0 + (ht - 3))
                            if is_shared:
                                src = wsd_bf[e_or_ic * IM : (e_or_ic + 1) * IM, :].rearrange(
                                    "(k p) h -> p k h", p=128)[:, :, ht * 128 : (ht + 1) * 128]
                            else:
                                src = wdn_bf[e_or_ic].rearrange("(k p) h -> p k h", p=128)[
                                    :, :, ht * 128 : (ht + 1) * 128]
                            s_w4.inc(
                                sy.dma_start(
                                    out=wdn_s[:, (ht % 4) * 8 * 128 : (ht % 4 + 1) * 8 * 128],
                                    in_=src,
                                ),
                                (sid, ht),
                            )

                    @blk.tensor
                    def _(t):
                        for ht in range(KT):
                            s_w4.wait(t, (sid, ht))
                            q = (ht % 2) * NB
                            if ht >= 2:
                                t.wait_ge(s_eo, eo0 + (ht - 1) * NB)
                            for ik in range(8):
                                for tb in range(NB):
                                    inst = t.matmul(
                                        ps_q[q + tb][:, :],
                                        wdn_s[:, (ht % 4) * 8 * 128 + ik * 128 : (ht % 4) * 8 * 128 + (ik + 1) * 128],
                                        actb[:, ik * TC + tb * BLK : ik * TC + (tb + 1) * BLK],
                                        start=(ik == 0), stop=(ik == 7),
                                    )
                            inst.then_inc(s_mm, 1)

                    @blk.vector
                    def _(v):
                        for ht in range(KT):
                            q = (ht % 2) * NB
                            v.wait_ge(s_mm, mm0 + ht + 1)
                            for tb in range(NB):
                                sl = racc[:, ht * TC + tb * BLK : ht * TC + (tb + 1) * BLK]
                                v.tensor_tensor(sl, sl, ps_q[q + tb][:, :],
                                                ALU.add).then_inc(s_eo, 1)

                    mmv[0] = mm0 + KT
                    eov[0] = eo0 + KT * NB

            for e in range(E):
                with nc.Block() as blk:
                    mm0 = mmv[0]
                    cb0 = cbv[0]

                    @blk.tensor
                    def _(t):
                        t.wait_ge(s_rt, RT_DONE)
                        for tb in range(NB):
                            if tb >= 1:
                                t.wait_ge(s_cb, cb0 + tb)
                            t.matmul(ps_q[0][:, :], ohm[0:8, e * 128 : (e + 1) * 128],
                                     comb_cm[0:8, tb * BLK : (tb + 1) * BLK],
                                     start=True, stop=True).then_inc(s_mm, 1)

                    @blk.vector
                    def _(v):
                        for tb in range(NB):
                            v.wait_ge(s_mm, mm0 + tb + 1)
                            v.tensor_copy(cbb[:, tb * BLK : (tb + 1) * BLK],
                                          ps_q[0][:, :]).then_inc(s_cb, 1)

                    mmv[0] = mm0 + NB
                    cbv[0] = cb0 + NB

                gu_stage(False, e)
                down_stage(False, e)

            for ic in range(4):
                gu_stage(True, ic)
                down_stage(True, ic)

            with nc.Block() as blk:

                @blk.sync
                def _(sy):
                    for ct in range(KT):
                        sy.dma_start(
                            out=ffn_bf[ct * 128 : (ct + 1) * 128, :],
                            in_=racc[:, ct * TC : (ct + 1) * TC],
                        ).then_inc(s_wg, 16)
                    sy.wait_ge(s_wg, 16 * KT + 16 * KT)

        with contextlib.ExitStack() as sc:
            e7 = sc.enter_context
            fnt = e7(nc.sbuf_tensor("fnt", [128, 2 * D], BF16))
            dump = e7(nc.sbuf_tensor("dump7", [128, D], F32))
            x2l = e7(nc.sbuf_tensor("x2l", [128, 2 * D], F32))
            yb = e7(nc.sbuf_tensor("yb", [128, 2 * D], F32))
            s_tp = PS(en, nc, "s7_tp")
            s_ld = PS(en, nc, "s7_ld")
            s_st = PS(en, nc, "s7_st")
            for _j in range(TT):
                s_tp.reserve(_j)
            for _j in range(TT):
                s_ld.reserve(_j)
            for _j in range(TT):
                s_st.reserve(_j)
            s_v = en(nc.semaphore("s7_v"))
            s_sc = en(nc.semaphore("s7_sc"))
            s_fx = en(nc.semaphore("s7_fx"))

            with nc.Block() as blk:

                @blk.sync
                def _(sy):
                    for j in range(TT):
                        if j >= 2:
                            sy.wait_ge(s_v, 4 * (j - 1))
                        s_ld.inc(
                            sy.dma_start(
                                out=x2l[:, (j % 2) * D : (j % 2 + 1) * D],
                                in_=x2_d[j * 128 : (j + 1) * 128, :],
                            ),
                            j,
                        )
                        if j >= 1:
                            sy.wait_ge(s_v, 4 * j)
                            s_st.inc(
                                sy.dma_start(
                                    out=y_out[(j - 1) * 128 : j * 128, :],
                                    in_=yb[:, ((j - 1) % 2) * D : ((j - 1) % 2 + 1) * D],
                                ),
                                j - 1,
                            )
                    sy.wait_ge(s_v, 4 * TT)
                    s_st.inc(
                        sy.dma_start(
                            out=y_out[(TT - 1) * 128 : TT * 128, :],
                            in_=yb[:, ((TT - 1) % 2) * D : ((TT - 1) % 2 + 1) * D],
                        ),
                        TT - 1,
                    )
                    s_st.wait_all(sy)

                @blk.scalar
                def _(s):
                    s_tp.inc(
                        s.dma_start_transpose(fnt[:, 0:D], ffn_bf[:, 0:128]), 0
                    )
                    s_tp.inc(
                        s.dma_start_transpose(fnt[:, D : 2 * D], ffn_bf[:, 128:256]), 1
                    )
                    for j in range(TT):
                        c = j * 8
                        f_ = fnt[:, (j % 2) * D : (j % 2 + 1) * D]
                        s_tp.wait(s, j)
                        s.activation(out=dump[:, :], in_=f_, func=AF.Square,
                                     accum_out=stt[:, c + 1 : c + 2]).then_inc(s_sc, 1)
                        s.wait_ge(s_sc, 2 * j + 1)
                        s.wait_ge(s_v, 4 * j + 2)
                        s.activation(out=stt[:, c + 4 : c + 5],
                                     in_=stt[:, c + 1 : c + 2], func=AF.Sqrt,
                                     scale=1.0 / D,
                                     bias=stt[:, c + 3 : c + 4]).then_inc(s_sc, 1)
                        if j + 2 < TT:
                            s.wait_ge(s_v, 4 * (j + 1))
                            s_tp.inc(
                                s.dma_start_transpose(
                                    fnt[:, (j % 2) * D : (j % 2 + 1) * D],
                                    ffn_bf[:, (j + 2) * 128 : (j + 3) * 128],
                                ),
                                j + 2,
                            )

                @blk.vector
                def _(v):
                    fx = 0
                    for j in range(TT):
                        c = j * 8
                        f_ = fnt[:, (j % 2) * D : (j % 2 + 1) * D]
                        x2_ = x2l[:, (j % 2) * D : (j % 2 + 1) * D]
                        y_ = yb[:, (j % 2) * D : (j % 2 + 1) * D]
                        m = stt[:, c + 2 : c + 3]
                        s_tp.wait(v, j)
                        v.tensor_reduce(stt[:, c : c + 1], f_, AX.X,
                                        ALU.add).then_inc(s_fx, 1)
                        fx += 1
                        v.wait_ge(s_fx, fx)
                        v.tensor_scalar(m, stt[:, c : c + 1], 1.0 / D, None,
                                        ALU.mult).then_inc(s_v, 1)
                        v.wait_ge(s_v, 4 * j + 1)
                        v.tensor_mul(stt[:, c + 3 : c + 4], m, m).then_inc(s_fx, 1)
                        fx += 1
                        v.wait_ge(s_fx, fx)
                        v.tensor_scalar(stt[:, c + 3 : c + 4],
                                        stt[:, c + 3 : c + 4], -1.0, EPS,
                                        ALU.mult, ALU.add).then_inc(s_v, 1)
                        v.wait_ge(s_sc, 2 * (j + 1))
                        v.reciprocal(stt[:, c + 5 : c + 6],
                                     stt[:, c + 4 : c + 5]).then_inc(s_fx, 1)
                        fx += 1
                        v.wait_ge(s_fx, fx)
                        s_ld.wait(v, j)
                        if j >= 2:
                            s_st.wait(v, j - 2)
                        v.tensor_scalar(y_, f_, m, stt[:, c + 5 : c + 6],
                                        ALU.subtract, ALU.mult).then_inc(s_v, 1)
                        v.wait_ge(s_v, 4 * j + 3)
                        v.tensor_add(y_, y_, x2_).then_inc(s_v, 1)


    if debug:
        with nc.semaphore("s_dbg") as s_dbg, nc.Block() as blk:

            @blk.gpsimd
            def _(g):
                n = 0
                for name, (o, s) in dbg.items():
                    g.dma_start(out=o[:, :], in_=s[:, :]).then_inc(s_dbg, 16)
                    n += 16
                g.wait_ge(s_dbg, n)

    return nc


_NC_CACHE = None


def kernel(**inputs):
    global _NC_CACHE
    x = np.ascontiguousarray(np.asarray(inputs["L0_attn_x"], np.float32))
    wnames = {
        "A": "L0_attn_A", "B": "L0_attn_B", "C_w": "L0_attn_C_w",
        "C_b": "L0_attn_C_b", "D_w": "L0_attn_D_w", "D_b": "L0_attn_D_b",
        "log_delta": "L0_attn_log_delta", "w_gate": "L0_ffn_w_gate",
        "w_gate_up": "L0_ffn_w_gate_up", "w_down": "L0_ffn_w_down",
        "w_sh_gate_up": "L0_ffn_w_sh_gate_up", "w_sh_down": "L0_ffn_w_sh_down",
    }
    weights = {k: np.ascontiguousarray(np.asarray(inputs[v], np.float32))
               for k, v in wnames.items()}

    if _NC_CACHE is None:
        _NC_CACHE = build_nc()
    nc = _NC_CACHE

    in_maps = []
    for c in range(8):
        b, q = c // 4, c % 4
        shard = np.ascontiguousarray(x[b, q * TC : (q + 1) * TC, :])
        sel = np.zeros((3, 8), np.float32)
        for s_ in range(3):
            r = q - 1 - s_
            if r >= 0:
                sel[s_, 4 * b + r] = 1.0
        m = {"x": shard, "sel": sel}
        m.update(weights)
        in_maps.append(m)

    res = run_bass_kernel_spmd(nc, in_maps, core_ids=list(range(8)))
    out = np.empty((2, 8192, D), np.float32)
    for c in range(8):
        b, q = c // 4, c % 4
        out[b, q * TC : (q + 1) * TC, :] = res.results[c]["y"]
    return out


if __name__ == "__main__":
    nc = build_nc()
    print("built ok")


# version 07
# speedup vs baseline: 1.0134x; 1.0134x over previous
import contextlib
import numpy as np
import concourse.bass as bass
import concourse.mybir as mybir
from concourse.bass_utils import run_bass_kernel_spmd

F32 = mybir.dt.float32
BF16 = mybir.dt.bfloat16
U32 = mybir.dt.uint32
AF = mybir.ActivationFunctionType
ALU = mybir.AluOpType
AX = mybir.AxisListType

D = 2048
TC = 2048
KT = 16
TT = 16
NB = 4
BLK = 512
E = 8
IM = 1024
ISH = 4096
EPS = 1e-5


class PS:

    def __init__(self, en, nc, name, ways=2):
        self.s = [en(nc.semaphore(f"{name}_{i}")) for i in range(ways)]
        self.c = [0] * ways
        self.ways = ways
        self.n = 0
        self.mark = {}

    def reserve(self, key):
        assert key not in self.mark
        w = self.n % self.ways
        self.n += 1
        self.c[w] += 16
        self.mark[key] = (w, self.c[w])

    def inc(self, inst, key):
        w, _ = self.mark[key]
        inst.then_inc(self.s[w], 16)

    def wait(self, eng, key):
        w, v = self.mark[key]
        eng.wait_ge(self.s[w], v)

    def wait_all(self, eng):
        for w in range(self.ways):
            if self.c[w]:
                eng.wait_ge(self.s[w], self.c[w])


def build_nc(tc=2048, blk=512, debug=False):
    global TC, TT, NB, BLK
    TC, BLK = tc, blk
    TT, NB = TC // 128, TC // BLK

    nc = bass.Bass("TRN2", target_bir_lowering=False)

    x_in = nc.dram_tensor("x", [TC, D], F32, kind="ExternalInput").ap()
    sel_in = nc.dram_tensor("sel", [3, 8], F32, kind="ExternalInput").ap()
    A_in = nc.dram_tensor("A", [D, 64], F32, kind="ExternalInput").ap()
    B_in = nc.dram_tensor("B", [D, 64], F32, kind="ExternalInput").ap()
    Cw_in = nc.dram_tensor("C_w", [D, D], F32, kind="ExternalInput").ap()
    Cb_in = nc.dram_tensor("C_b", [D], F32, kind="ExternalInput").ap()
    Dw_in = nc.dram_tensor("D_w", [D, D], F32, kind="ExternalInput").ap()
    Db_in = nc.dram_tensor("D_b", [D], F32, kind="ExternalInput").ap()
    ld_in = nc.dram_tensor("log_delta", [D], F32, kind="ExternalInput").ap()
    wg_in = nc.dram_tensor("w_gate", [D, E], F32, kind="ExternalInput").ap()
    wgu_in = nc.dram_tensor("w_gate_up", [E, D, 2 * IM], F32, kind="ExternalInput").ap()
    wdn_in = nc.dram_tensor("w_down", [E, IM, D], F32, kind="ExternalInput").ap()
    wsg_in = nc.dram_tensor("w_sh_gate_up", [D, 2 * ISH], F32, kind="ExternalInput").ap()
    wsd_in = nc.dram_tensor("w_sh_down", [ISH, D], F32, kind="ExternalInput").ap()
    y_out = nc.dram_tensor("y", [TC, D], F32, kind="ExternalOutput").ap()

    cw_bf = nc.dram_tensor("cw_bf", [D, D], BF16).ap()
    dw_bf = nc.dram_tensor("dw_bf", [D, D], BF16).ap()
    wgu_bf = nc.dram_tensor("wgu_bf", [E, D, 2 * IM], BF16).ap()
    wdn_bf = nc.dram_tensor("wdn_bf", [E, IM, D], BF16).ap()
    wsg_bf = nc.dram_tensor("wsg_bf", [D, 2 * ISH], BF16).ap()
    wsd_bf = nc.dram_tensor("wsd_bf", [ISH, D], BF16).ap()
    xln_tok = nc.dram_tensor("xln_tok", [TC, D], BF16).ap()
    xproj_d = nc.dram_tensor("xproj_d", [D, TC], F32).ap()
    ao_bf = nc.dram_tensor("ao_bf", [D, TC], BF16).ap()
    x2_d = nc.dram_tensor("x2_d", [TC, D], F32).ap()
    xln3_tok = nc.dram_tensor("xln3_tok", [TC, D], BF16).ap()
    ffn_bf = nc.dram_tensor("ffn_bf", [D, TC], BF16).ap()
    li_b = nc.dram_tensor("lasts_in", [128, KT], F32).ap()
    lo_b = nc.dram_tensor("lasts_out", [8 * 128, KT], F32).ap()
    if debug:
        dbg = {
            "d_xln": (nc.dram_tensor("d_xln", [TC, D], F32, kind="ExternalOutput").ap(), xln_tok),
            "d_xp": (nc.dram_tensor("d_xp", [D, TC], F32, kind="ExternalOutput").ap(), xproj_d),
            "d_ao": (nc.dram_tensor("d_ao", [D, TC], F32, kind="ExternalOutput").ap(), ao_bf),
            "d_x2": (nc.dram_tensor("d_x2", [TC, D], F32, kind="ExternalOutput").ap(), x2_d),
            "d_x3": (nc.dram_tensor("d_x3", [TC, D], F32, kind="ExternalOutput").ap(), xln3_tok),
            "d_ffn": (nc.dram_tensor("d_ffn", [D, TC], F32, kind="ExternalOutput").ap(), ffn_bf),
        }

    stk = contextlib.ExitStack()
    with stk:
        en = stk.enter_context

        adiag = en(nc.sbuf_tensor("adiag", [128, KT], F32))
        a2048 = en(nc.sbuf_tensor("a2048", [128, KT], F32))
        asq = en(nc.sbuf_tensor("asq", [128, KT], F32))
        ab = en(nc.sbuf_tensor("ab", [128, KT], F32))
        cb_s = en(nc.sbuf_tensor("cb_s", [128, KT], F32))
        db_s = en(nc.sbuf_tensor("db_s", [128, KT], F32))
        ldl = en(nc.sbuf_tensor("ldl", [128, KT], F32))
        abt = en(nc.sbuf_tensor("abt", [128, 2 * 64], F32))
        bbt = en(nc.sbuf_tensor("bbt", [128, 2 * 64], F32))
        lasts = en(nc.sbuf_tensor("lasts", [128, KT], F32))
        lall = en(nc.sbuf_tensor("lall", [128, 8 * KT], F32))
        gcar = en(nc.sbuf_tensor("gcar", [128, KT], F32))
        pred = en(nc.sbuf_tensor("pred", [128, 3 * KT], F32))
        selr = en(nc.sbuf_tensor("selr", [128, 24], F32))
        selb = en(nc.sbuf_tensor("selb", [128, 24], F32))
        onesr = en(nc.sbuf_tensor("onesr", [128, 128], F32))
        stt = en(nc.sbuf_tensor("stt", [128, 16 * 8], F32))

        s_p0 = en(nc.semaphore("s_p0"))
        s_ab0 = en(nc.semaphore("s_ab0"))
        s_ab1 = en(nc.semaphore("s_ab1"))
        s_abp = [s_ab0, s_ab1]
        s_c_cd = en(nc.semaphore("s_c_cd"))
        s_c_sh = en(nc.semaphore("s_c_sh"))
        s_c_gu = en(nc.semaphore("s_c_gu"))
        s_c_dn = en(nc.semaphore("s_c_dn"))
        s_pre = en(nc.semaphore("s_pre"))
        s_sel = en(nc.semaphore("s_sel"))
        s_fx0 = en(nc.semaphore("s_fx0"))

        CAST_CD = 2 * 16
        CAST_SH = 32
        CAST_GU = E * 16
        CAST_DN = E * 16
        P0_PARAMS = 49 * 16

        def p0_bodies(blk):

            @blk.gpsimd
            def _(g):
                g.dma_start(out=cw_bf[:, :], in_=Cw_in[:, :]).then_inc(s_c_cd, 16)
                g.dma_start(out=dw_bf[:, :], in_=Dw_in[:, :]).then_inc(s_c_cd, 16)

            @blk.sync
            def _(sy):
                for j in range(KT):
                    sl = slice(j * 128, (j + 1) * 128)
                    o1 = lambda t: t.rearrange("(p o) -> p o", o=1)
                    sy.dma_start(out=ldl[:, j : j + 1], in_=o1(ld_in[sl])).then_inc(s_p0, 16)
                    sy.dma_start(out=cb_s[:, j : j + 1], in_=o1(Cb_in[sl])).then_inc(s_p0, 16)
                    sy.dma_start(out=db_s[:, j : j + 1], in_=o1(Db_in[sl])).then_inc(s_p0, 16)
                sy.dma_start(
                    out=selr[0:1, 0:24],
                    in_=sel_in.rearrange("a b -> (a b)").rearrange("(o f) -> o f", o=1),
                ).then_inc(s_p0, 16)
                for j in range(KT):
                    if j >= 2:
                        sy.wait_ge(s_pre, 3 + j - 1)
                    p = j % 2
                    sy.dma_start(out=abt[:, p * 64 : (p + 1) * 64], in_=A_in[j * 128 : (j + 1) * 128, :]).then_inc(s_abp[p], 16)
                    sy.dma_start(out=bbt[:, p * 64 : (p + 1) * 64], in_=B_in[j * 128 : (j + 1) * 128, :]).then_inc(s_abp[p], 16)

            @blk.scalar
            def _(s):
                s.wait_ge(s_p0, P0_PARAMS)
                s.activation(out=adiag[:, :], in_=ldl[:, :], func=AF.Sigmoid,
                             scale=-1.0).then_inc(s_pre, 1)

            @blk.vector
            def _(v):
                fx0 = [0]

                def edge(inst):
                    inst.then_inc(s_fx0, 1)
                    fx0[0] += 1
                    v.wait_ge(s_fx0, fx0[0])

                v.wait_ge(s_pre, 1)
                edge(v.tensor_mul(asq[:, :], adiag[:, :], adiag[:, :]))
                cur, oth = asq, a2048
                for _ in range(TC.bit_length() - 2):
                    edge(v.tensor_mul(oth[:, :], cur[:, :], cur[:, :]))
                    cur, oth = oth, cur
                if cur is not a2048:
                    edge(v.tensor_copy(a2048[:, :], cur[:, :]))
                v.memset(onesr[0:1, :], 1.0).then_inc(s_pre, 2)
                for j in range(KT):
                    p = j % 2
                    v.wait_ge(s_abp[p], 32 * (j // 2 + 1))
                    edge(v.tensor_mul(bbt[:, p * 64 : (p + 1) * 64],
                                      abt[:, p * 64 : (p + 1) * 64],
                                      bbt[:, p * 64 : (p + 1) * 64]))
                    v.tensor_reduce(ab[:, j : j + 1],
                                    bbt[:, p * 64 : (p + 1) * 64], AX.X,
                                    ALU.add).then_inc(s_pre, 1)

            @blk.tensor
            def _(t):
                t.wait_ge(s_pre, 3)
                t.wait_ge(s_p0, P0_PARAMS)
                t.matmul(pss[:, 0:24], onesr[0:1, :], selr[0:1, 0:24],
                         start=True, stop=True).then_inc(s_sel, 1)

        with contextlib.ExitStack() as sc:
            e1 = sc.enter_context
            xb = e1(nc.sbuf_tensor("xb", [128, 2 * D], F32))
            xlnb = e1(nc.sbuf_tensor("xlnb", [128, 2 * D], BF16))
            dump = e1(nc.sbuf_tensor("dump1", [128, D], F32))
            pss = e1(nc.psum_tensor("pss", [128, 32], F32))
            s_ld = PS(en, nc, "s1_ld")
            s_st = PS(en, nc, "s1_st")
            for _j in range(TT):
                s_ld.reserve(_j)
            for _j in range(TT):
                s_st.reserve(_j)
            s_v = en(nc.semaphore("s1_v"))
            s_sc = en(nc.semaphore("s1_sc"))
            s_fx = en(nc.semaphore("s1_fx"))

            with nc.Block() as blk:
                p0_bodies(blk)

                @blk.sync
                def _(sy):
                    for j in range(TT):
                        if j >= 2:
                            sy.wait_ge(s_v, 4 * (j - 1))
                            sy.wait_ge(s_sc, 2 * (j - 1))
                            s_st.wait(sy, j - 2)
                        s_ld.inc(
                            sy.dma_start(
                                out=xb[:, (j % 2) * D : (j % 2 + 1) * D],
                                in_=x_in[j * 128 : (j + 1) * 128, :],
                            ),
                            j,
                        )
                        if j >= 1:
                            sy.wait_ge(s_v, 4 * j)
                            s_st.inc(
                                sy.dma_start(
                                    out=xln_tok[(j - 1) * 128 : j * 128, :],
                                    in_=xlnb[:, ((j - 1) % 2) * D : ((j - 1) % 2 + 1) * D],
                                ),
                                j - 1,
                            )
                    sy.wait_ge(s_v, 4 * TT)
                    s_st.inc(
                        sy.dma_start(
                            out=xln_tok[(TT - 1) * 128 : TT * 128, :],
                            in_=xlnb[:, ((TT - 1) % 2) * D : ((TT - 1) % 2 + 1) * D],
                        ),
                        TT - 1,
                    )
                    s_st.wait_all(sy)

                @blk.scalar
                def _(s):
                    for j in range(TT):
                        xj = xb[:, (j % 2) * D : (j % 2 + 1) * D]
                        c = j * 8
                        s_ld.wait(s, j)
                        s.activation(out=dump[:, :], in_=xj, func=AF.Square,
                                     accum_out=stt[:, c + 1 : c + 2]).then_inc(s_sc, 1)
                        s.wait_ge(s_sc, 2 * j + 1)
                        s.wait_ge(s_v, 4 * j + 2)
                        s.activation(out=stt[:, c + 4 : c + 5],
                                     in_=stt[:, c + 1 : c + 2], func=AF.Sqrt,
                                     scale=1.0 / D,
                                     bias=stt[:, c + 3 : c + 4]).then_inc(s_sc, 1)

                @blk.vector
                def _(v):
                    fx = 0
                    for j in range(TT):
                        xj = xb[:, (j % 2) * D : (j % 2 + 1) * D]
                        c = j * 8
                        m = stt[:, c + 2 : c + 3]
                        s_ld.wait(v, j)
                        v.tensor_reduce(stt[:, c : c + 1], xj, AX.X,
                                        ALU.add).then_inc(s_fx, 1)
                        fx += 1
                        v.wait_ge(s_fx, fx)
                        v.tensor_scalar(m, stt[:, c : c + 1], 1.0 / D, None,
                                        ALU.mult).then_inc(s_v, 1)
                        v.wait_ge(s_v, 4 * j + 1)
                        v.tensor_mul(stt[:, c + 3 : c + 4], m, m).then_inc(s_fx, 1)
                        fx += 1
                        v.wait_ge(s_fx, fx)
                        v.tensor_scalar(stt[:, c + 3 : c + 4],
                                        stt[:, c + 3 : c + 4], -1.0, EPS,
                                        ALU.mult, ALU.add).then_inc(s_v, 1)
                        v.wait_ge(s_sc, 2 * (j + 1))
                        v.reciprocal(stt[:, c + 5 : c + 6],
                                     stt[:, c + 4 : c + 5]).then_inc(s_fx, 1)
                        fx += 1
                        v.wait_ge(s_fx, fx)
                        v.tensor_scalar(xlnb[:, (j % 2) * D : (j % 2 + 1) * D],
                                        xj, m, stt[:, c + 5 : c + 6],
                                        ALU.subtract, ALU.mult).then_inc(s_v, 2)

        
                @blk.vector
                def _(v):
                    v.wait_ge(s_sel, 1)
                    v.tensor_copy(selb[:, :], pss[:, 0:24]).then_inc(s_sel, 1)

        with contextlib.ExitStack() as sc:
            e2 = sc.enter_context
            xlnT = e2(nc.sbuf_tensor("xlnT", [128, KT * TC], BF16))
            cwT = e2(nc.sbuf_tensor("cwT", [128, KT * 1024], BF16))
            dwT = e2(nc.sbuf_tensor("dwT", [128, KT * 1024], BF16))
            sdb = e2(nc.sbuf_tensor("sdb", [128, NB * BLK], F32))
            xpb = e2(nc.sbuf_tensor("xpb", [128, 2 * BLK], F32))
            psC = [e2(nc.psum_tensor(f"psC{i}", [128, BLK], F32)) for i in range(NB)]
            psD = [e2(nc.psum_tensor(f"psD{i}", [128, BLK], F32)) for i in range(NB)]
            s_tp = en(nc.semaphore("s2_tp"))
            s_w = en(nc.semaphore("s2_w"))
            s_mm = en(nc.semaphore("s2_mm"))
            s_sg = en(nc.semaphore("s2_sg"))
            s_ev = en(nc.semaphore("s2_ev"))
            s_st = PS(en, nc, "s2_st")
            GH = NB * 8
            for _i in range(2 * GH):
                s_st.reserve(_i)

            with nc.Block() as blk:

                @blk.gpsimd
                def _(g):
                    g.wait_ge(s_c_cd, CAST_CD)
                    for e in range(E):
                        g.dma_start(out=wgu_bf[e, :, :], in_=wgu_in[e, :, :]).then_inc(
                            s_c_gu, 16
                        )
                    for e in range(E):
                        g.dma_start(out=wdn_bf[e, :, :], in_=wdn_in[e, :, :]).then_inc(
                            s_c_dn, 16
                        )
                    g.dma_start(out=wsg_bf[:, :], in_=wsg_in[:, :]).then_inc(s_c_sh, 16)
                    g.dma_start(out=wsd_bf[:, :], in_=wsd_in[:, :]).then_inc(s_c_sh, 16)

                @blk.scalar
                def _(s):
                    for ct in range(KT):
                        s.dma_start_transpose(
                            xlnT[:, ct * TC : (ct + 1) * TC],
                            xln_tok[:, ct * 128 : (ct + 1) * 128],
                        ).then_inc(s_tp, 16)
                    s.wait_ge(s_c_cd, CAST_CD)
                    for k in range(KT):
                        s.dma_start_transpose(
                            cwT[:, k * 1024 : (k + 1) * 1024],
                            cw_bf[0:1024, k * 128 : (k + 1) * 128],
                        ).then_inc(s_w, 16)
                        s.dma_start_transpose(
                            dwT[:, k * 1024 : (k + 1) * 1024],
                            dw_bf[0:1024, k * 128 : (k + 1) * 128],
                        ).then_inc(s_w, 16)

                @blk.sync
                def _(sy):
                    for i in range(2 * GH):
                        g, tb = i // NB, i % NB
                        sy.wait_ge(s_ev, i + 1)
                        s_st.inc(
                            sy.dma_start(
                                out=xproj_d[g * 128 : (g + 1) * 128,
                                            tb * BLK : (tb + 1) * BLK],
                                in_=xpb[:, (i % 2) * BLK : (i % 2 + 1) * BLK],
                            ),
                            i,
                        )
                    s_st.wait_all(sy)

                @blk.tensor
                def _(t):
                    t.wait_ge(s_tp, 16 * KT)
                    for oh in range(2):
                        t.wait_ge(s_w, 32 * 16 * (oh + 1))
                        for oc8 in range(8):
                            g = oh * 8 + oc8
                            if g >= 1:
                                t.wait_ge(s_ev, NB * g)
                            for k in range(KT):
                                for tb in range(NB):
                                    t.matmul(
                                        psC[tb][:, :],
                                        cwT[:, k * 1024 + oc8 * 128 : k * 1024 + (oc8 + 1) * 128],
                                        xlnT[:, k * TC + tb * BLK : k * TC + (tb + 1) * BLK],
                                        start=(k == 0), stop=(k == KT - 1),
                                    )
                            for k in range(KT):
                                for tb in range(NB):
                                    inst = t.matmul(
                                        psD[tb][:, :],
                                        dwT[:, k * 1024 + oc8 * 128 : k * 1024 + (oc8 + 1) * 128],
                                        xlnT[:, k * TC + tb * BLK : k * TC + (tb + 1) * BLK],
                                        start=(k == 0), stop=(k == KT - 1),
                                    )
                            inst.then_inc(s_mm, 1)

                @blk.scalar
                def _(s):
                    for g in range(16):
                        if g == 8:
                            s.wait_ge(s_mm, 8)
                            for k in range(KT):
                                s.dma_start_transpose(
                                    cwT[:, k * 1024 : (k + 1) * 1024],
                                    cw_bf[1024:2048, k * 128 : (k + 1) * 128],
                                ).then_inc(s_w, 16)
                                s.dma_start_transpose(
                                    dwT[:, k * 1024 : (k + 1) * 1024],
                                    dw_bf[1024:2048, k * 128 : (k + 1) * 128],
                                ).then_inc(s_w, 16)
                        oc = g
                        s.wait_ge(s_mm, g + 1)
                        for tb in range(NB):
                            if g >= 1:
                                s.wait_ge(s_ev, NB * g)
                            s.activation(
                                out=sdb[:, tb * BLK : (tb + 1) * BLK],
                                in_=psD[tb][:, :], func=AF.Sigmoid,
                                bias=db_s[:, oc : oc + 1], scale=1.0,
                            ).then_inc(s_sg, 1)

                @blk.vector
                def _(v):
                    for g in range(16):
                        oc = g
                        for tb in range(NB):
                            i = g * NB + tb
                            v.wait_ge(s_sg, i + 1)
                            if i >= 2:
                                s_st.wait(v, i - 2)
                            v.scalar_tensor_tensor(
                                xpb[:, (i % 2) * BLK : (i % 2 + 1) * BLK],
                                psC[tb][:, :], cb_s[:, oc : oc + 1],
                                sdb[:, tb * BLK : (tb + 1) * BLK],
                                ALU.add, ALU.mult,
                            ).then_inc(s_ev, 1)

        with contextlib.ExitStack() as sc:
            e3 = sc.enter_context
            gs = e3(nc.sbuf_tensor("gs", [128, KT * TC], BF16))
            xpl = e3(nc.sbuf_tensor("xpl", [128, 2 * TC], F32))
            aob = e3(nc.sbuf_tensor("aob", [128, 2 * TC], BF16))
            ztile = e3(nc.sbuf_tensor("ztile", [128, TC], F32))
            zeros = e3(nc.sbuf_tensor("zeros", [128, TC], F32))
            abc = e3(nc.sbuf_tensor("abc", [128, TC], F32))
            s_ld = PS(en, nc, "s3_ld")
            s_gx = en(nc.semaphore("s3_gx"))
            s_v = en(nc.semaphore("s3_v"))
            s_cc = en(nc.semaphore("s3_cc"))
            s_st = PS(en, nc, "s3_st")
            s_fx = en(nc.semaphore("s3_fx"))
            for _ct in range(KT):
                s_ld.reserve(_ct)
            for _ct in range(KT):
                s_st.reserve(_ct)

            with nc.Block() as blk:

                @blk.sync
                def _(sy):
                    for ct in range(KT):
                        if ct >= 2:
                            sy.wait_ge(s_v, 2 * (ct - 1))
                        s_ld.inc(
                            sy.dma_start(
                                out=xpl[:, (ct % 2) * TC : (ct % 2 + 1) * TC],
                                in_=xproj_d[ct * 128 : (ct + 1) * 128, :],
                            ),
                            ct,
                        )
                    sy.wait_ge(s_v, 2 * KT)
                    sy.dma_start(out=li_b[:, :], in_=lasts[:, :]).then_inc(s_gx, 16)
                    sy.wait_ge(s_cc, 1)
                    for r in range(8):
                        sy.dma_start(
                            out=lall[:, r * KT : (r + 1) * KT],
                            in_=lo_b[r * 128 : (r + 1) * 128, :],
                        ).then_inc(s_gx, 16)

                @blk.vector
                def _(v):
                    fx4 = 0
                    v.memset(zeros[:, :], 0.0).then_inc(s_fx, 1)
                    fx4 += 1
                    v.wait_ge(s_fx, fx4)
                    for ct in range(KT):
                        s_ld.wait(v, ct)
                        v.tensor_single_scalar(abc[:, :], zeros[:, :],
                                               adiag[:, ct : ct + 1],
                                               ALU.add).then_inc(s_fx, 1)
                        fx4 += 1
                        v.wait_ge(s_fx, fx4)
                        v.tensor_tensor_scan(
                            gs[:, ct * TC : (ct + 1) * TC], abc[:, :],
                            xpl[:, (ct % 2) * TC : (ct % 2 + 1) * TC],
                            0.0, ALU.mult, ALU.add,
                        ).then_inc(s_v, 1)
                        v.wait_ge(s_v, 2 * ct + 1)
                        v.tensor_copy(
                            lasts[:, ct : ct + 1],
                            gs[:, ct * TC + TC - 1 : ct * TC + TC],
                        ).then_inc(s_v, 1)

                @blk.gpsimd
                def _(g):
                    g.wait_ge(s_gx, 16)
                    g.collective_compute(
                        "AllGather", ALU.bypass,
                        replica_groups=[list(range(8))],
                        ins=[li_b.opt()], outs=[lo_b.opt()],
                    ).then_inc(s_cc, 1)

                @blk.vector
                def _(v):
                    fx6 = [KT + 1]

                    def edge(inst):
                        inst.then_inc(s_fx, 1)
                        fx6[0] += 1
                        v.wait_ge(s_fx, fx6[0])

                    v.wait_ge(s_gx, 16 * 9)
                    for s_ in range(3):
                        edge(v.tensor_scalar_mul(pred[:, s_ * KT : (s_ + 1) * KT],
                                                 lall[:, 0:KT],
                                                 selb[:, s_ * 8 : s_ * 8 + 1]))
                        for r in range(1, 8):
                            edge(v.scalar_tensor_tensor(
                                pred[:, s_ * KT : (s_ + 1) * KT],
                                lall[:, r * KT : (r + 1) * KT],
                                selb[:, s_ * 8 + r : s_ * 8 + r + 1],
                                pred[:, s_ * KT : (s_ + 1) * KT],
                                ALU.mult, ALU.add,
                            ))
                    edge(v.tensor_mul(gcar[:, :], a2048[:, :],
                                      pred[:, 2 * KT : 3 * KT]))
                    edge(v.tensor_add(gcar[:, :], gcar[:, :],
                                      pred[:, KT : 2 * KT]))
                    edge(v.tensor_mul(gcar[:, :], a2048[:, :], gcar[:, :]))
                    v.tensor_add(gcar[:, :], gcar[:, :],
                                 pred[:, 0:KT]).then_inc(s_v, 1)
                    v.wait_ge(s_v, 2 * KT + 1)
                    for ct in range(KT):
                        if ct >= 2:
                            s_st.wait(v, ct - 2)
                        edge(v.tensor_single_scalar(abc[:, :], zeros[:, :],
                                                    adiag[:, ct : ct + 1],
                                                    ALU.add))
                        edge(v.tensor_tensor_scan(
                            ztile[:, :], abc[:, :], zeros[:, :],
                            gcar[:, ct : ct + 1], ALU.mult, ALU.add,
                        ))
                        edge(v.tensor_add(ztile[:, :],
                                          gs[:, ct * TC : (ct + 1) * TC],
                                          ztile[:, :]))
                        v.tensor_scalar_mul(
                            aob[:, (ct % 2) * TC : (ct % 2 + 1) * TC],
                            ztile[:, :], ab[:, ct : ct + 1],
                        ).then_inc(s_v, 1)

                @blk.sync
                def _(sy):
                    for ct in range(KT):
                        sy.wait_ge(s_v, 2 * KT + 1 + ct + 1)
                        s_st.inc(
                            sy.dma_start(
                                out=ao_bf[ct * 128 : (ct + 1) * 128, :],
                                in_=aob[:, (ct % 2) * TC : (ct % 2 + 1) * TC],
                            ),
                            ct,
                        )
                    s_st.wait_all(sy)

        with contextlib.ExitStack() as sc:
            e4 = sc.enter_context
            aot = e4(nc.sbuf_tensor("aot", [128, 2 * D], BF16))
            dump = e4(nc.sbuf_tensor("dump4", [128, D], F32))
            xb2 = e4(nc.sbuf_tensor("xb2", [128, 2 * D], F32))
            x2b = e4(nc.sbuf_tensor("x2b", [128, 2 * D], F32))
            x3b = e4(nc.sbuf_tensor("x3b", [128, 2 * D], BF16))
            s_tp = PS(en, nc, "s4_tp")
            s_ld = PS(en, nc, "s4_ld")
            s_st = PS(en, nc, "s4_st", ways=4)
            for _j in range(TT):
                s_tp.reserve(_j)
            for _j in range(TT):
                s_ld.reserve(_j)
            for _j in range(TT):
                s_st.reserve((_j, 0))
                s_st.reserve((_j, 1))
            s_v = en(nc.semaphore("s4_v"))
            s_sc = en(nc.semaphore("s4_sc"))
            s_fx = en(nc.semaphore("s4_fx"))

            with nc.Block() as blk:

                @blk.sync
                def _(sy):
                    for j in range(TT):
                        if j >= 2:
                            sy.wait_ge(s_v, 8 * (j - 1))
                            s_st.wait(sy, (j - 2, 0))
                            s_st.wait(sy, (j - 2, 1))
                        s_ld.inc(
                            sy.dma_start(
                                out=xb2[:, (j % 2) * D : (j % 2 + 1) * D],
                                in_=x_in[j * 128 : (j + 1) * 128, :],
                            ),
                            j,
                        )
                        if j >= 1:
                            sy.wait_ge(s_v, 8 * (j - 1) + 4)
                            s_st.inc(
                                sy.dma_start(
                                    out=x2_d[(j - 1) * 128 : j * 128, :],
                                    in_=x2b[:, ((j - 1) % 2) * D : ((j - 1) % 2 + 1) * D],
                                ),
                                (j - 1, 0),
                            )
                            sy.wait_ge(s_v, 8 * j)
                            s_st.inc(
                                sy.dma_start(
                                    out=xln3_tok[(j - 1) * 128 : j * 128, :],
                                    in_=x3b[:, ((j - 1) % 2) * D : ((j - 1) % 2 + 1) * D],
                                ),
                                (j - 1, 1),
                            )
                    sy.wait_ge(s_v, 8 * (TT - 1) + 4)
                    s_st.inc(
                        sy.dma_start(
                            out=x2_d[(TT - 1) * 128 : TT * 128, :],
                            in_=x2b[:, ((TT - 1) % 2) * D : ((TT - 1) % 2 + 1) * D],
                        ),
                        (TT - 1, 0),
                    )
                    sy.wait_ge(s_v, 8 * TT)
                    s_st.inc(
                        sy.dma_start(
                            out=xln3_tok[(TT - 1) * 128 : TT * 128, :],
                            in_=x3b[:, ((TT - 1) % 2) * D : ((TT - 1) % 2 + 1) * D],
                        ),
                        (TT - 1, 1),
                    )
                    s_st.wait_all(sy)

                @blk.scalar
                def _(s):
                    s_tp.inc(
                        s.dma_start_transpose(
                            aot[:, 0:D], ao_bf[:, 0:128]
                        ),
                        0,
                    )
                    s_tp.inc(
                        s.dma_start_transpose(
                            aot[:, D : 2 * D], ao_bf[:, 128:256]
                        ),
                        1,
                    )
                    for j in range(TT):
                        c = j * 8
                        a_ = aot[:, (j % 2) * D : (j % 2 + 1) * D]
                        x2_ = x2b[:, (j % 2) * D : (j % 2 + 1) * D]
                        s_tp.wait(s, j)
                        s.activation(out=dump[:, :], in_=a_, func=AF.Square,
                                     accum_out=stt[:, c + 1 : c + 2]).then_inc(s_sc, 1)
                        s.wait_ge(s_sc, 4 * j + 1)
                        s.wait_ge(s_v, 8 * j + 2)
                        s.activation(out=stt[:, c + 4 : c + 5],
                                     in_=stt[:, c + 1 : c + 2], func=AF.Sqrt,
                                     scale=1.0 / D,
                                     bias=stt[:, c + 3 : c + 4]).then_inc(s_sc, 1)
                        s.wait_ge(s_v, 8 * j + 4)
                        s.activation(out=dump[:, :], in_=x2_, func=AF.Square,
                                     accum_out=stt[:, c + 6 : c + 7]).then_inc(s_sc, 1)
                        s.wait_ge(s_sc, 4 * j + 3)
                        s.wait_ge(s_v, 8 * j + 6)
                        s.activation(out=stt[:, c + 4 : c + 5],
                                     in_=stt[:, c + 6 : c + 7], func=AF.Sqrt,
                                     scale=1.0 / D,
                                     bias=stt[:, c + 3 : c + 4]).then_inc(s_sc, 1)
                        if j + 2 < TT:
                            s_tp.inc(
                                s.dma_start_transpose(
                                    aot[:, (j % 2) * D : (j % 2 + 1) * D],
                                    ao_bf[:, (j + 2) * 128 : (j + 3) * 128],
                                ),
                                j + 2,
                            )

                @blk.vector
                def _(v):
                    fx = 0
                    for j in range(TT):
                        c = j * 8
                        a_ = aot[:, (j % 2) * D : (j % 2 + 1) * D]
                        x_ = xb2[:, (j % 2) * D : (j % 2 + 1) * D]
                        x2_ = x2b[:, (j % 2) * D : (j % 2 + 1) * D]
                        m = stt[:, c + 2 : c + 3]
                        s_tp.wait(v, j)
                        v.tensor_reduce(stt[:, c : c + 1], a_, AX.X,
                                        ALU.add).then_inc(s_fx, 1)
                        fx += 1
                        v.wait_ge(s_fx, fx)
                        v.tensor_scalar(m, stt[:, c : c + 1], 1.0 / D, None,
                                        ALU.mult).then_inc(s_v, 1)
                        v.wait_ge(s_v, 8 * j + 1)
                        v.tensor_mul(stt[:, c + 3 : c + 4], m, m).then_inc(s_fx, 1)
                        fx += 1
                        v.wait_ge(s_fx, fx)
                        v.tensor_scalar(stt[:, c + 3 : c + 4],
                                        stt[:, c + 3 : c + 4], -1.0, EPS,
                                        ALU.mult, ALU.add).then_inc(s_v, 1)
                        v.wait_ge(s_sc, 4 * j + 2)
                        v.reciprocal(stt[:, c + 5 : c + 6],
                                     stt[:, c + 4 : c + 5]).then_inc(s_fx, 1)
                        fx += 1
                        v.wait_ge(s_fx, fx)
                        s_ld.wait(v, j)
                        v.tensor_scalar(x2_, a_, m, stt[:, c + 5 : c + 6],
                                        ALU.subtract, ALU.mult).then_inc(s_v, 1)
                        v.wait_ge(s_v, 8 * j + 3)
                        v.tensor_add(x2_, x2_, x_).then_inc(s_v, 1)
                        v.wait_ge(s_v, 8 * j + 4)
                        v.tensor_reduce(stt[:, c : c + 1], x2_, AX.X,
                                        ALU.add).then_inc(s_fx, 1)
                        fx += 1
                        v.wait_ge(s_fx, fx)
                        v.tensor_scalar(m, stt[:, c : c + 1], 1.0 / D, None,
                                        ALU.mult).then_inc(s_v, 1)
                        v.wait_ge(s_v, 8 * j + 5)
                        v.tensor_mul(stt[:, c + 3 : c + 4], m, m).then_inc(s_fx, 1)
                        fx += 1
                        v.wait_ge(s_fx, fx)
                        v.tensor_scalar(stt[:, c + 3 : c + 4],
                                        stt[:, c + 3 : c + 4], -1.0, EPS,
                                        ALU.mult, ALU.add).then_inc(s_v, 1)
                        v.wait_ge(s_sc, 4 * (j + 1))
                        v.reciprocal(stt[:, c + 5 : c + 6],
                                     stt[:, c + 4 : c + 5]).then_inc(s_fx, 1)
                        fx += 1
                        v.wait_ge(s_fx, fx)
                        v.tensor_scalar(x3b[:, (j % 2) * D : (j % 2 + 1) * D],
                                        x2_, m, stt[:, c + 5 : c + 6],
                                        ALU.subtract, ALU.mult).then_inc(s_v, 2)

        with contextlib.ExitStack() as sc:
            e5 = sc.enter_context
            x3T = e5(nc.sbuf_tensor("x3T", [128, KT * TC], BF16))
            wgf = e5(nc.sbuf_tensor("wgf", [128, KT * E], F32))
            wgs = e5(nc.sbuf_tensor("wgs", [128, KT * E], BF16))
            iden = e5(nc.sbuf_tensor("iden", [128, 128], F32))
            icol = e5(nc.sbuf_tensor("icol", [128, 128], F32))
            ipart = e5(nc.sbuf_tensor("ipart", [128, 1], F32))
            i8 = e5(nc.sbuf_tensor("i8", [128, E], F32))
            ohm = e5(nc.sbuf_tensor("ohm", [128, E * 128], BF16))
            pbc = e5(nc.sbuf_tensor("pbc", [128, 128], F32))
            lgt = e5(nc.sbuf_tensor("lgt", [128, E], F32))
            top8 = e5(nc.sbuf_tensor("top8", [128, 8], F32))
            idx8 = e5(nc.sbuf_tensor("idx8", [128, 8], U32))
            idxf = e5(nc.sbuf_tensor("idxf", [128, 8], F32))
            oh1 = e5(nc.sbuf_tensor("oh1", [128, E], F32))
            oh2 = e5(nc.sbuf_tensor("oh2", [128, E], F32))
            comb = e5(nc.sbuf_tensor("comb", [128, E], F32))
            comb_cm = e5(nc.sbuf_tensor("comb_cm", [128, TC], BF16))
            cbb = e5(nc.sbuf_tensor("cbb", [128, TC], BF16))
            sgb = e5(nc.sbuf_tensor("sgb", [128, TC], BF16))
            ucb = e5(nc.sbuf_tensor("ucb", [128, BLK], F32))
            actb = e5(nc.sbuf_tensor("actb", [128, 8 * TC], BF16))
            wgu_s = e5(nc.sbuf_tensor("wgu_s", [128, 4 * KT * 128], BF16))
            wdn_s = e5(nc.sbuf_tensor("wdn_s", [128, 4 * 8 * 128], BF16))
            racc = e5(nc.sbuf_tensor("racc", [128, KT * TC], BF16))
            rp_stack = contextlib.ExitStack()
            ps_r = rp_stack.enter_context(nc.psum_tensor("ps_r", [128, E], F32))
            ps_t = rp_stack.enter_context(nc.psum_tensor("ps_t", [128, 128], F32))
            s_tp = en(nc.semaphore("s5_tp"))
            s_wg = en(nc.semaphore("s5_wg"))
            s_w4 = PS(en, nc, "s5_w", ways=4)
            s_mm = en(nc.semaphore("s5_mm"))
            s_sil = en(nc.semaphore("s5_sil"))
            s_act = en(nc.semaphore("s5_act"))
            s_eo = en(nc.semaphore("s5_eo"))
            s_rt = en(nc.semaphore("s5_rt"))
            s_sg = en(nc.semaphore("s5_sg"))
            s_cb = en(nc.semaphore("s5_cb"))
            s_fx8 = en(nc.semaphore("s5_fx8"))
            s_ux = en(nc.semaphore("s5_ux"))
            uxv = [0]

            mmv = [0]
            silv = [0]
            actv = [0]
            eov = [0]
            cbv = [0]
            stage_id = [0]

            with nc.Block() as blk:

                @blk.scalar
                def _(s):
                    for ct in range(KT):
                        s.dma_start_transpose(
                            x3T[:, ct * TC : (ct + 1) * TC],
                            xln3_tok[:, ct * 128 : (ct + 1) * 128],
                        ).then_inc(s_tp, 16)

                @blk.sync
                def _(sy):
                    for k in range(KT):
                        sy.dma_start(out=wgf[:, k * E : (k + 1) * E],
                                     in_=wg_in[k * 128 : (k + 1) * 128, :]).then_inc(
                            s_wg, 16)

                @blk.gpsimd
                def _(g):
                    g.iota(i8[:, :], [[1, E]], base=0, channel_multiplier=0,
                           allow_small_or_imprecise_dtypes=True)
                    g.iota(icol[:, :], [[1, 128]], base=0, channel_multiplier=0,
                           allow_small_or_imprecise_dtypes=True)
                    g.iota(ipart[:, :], [[0, 1]], base=0, channel_multiplier=1,
                           allow_small_or_imprecise_dtypes=True).then_inc(s_rt, 1)

                @blk.vector
                def _(v):
                    v.wait_ge(s_rt, 1)
                    v.tensor_single_scalar(iden[:, :], icol[:, :], ipart[:, 0:1],
                                           ALU.is_equal)
                    v.tensor_scalar_mul(pbc[:, :], icol[:, :], 0.0).then_inc(s_fx8, 1)
                    v.wait_ge(s_fx8, 1)
                    v.tensor_scalar_add(pbc[:, :], pbc[:, :],
                                        ipart[:, 0:1]).then_inc(s_fx8, 1)
                    v.wait_ge(s_fx8, 2)
                    for e in range(E):
                        v.tensor_single_scalar(ohm[:, e * 128 : (e + 1) * 128],
                                               pbc[:, :], float(e), ALU.is_equal)
                    v.memset(racc[:, :], 0.0)
                    v.wait_ge(s_wg, 16 * KT)
                    v.tensor_copy(wgs[:, :], wgf[:, :]).then_inc(s_rt, 1)

            MASKS = 2

            with nc.Block() as blk:
                mm0 = mmv[0]

                @blk.tensor
                def _(t):
                    t.wait_ge(s_tp, 16 * KT)
                    t.wait_ge(s_rt, MASKS)
                    for j in range(TT):
                        if j >= 1:
                            t.wait_ge(s_rt, MASKS + 10 * (j - 1) + 1)
                        for k in range(KT):
                            inst = t.matmul(
                                ps_r[:, :],
                                x3T[:, k * TC + j * 128 : k * TC + (j + 1) * 128],
                                wgs[:, k * E : (k + 1) * E],
                                start=(k == 0), stop=(k == KT - 1),
                            )
                        inst.then_inc(s_mm, 1)
                        t.wait_ge(s_rt, MASKS + 10 * j + 9)
                        t.transpose(ps_t[0:8, 0:128], comb[:, :],
                                    iden[:, :]).then_inc(s_mm, 1)

                @blk.vector
                def _(v):
                    for j in range(TT):
                        base = MASKS + 10 * j
                        v.wait_ge(s_mm, mm0 + 2 * j + 1)
                        v.tensor_copy(lgt[:, :], ps_r[:, :]).then_inc(s_rt, 1)
                        if j >= 1:
                            v.wait_ge(s_sg, j)
                        v.wait_ge(s_rt, base + 1)
                        v.max(top8[:, :], lgt[:, :]).then_inc(s_rt, 1)
                        v.wait_ge(s_rt, base + 2)
                        v.max_index(idx8[:, :], top8[:, :], lgt[:, :]).then_inc(s_rt, 1)
                        v.wait_ge(s_rt, base + 3)
                        v.tensor_copy(idxf[:, :], idx8[:, :]).then_inc(s_rt, 1)
                        v.tensor_sub(stt[:, 0:1], top8[:, 0:1],
                                     top8[:, 1:2]).then_inc(s_rt, 1)
                        v.wait_ge(s_sg, j + 1)
                        v.wait_ge(s_rt, base + 4)
                        v.tensor_single_scalar(oh1[:, :], i8[:, :], idxf[:, 0:1],
                                               ALU.is_equal).then_inc(s_rt, 1)
                        v.tensor_single_scalar(oh2[:, :], i8[:, :], idxf[:, 1:2],
                                               ALU.is_equal).then_inc(s_rt, 1)
                        v.wait_ge(s_rt, base + 7)
                        v.tensor_scalar_mul(comb[:, :], oh1[:, :],
                                            stt[:, 2:3]).then_inc(s_rt, 1)
                        v.wait_ge(s_rt, base + 8)
                        v.scalar_tensor_tensor(comb[:, :], oh2[:, :], stt[:, 3:4],
                                               comb[:, :], ALU.mult,
                                               ALU.add).then_inc(s_rt, 1)
                        v.wait_ge(s_mm, mm0 + 2 * j + 2)
                        v.tensor_copy(comb_cm[0:8, j * 128 : (j + 1) * 128],
                                      ps_t[0:8, 0:128]).then_inc(s_rt, 1)

                @blk.scalar
                def _(s):
                    for j in range(TT):
                        base = MASKS + 10 * j
                        s.wait_ge(s_rt, base + 5)
                        s.activation(out=stt[:, 2:3], in_=stt[:, 0:1],
                                     func=AF.Sigmoid, scale=1.0)
                        s.activation(out=stt[:, 3:4], in_=stt[:, 0:1],
                                     func=AF.Sigmoid, scale=-1.0).then_inc(s_sg, 1)

                mmv[0] = mm0 + 2 * TT

            RT_DONE = MASKS + 10 * TT
            rp_stack.close()
            ps_q = [e5(nc.psum_tensor(f"ps_q{i}", [128, BLK], F32))
                    for i in range(2 * NB)]

            def gu_stage(is_shared, e_or_ic):
                with nc.Block() as blk:
                    mm0 = mmv[0]
                    sil0 = silv[0]
                    act0 = actv[0]
                    cb_need = cbv[0]
                    sid = stage_id[0]
                    stage_id[0] += 1
                    for _li in range(16):
                        s_w4.reserve((sid, _li))

                    @blk.sync
                    def _(sy):
                        if is_shared:
                            sy.wait_ge(s_c_sh, CAST_SH)
                        else:
                            sy.wait_ge(s_c_gu, CAST_GU)
                        if mm0:
                            sy.wait_ge(s_mm, mm0)
                        for fi in range(8):
                            for half in range(2):
                                li = fi * 2 + half
                                if li >= 4:
                                    sy.wait_ge(s_mm, mm0 + (li - 3))
                                if is_shared:
                                    ft = e_or_ic * 8 + fi if half == 0 else 32 + e_or_ic * 8 + fi
                                    src = wsg_bf.rearrange("(k p) f -> p k f", p=128)[
                                        :, :, ft * 128 : (ft + 1) * 128]
                                else:
                                    ft = fi if half == 0 else 8 + fi
                                    src = wgu_bf[e_or_ic].rearrange(
                                        "(k p) f -> p k f", p=128)[
                                        :, :, ft * 128 : (ft + 1) * 128]
                                s_w4.inc(
                                    sy.dma_start(
                                        out=wgu_s[:, (li % 4) * KT * 128 : (li % 4 + 1) * KT * 128],
                                        in_=src,
                                    ),
                                    (sid, li),
                                )

                    @blk.tensor
                    def _(t):
                        if not is_shared:
                            t.wait_ge(s_cb, cb_need)
                        for li in range(16):
                            s_w4.wait(t, (sid, li))
                            q = (li % 2) * NB
                            if li >= 2:
                                pl = li - 2
                                if pl % 2 == 0:
                                    t.wait_ge(s_sil, sil0 + (pl // 2) * NB + NB)
                                else:
                                    t.wait_ge(s_act, act0 + (pl // 2) * NB + NB)
                            for k in range(KT):
                                for tb in range(NB):
                                    inst = t.matmul(
                                        ps_q[q + tb][:, :],
                                        wgu_s[:, (li % 4) * KT * 128 + k * 128 : (li % 4) * KT * 128 + (k + 1) * 128],
                                        x3T[:, k * TC + tb * BLK : k * TC + (tb + 1) * BLK],
                                        start=(k == 0), stop=(k == KT - 1),
                                    )
                            inst.then_inc(s_mm, 1)

                    @blk.scalar
                    def _(s):
                        for fi in range(8):
                            li = fi * 2
                            q = (li % 2) * NB
                            s.wait_ge(s_mm, mm0 + li + 1)
                            for tb in range(NB):
                                if fi >= 1:
                                    s.wait_ge(s_act, act0 + (fi - 1) * NB + tb + 1)
                                s.activation(
                                    out=sgb[:, tb * BLK : (tb + 1) * BLK],
                                    in_=ps_q[q + tb][:, :], func=AF.Silu,
                                ).then_inc(s_sil, 1)

                    @blk.vector
                    def _(v):
                        for fi in range(8):
                            li = fi * 2 + 1
                            q = (li % 2) * NB
                            v.wait_ge(s_mm, mm0 + li + 1)
                            for tb in range(NB):
                                if is_shared:
                                    inst = v.tensor_copy(ucb[:, :], ps_q[q + tb][:, :])
                                else:
                                    inst = v.tensor_tensor(
                                        ucb[:, :], ps_q[q + tb][:, :],
                                        cbb[:, tb * BLK : (tb + 1) * BLK], ALU.mult)
                                inst.then_inc(s_ux, 1)
                                uxv[0] += 1
                                v.wait_ge(s_ux, uxv[0])
                                v.wait_ge(s_sil, sil0 + fi * NB + tb + 1)
                                v.tensor_tensor(
                                    actb[:, fi * TC + tb * BLK : fi * TC + (tb + 1) * BLK],
                                    sgb[:, tb * BLK : (tb + 1) * BLK], ucb[:, :],
                                    ALU.mult,
                                ).then_inc(s_act, 1)

                    mmv[0] = mm0 + 16
                    silv[0] = sil0 + 8 * NB
                    actv[0] = act0 + 8 * NB

            def down_stage(is_shared, e_or_ic):
                with nc.Block() as blk:
                    mm0 = mmv[0]
                    eo0 = eov[0]
                    sid = stage_id[0]
                    stage_id[0] += 1
                    for _ht in range(KT):
                        s_w4.reserve((sid, _ht))

                    @blk.sync
                    def _(sy):
                        if not is_shared:
                            sy.wait_ge(s_c_dn, CAST_DN)
                        sy.wait_ge(s_mm, mm0)
                        for ht in range(KT):
                            if ht >= 4:
                                sy.wait_ge(s_mm, mm0 + (ht - 3))
                            if is_shared:
                                src = wsd_bf[e_or_ic * IM : (e_or_ic + 1) * IM, :].rearrange(
                                    "(k p) h -> p k h", p=128)[:, :, ht * 128 : (ht + 1) * 128]
                            else:
                                src = wdn_bf[e_or_ic].rearrange("(k p) h -> p k h", p=128)[
                                    :, :, ht * 128 : (ht + 1) * 128]
                            s_w4.inc(
                                sy.dma_start(
                                    out=wdn_s[:, (ht % 4) * 8 * 128 : (ht % 4 + 1) * 8 * 128],
                                    in_=src,
                                ),
                                (sid, ht),
                            )

                    @blk.tensor
                    def _(t):
                        for ht in range(KT):
                            s_w4.wait(t, (sid, ht))
                            q = (ht % 2) * NB
                            if ht >= 2:
                                t.wait_ge(s_eo, eo0 + (ht - 1) * NB)
                            for ik in range(8):
                                for tb in range(NB):
                                    inst = t.matmul(
                                        ps_q[q + tb][:, :],
                                        wdn_s[:, (ht % 4) * 8 * 128 + ik * 128 : (ht % 4) * 8 * 128 + (ik + 1) * 128],
                                        actb[:, ik * TC + tb * BLK : ik * TC + (tb + 1) * BLK],
                                        start=(ik == 0), stop=(ik == 7),
                                    )
                            inst.then_inc(s_mm, 1)

                    @blk.vector
                    def _(v):
                        for ht in range(KT):
                            q = (ht % 2) * NB
                            v.wait_ge(s_mm, mm0 + ht + 1)
                            for tb in range(NB):
                                sl = racc[:, ht * TC + tb * BLK : ht * TC + (tb + 1) * BLK]
                                v.tensor_tensor(sl, sl, ps_q[q + tb][:, :],
                                                ALU.add).then_inc(s_eo, 1)

                    mmv[0] = mm0 + KT
                    eov[0] = eo0 + KT * NB

            for e in range(E):
                with nc.Block() as blk:
                    mm0 = mmv[0]
                    cb0 = cbv[0]

                    @blk.tensor
                    def _(t):
                        t.wait_ge(s_rt, RT_DONE)
                        for tb in range(NB):
                            if tb >= 1:
                                t.wait_ge(s_cb, cb0 + tb)
                            t.matmul(ps_q[0][:, :], ohm[0:8, e * 128 : (e + 1) * 128],
                                     comb_cm[0:8, tb * BLK : (tb + 1) * BLK],
                                     start=True, stop=True).then_inc(s_mm, 1)

                    @blk.vector
                    def _(v):
                        for tb in range(NB):
                            v.wait_ge(s_mm, mm0 + tb + 1)
                            v.tensor_copy(cbb[:, tb * BLK : (tb + 1) * BLK],
                                          ps_q[0][:, :]).then_inc(s_cb, 1)

                    mmv[0] = mm0 + NB
                    cbv[0] = cb0 + NB

                gu_stage(False, e)
                down_stage(False, e)

            for ic in range(4):
                gu_stage(True, ic)
                down_stage(True, ic)

            with nc.Block() as blk:

                @blk.sync
                def _(sy):
                    for ct in range(KT):
                        sy.dma_start(
                            out=ffn_bf[ct * 128 : (ct + 1) * 128, :],
                            in_=racc[:, ct * TC : (ct + 1) * TC],
                        ).then_inc(s_wg, 16)
                    sy.wait_ge(s_wg, 16 * KT + 16 * KT)

        with contextlib.ExitStack() as sc:
            e7 = sc.enter_context
            fnt = e7(nc.sbuf_tensor("fnt", [128, 2 * D], BF16))
            dump = e7(nc.sbuf_tensor("dump7", [128, D], F32))
            x2l = e7(nc.sbuf_tensor("x2l", [128, 2 * D], F32))
            yb = e7(nc.sbuf_tensor("yb", [128, 2 * D], F32))
            s_tp = PS(en, nc, "s7_tp")
            s_ld = PS(en, nc, "s7_ld")
            s_st = PS(en, nc, "s7_st")
            for _j in range(TT):
                s_tp.reserve(_j)
            for _j in range(TT):
                s_ld.reserve(_j)
            for _j in range(TT):
                s_st.reserve(_j)
            s_v = en(nc.semaphore("s7_v"))
            s_sc = en(nc.semaphore("s7_sc"))
            s_fx = en(nc.semaphore("s7_fx"))

            with nc.Block() as blk:

                @blk.sync
                def _(sy):
                    for j in range(TT):
                        if j >= 2:
                            sy.wait_ge(s_v, 4 * (j - 1))
                        s_ld.inc(
                            sy.dma_start(
                                out=x2l[:, (j % 2) * D : (j % 2 + 1) * D],
                                in_=x2_d[j * 128 : (j + 1) * 128, :],
                            ),
                            j,
                        )
                        if j >= 1:
                            sy.wait_ge(s_v, 4 * j)
                            s_st.inc(
                                sy.dma_start(
                                    out=y_out[(j - 1) * 128 : j * 128, :],
                                    in_=yb[:, ((j - 1) % 2) * D : ((j - 1) % 2 + 1) * D],
                                ),
                                j - 1,
                            )
                    sy.wait_ge(s_v, 4 * TT)
                    s_st.inc(
                        sy.dma_start(
                            out=y_out[(TT - 1) * 128 : TT * 128, :],
                            in_=yb[:, ((TT - 1) % 2) * D : ((TT - 1) % 2 + 1) * D],
                        ),
                        TT - 1,
                    )
                    s_st.wait_all(sy)

                @blk.scalar
                def _(s):
                    s_tp.inc(
                        s.dma_start_transpose(fnt[:, 0:D], ffn_bf[:, 0:128]), 0
                    )
                    s_tp.inc(
                        s.dma_start_transpose(fnt[:, D : 2 * D], ffn_bf[:, 128:256]), 1
                    )
                    for j in range(TT):
                        c = j * 8
                        f_ = fnt[:, (j % 2) * D : (j % 2 + 1) * D]
                        s_tp.wait(s, j)
                        s.activation(out=dump[:, :], in_=f_, func=AF.Square,
                                     accum_out=stt[:, c + 1 : c + 2]).then_inc(s_sc, 1)
                        s.wait_ge(s_sc, 2 * j + 1)
                        s.wait_ge(s_v, 4 * j + 2)
                        s.activation(out=stt[:, c + 4 : c + 5],
                                     in_=stt[:, c + 1 : c + 2], func=AF.Sqrt,
                                     scale=1.0 / D,
                                     bias=stt[:, c + 3 : c + 4]).then_inc(s_sc, 1)
                        if j + 2 < TT:
                            s.wait_ge(s_v, 4 * (j + 1))
                            s_tp.inc(
                                s.dma_start_transpose(
                                    fnt[:, (j % 2) * D : (j % 2 + 1) * D],
                                    ffn_bf[:, (j + 2) * 128 : (j + 3) * 128],
                                ),
                                j + 2,
                            )

                @blk.vector
                def _(v):
                    fx = 0
                    for j in range(TT):
                        c = j * 8
                        f_ = fnt[:, (j % 2) * D : (j % 2 + 1) * D]
                        x2_ = x2l[:, (j % 2) * D : (j % 2 + 1) * D]
                        y_ = yb[:, (j % 2) * D : (j % 2 + 1) * D]
                        m = stt[:, c + 2 : c + 3]
                        s_tp.wait(v, j)
                        v.tensor_reduce(stt[:, c : c + 1], f_, AX.X,
                                        ALU.add).then_inc(s_fx, 1)
                        fx += 1
                        v.wait_ge(s_fx, fx)
                        v.tensor_scalar(m, stt[:, c : c + 1], 1.0 / D, None,
                                        ALU.mult).then_inc(s_v, 1)
                        v.wait_ge(s_v, 4 * j + 1)
                        v.tensor_mul(stt[:, c + 3 : c + 4], m, m).then_inc(s_fx, 1)
                        fx += 1
                        v.wait_ge(s_fx, fx)
                        v.tensor_scalar(stt[:, c + 3 : c + 4],
                                        stt[:, c + 3 : c + 4], -1.0, EPS,
                                        ALU.mult, ALU.add).then_inc(s_v, 1)
                        v.wait_ge(s_sc, 2 * (j + 1))
                        v.reciprocal(stt[:, c + 5 : c + 6],
                                     stt[:, c + 4 : c + 5]).then_inc(s_fx, 1)
                        fx += 1
                        v.wait_ge(s_fx, fx)
                        s_ld.wait(v, j)
                        if j >= 2:
                            s_st.wait(v, j - 2)
                        v.tensor_scalar(y_, f_, m, stt[:, c + 5 : c + 6],
                                        ALU.subtract, ALU.mult).then_inc(s_v, 1)
                        v.wait_ge(s_v, 4 * j + 3)
                        v.tensor_add(y_, y_, x2_).then_inc(s_v, 1)


    if debug:
        with nc.semaphore("s_dbg") as s_dbg, nc.Block() as blk:

            @blk.gpsimd
            def _(g):
                n = 0
                for name, (o, s) in dbg.items():
                    g.dma_start(out=o[:, :], in_=s[:, :]).then_inc(s_dbg, 16)
                    n += 16
                g.wait_ge(s_dbg, n)

    return nc


_NC_CACHE = None


def kernel(**inputs):
    global _NC_CACHE
    x = np.ascontiguousarray(np.asarray(inputs["L0_attn_x"], np.float32))
    wnames = {
        "A": "L0_attn_A", "B": "L0_attn_B", "C_w": "L0_attn_C_w",
        "C_b": "L0_attn_C_b", "D_w": "L0_attn_D_w", "D_b": "L0_attn_D_b",
        "log_delta": "L0_attn_log_delta", "w_gate": "L0_ffn_w_gate",
        "w_gate_up": "L0_ffn_w_gate_up", "w_down": "L0_ffn_w_down",
        "w_sh_gate_up": "L0_ffn_w_sh_gate_up", "w_sh_down": "L0_ffn_w_sh_down",
    }
    weights = {k: np.ascontiguousarray(np.asarray(inputs[v], np.float32))
               for k, v in wnames.items()}

    if _NC_CACHE is None:
        _NC_CACHE = build_nc()
    nc = _NC_CACHE

    in_maps = []
    for c in range(8):
        b, q = c // 4, c % 4
        shard = np.ascontiguousarray(x[b, q * TC : (q + 1) * TC, :])
        sel = np.zeros((3, 8), np.float32)
        for s_ in range(3):
            r = q - 1 - s_
            if r >= 0:
                sel[s_, 4 * b + r] = 1.0
        m = {"x": shard, "sel": sel}
        m.update(weights)
        in_maps.append(m)

    res = run_bass_kernel_spmd(nc, in_maps, core_ids=list(range(8)))
    out = np.empty((2, 8192, D), np.float32)
    for c in range(8):
        b, q = c // 4, c % 4
        out[b, q * TC : (q + 1) * TC, :] = res.results[c]["y"]
    return out


if __name__ == "__main__":
    nc = build_nc()
    print("built ok")


# version 08
# speedup vs baseline: 1.0279x; 1.0143x over previous
import contextlib
import numpy as np
import concourse.bass as bass
import concourse.mybir as mybir
from concourse.bass_utils import run_bass_kernel_spmd

F32 = mybir.dt.float32
BF16 = mybir.dt.bfloat16
U32 = mybir.dt.uint32
AF = mybir.ActivationFunctionType
ALU = mybir.AluOpType
AX = mybir.AxisListType

D = 2048
TC = 2048
KT = 16
TT = 16
NB = 4
BLK = 512
E = 8
IM = 1024
ISH = 4096
EPS = 1e-5


class PS:

    def __init__(self, en, nc, name, ways=2):
        self.s = [en(nc.semaphore(f"{name}_{i}")) for i in range(ways)]
        self.c = [0] * ways
        self.ways = ways
        self.n = 0
        self.mark = {}

    def reserve(self, key):
        assert key not in self.mark
        w = self.n % self.ways
        self.n += 1
        self.c[w] += 16
        self.mark[key] = (w, self.c[w])

    def inc(self, inst, key):
        w, _ = self.mark[key]
        inst.then_inc(self.s[w], 16)

    def wait(self, eng, key):
        w, v = self.mark[key]
        eng.wait_ge(self.s[w], v)

    def wait_all(self, eng):
        for w in range(self.ways):
            if self.c[w]:
                eng.wait_ge(self.s[w], self.c[w])


def build_nc(tc=2048, blk=512, debug=False):
    global TC, TT, NB, BLK
    TC, BLK = tc, blk
    TT, NB = TC // 128, TC // BLK

    nc = bass.Bass("TRN2", target_bir_lowering=False)

    x_in = nc.dram_tensor("x", [TC, D], F32, kind="ExternalInput").ap()
    sel_in = nc.dram_tensor("sel", [3, 8], F32, kind="ExternalInput").ap()
    A_in = nc.dram_tensor("A", [D, 64], F32, kind="ExternalInput").ap()
    B_in = nc.dram_tensor("B", [D, 64], F32, kind="ExternalInput").ap()
    Cw_in = nc.dram_tensor("C_w", [D, D], F32, kind="ExternalInput").ap()
    Cb_in = nc.dram_tensor("C_b", [D], F32, kind="ExternalInput").ap()
    Dw_in = nc.dram_tensor("D_w", [D, D], F32, kind="ExternalInput").ap()
    Db_in = nc.dram_tensor("D_b", [D], F32, kind="ExternalInput").ap()
    ld_in = nc.dram_tensor("log_delta", [D], F32, kind="ExternalInput").ap()
    wg_in = nc.dram_tensor("w_gate", [D, E], F32, kind="ExternalInput").ap()
    wgu_in = nc.dram_tensor("w_gate_up", [E, D, 2 * IM], F32, kind="ExternalInput").ap()
    wdn_in = nc.dram_tensor("w_down", [E, IM, D], F32, kind="ExternalInput").ap()
    wsg_in = nc.dram_tensor("w_sh_gate_up", [D, 2 * ISH], F32, kind="ExternalInput").ap()
    wsd_in = nc.dram_tensor("w_sh_down", [ISH, D], F32, kind="ExternalInput").ap()
    y_out = nc.dram_tensor("y", [TC, D], F32, kind="ExternalOutput").ap()

    cw_bf = nc.dram_tensor("cw_bf", [D, D], BF16).ap()
    dw_bf = nc.dram_tensor("dw_bf", [D, D], BF16).ap()
    wgu_bf = nc.dram_tensor("wgu_bf", [E, D, 2 * IM], BF16).ap()
    wdn_bf = nc.dram_tensor("wdn_bf", [E, IM, D], BF16).ap()
    wsg_bf = nc.dram_tensor("wsg_bf", [D, 2 * ISH], BF16).ap()
    wsd_bf = nc.dram_tensor("wsd_bf", [ISH, D], BF16).ap()
    xln_tok = nc.dram_tensor("xln_tok", [TC, D], BF16).ap()
    xproj_d = nc.dram_tensor("xproj_d", [D, TC], F32).ap()
    ao_bf = nc.dram_tensor("ao_bf", [D, TC], BF16).ap()
    x2_d = nc.dram_tensor("x2_d", [TC, D], F32).ap()
    xln3_tok = nc.dram_tensor("xln3_tok", [TC, D], BF16).ap()
    ffn_bf = nc.dram_tensor("ffn_bf", [D, TC], BF16).ap()
    li_b = nc.dram_tensor("lasts_in", [128, KT], F32).ap()
    lo_b = nc.dram_tensor("lasts_out", [8 * 128, KT], F32).ap()
    if debug:
        dbg = {
            "d_xln": (nc.dram_tensor("d_xln", [TC, D], F32, kind="ExternalOutput").ap(), xln_tok),
            "d_xp": (nc.dram_tensor("d_xp", [D, TC], F32, kind="ExternalOutput").ap(), xproj_d),
            "d_ao": (nc.dram_tensor("d_ao", [D, TC], F32, kind="ExternalOutput").ap(), ao_bf),
            "d_x2": (nc.dram_tensor("d_x2", [TC, D], F32, kind="ExternalOutput").ap(), x2_d),
            "d_x3": (nc.dram_tensor("d_x3", [TC, D], F32, kind="ExternalOutput").ap(), xln3_tok),
            "d_ffn": (nc.dram_tensor("d_ffn", [D, TC], F32, kind="ExternalOutput").ap(), ffn_bf),
        }

    stk = contextlib.ExitStack()
    with stk:
        en = stk.enter_context

        adiag = en(nc.sbuf_tensor("adiag", [128, KT], F32))
        a2048 = en(nc.sbuf_tensor("a2048", [128, KT], F32))
        asq = en(nc.sbuf_tensor("asq", [128, KT], F32))
        ab = en(nc.sbuf_tensor("ab", [128, KT], F32))
        cb_s = en(nc.sbuf_tensor("cb_s", [128, KT], F32))
        db_s = en(nc.sbuf_tensor("db_s", [128, KT], F32))
        ldl = en(nc.sbuf_tensor("ldl", [128, KT], F32))
        abt = en(nc.sbuf_tensor("abt", [128, 2 * 64], F32))
        bbt = en(nc.sbuf_tensor("bbt", [128, 2 * 64], F32))
        lasts = en(nc.sbuf_tensor("lasts", [128, KT], F32))
        lall = en(nc.sbuf_tensor("lall", [128, 8 * KT], F32))
        gcar = en(nc.sbuf_tensor("gcar", [128, KT], F32))
        pred = en(nc.sbuf_tensor("pred", [128, 3 * KT], F32))
        selr = en(nc.sbuf_tensor("selr", [128, 24], F32))
        selb = en(nc.sbuf_tensor("selb", [128, 24], F32))
        onesr = en(nc.sbuf_tensor("onesr", [128, 128], F32))
        stt = en(nc.sbuf_tensor("stt", [128, 16 * 8], F32))

        s_p0 = en(nc.semaphore("s_p0"))
        s_ab0 = en(nc.semaphore("s_ab0"))
        s_ab1 = en(nc.semaphore("s_ab1"))
        s_abp = [s_ab0, s_ab1]
        s_c_cd = en(nc.semaphore("s_c_cd"))
        s_c_sh = en(nc.semaphore("s_c_sh"))
        s_c_gu = en(nc.semaphore("s_c_gu"))
        s_c_dn = en(nc.semaphore("s_c_dn"))
        s_pre = en(nc.semaphore("s_pre"))
        s_sel = en(nc.semaphore("s_sel"))
        s_fx0 = en(nc.semaphore("s_fx0"))

        CAST_CD = 2 * 16
        CAST_SH = 32
        CAST_GU = E * 16
        CAST_DN = E * 16
        P0_PARAMS = 49 * 16

        def p0_bodies(blk):

            @blk.gpsimd
            def _(g):
                g.dma_start(out=cw_bf[:, :], in_=Cw_in[:, :]).then_inc(s_c_cd, 16)
                g.dma_start(out=dw_bf[:, :], in_=Dw_in[:, :]).then_inc(s_c_cd, 16)

            @blk.sync
            def _(sy):
                for j in range(KT):
                    sl = slice(j * 128, (j + 1) * 128)
                    o1 = lambda t: t.rearrange("(p o) -> p o", o=1)
                    sy.dma_start(out=ldl[:, j : j + 1], in_=o1(ld_in[sl])).then_inc(s_p0, 16)
                    sy.dma_start(out=cb_s[:, j : j + 1], in_=o1(Cb_in[sl])).then_inc(s_p0, 16)
                    sy.dma_start(out=db_s[:, j : j + 1], in_=o1(Db_in[sl])).then_inc(s_p0, 16)
                sy.dma_start(
                    out=selr[0:1, 0:24],
                    in_=sel_in.rearrange("a b -> (a b)").rearrange("(o f) -> o f", o=1),
                ).then_inc(s_p0, 16)
                for j in range(KT):
                    if j >= 2:
                        sy.wait_ge(s_pre, 3 + j - 1)
                    p = j % 2
                    sy.dma_start(out=abt[:, p * 64 : (p + 1) * 64], in_=A_in[j * 128 : (j + 1) * 128, :]).then_inc(s_abp[p], 16)
                    sy.dma_start(out=bbt[:, p * 64 : (p + 1) * 64], in_=B_in[j * 128 : (j + 1) * 128, :]).then_inc(s_abp[p], 16)

            @blk.scalar
            def _(s):
                s.wait_ge(s_p0, P0_PARAMS)
                s.activation(out=adiag[:, :], in_=ldl[:, :], func=AF.Sigmoid,
                             scale=-1.0).then_inc(s_pre, 1)

            @blk.vector
            def _(v):
                fx0 = [0]

                def edge(inst):
                    inst.then_inc(s_fx0, 1)
                    fx0[0] += 1
                    v.wait_ge(s_fx0, fx0[0])

                v.wait_ge(s_pre, 1)
                edge(v.tensor_mul(asq[:, :], adiag[:, :], adiag[:, :]))
                cur, oth = asq, a2048
                for _ in range(TC.bit_length() - 2):
                    edge(v.tensor_mul(oth[:, :], cur[:, :], cur[:, :]))
                    cur, oth = oth, cur
                if cur is not a2048:
                    edge(v.tensor_copy(a2048[:, :], cur[:, :]))
                v.memset(onesr[0:1, :], 1.0).then_inc(s_pre, 2)
                for j in range(KT):
                    p = j % 2
                    v.wait_ge(s_abp[p], 32 * (j // 2 + 1))
                    edge(v.tensor_mul(bbt[:, p * 64 : (p + 1) * 64],
                                      abt[:, p * 64 : (p + 1) * 64],
                                      bbt[:, p * 64 : (p + 1) * 64]))
                    v.tensor_reduce(ab[:, j : j + 1],
                                    bbt[:, p * 64 : (p + 1) * 64], AX.X,
                                    ALU.add).then_inc(s_pre, 1)

            @blk.tensor
            def _(t):
                t.wait_ge(s_pre, 3)
                t.wait_ge(s_p0, P0_PARAMS)
                t.matmul(pss[:, 0:24], onesr[0:1, :], selr[0:1, 0:24],
                         start=True, stop=True).then_inc(s_sel, 1)

        with contextlib.ExitStack() as sc:
            e1 = sc.enter_context
            xb = e1(nc.sbuf_tensor("xb", [128, 2 * D], F32))
            xlnb = e1(nc.sbuf_tensor("xlnb", [128, 2 * D], BF16))
            dump = e1(nc.sbuf_tensor("dump1", [128, D], F32))
            pss = e1(nc.psum_tensor("pss", [128, 32], F32))
            s_ld = PS(en, nc, "s1_ld")
            s_st = PS(en, nc, "s1_st")
            for _j in range(TT):
                s_ld.reserve(_j)
            for _j in range(TT):
                s_st.reserve(_j)
            s_v = en(nc.semaphore("s1_v"))
            s_sc = en(nc.semaphore("s1_sc"))
            s_fx = en(nc.semaphore("s1_fx"))

            with nc.Block() as blk:
                p0_bodies(blk)

                @blk.sync
                def _(sy):
                    for j in range(TT):
                        if j >= 2:
                            sy.wait_ge(s_v, 4 * (j - 1))
                            sy.wait_ge(s_sc, 2 * (j - 1))
                            s_st.wait(sy, j - 2)
                        s_ld.inc(
                            sy.dma_start(
                                out=xb[:, (j % 2) * D : (j % 2 + 1) * D],
                                in_=x_in[j * 128 : (j + 1) * 128, :],
                            ),
                            j,
                        )
                        if j >= 1:
                            sy.wait_ge(s_v, 4 * j)
                            s_st.inc(
                                sy.dma_start(
                                    out=xln_tok[(j - 1) * 128 : j * 128, :],
                                    in_=xlnb[:, ((j - 1) % 2) * D : ((j - 1) % 2 + 1) * D],
                                ),
                                j - 1,
                            )
                    sy.wait_ge(s_v, 4 * TT)
                    s_st.inc(
                        sy.dma_start(
                            out=xln_tok[(TT - 1) * 128 : TT * 128, :],
                            in_=xlnb[:, ((TT - 1) % 2) * D : ((TT - 1) % 2 + 1) * D],
                        ),
                        TT - 1,
                    )
                    s_st.wait_all(sy)

                @blk.scalar
                def _(s):
                    for j in range(TT):
                        xj = xb[:, (j % 2) * D : (j % 2 + 1) * D]
                        c = j * 8
                        s_ld.wait(s, j)
                        s.activation(out=dump[:, :], in_=xj, func=AF.Square,
                                     accum_out=stt[:, c + 1 : c + 2]).then_inc(s_sc, 1)
                        s.wait_ge(s_sc, 2 * j + 1)
                        s.wait_ge(s_v, 4 * j + 2)
                        s.activation(out=stt[:, c + 4 : c + 5],
                                     in_=stt[:, c + 1 : c + 2], func=AF.Sqrt,
                                     scale=1.0 / D,
                                     bias=stt[:, c + 3 : c + 4]).then_inc(s_sc, 1)

                @blk.vector
                def _(v):
                    fx = 0
                    for j in range(TT):
                        xj = xb[:, (j % 2) * D : (j % 2 + 1) * D]
                        c = j * 8
                        m = stt[:, c + 2 : c + 3]
                        s_ld.wait(v, j)
                        v.tensor_reduce(stt[:, c : c + 1], xj, AX.X,
                                        ALU.add).then_inc(s_fx, 1)
                        fx += 1
                        v.wait_ge(s_fx, fx)
                        v.tensor_scalar(m, stt[:, c : c + 1], 1.0 / D, None,
                                        ALU.mult).then_inc(s_v, 1)
                        v.wait_ge(s_v, 4 * j + 1)
                        v.tensor_mul(stt[:, c + 3 : c + 4], m, m).then_inc(s_fx, 1)
                        fx += 1
                        v.wait_ge(s_fx, fx)
                        v.tensor_scalar(stt[:, c + 3 : c + 4],
                                        stt[:, c + 3 : c + 4], -1.0, EPS,
                                        ALU.mult, ALU.add).then_inc(s_v, 1)
                        v.wait_ge(s_sc, 2 * (j + 1))
                        v.reciprocal(stt[:, c + 5 : c + 6],
                                     stt[:, c + 4 : c + 5]).then_inc(s_fx, 1)
                        fx += 1
                        v.wait_ge(s_fx, fx)
                        v.tensor_scalar(xlnb[:, (j % 2) * D : (j % 2 + 1) * D],
                                        xj, m, stt[:, c + 5 : c + 6],
                                        ALU.subtract, ALU.mult).then_inc(s_v, 2)

        
                @blk.vector
                def _(v):
                    v.wait_ge(s_sel, 1)
                    v.tensor_copy(selb[:, :], pss[:, 0:24]).then_inc(s_sel, 1)

        with contextlib.ExitStack() as sc:
            e2 = sc.enter_context
            xlnT = e2(nc.sbuf_tensor("xlnT", [128, KT * TC], BF16))
            cwT = e2(nc.sbuf_tensor("cwT", [128, KT * 1024], BF16))
            dwT = e2(nc.sbuf_tensor("dwT", [128, KT * 1024], BF16))
            sdb = e2(nc.sbuf_tensor("sdb", [128, NB * BLK], F32))
            xpb = e2(nc.sbuf_tensor("xpb", [128, 2 * BLK], F32))
            psC = [e2(nc.psum_tensor(f"psC{i}", [128, BLK], F32)) for i in range(NB)]
            psD = [e2(nc.psum_tensor(f"psD{i}", [128, BLK], F32)) for i in range(NB)]
            s_tp = en(nc.semaphore("s2_tp"))
            s_w = en(nc.semaphore("s2_w"))
            s_mm = en(nc.semaphore("s2_mm"))
            s_sg = en(nc.semaphore("s2_sg"))
            s_ev = en(nc.semaphore("s2_ev"))
            s_st = PS(en, nc, "s2_st")
            GH = NB * 8
            for _i in range(2 * GH):
                s_st.reserve(_i)

            with nc.Block() as blk:

                @blk.gpsimd
                def _(g):
                    g.wait_ge(s_c_cd, CAST_CD)
                    for e in range(E):
                        g.dma_start(out=wgu_bf[e, :, :], in_=wgu_in[e, :, :]).then_inc(
                            s_c_gu, 16
                        )
                    for e in range(E):
                        g.dma_start(out=wdn_bf[e, :, :], in_=wdn_in[e, :, :]).then_inc(
                            s_c_dn, 16
                        )
                    g.dma_start(out=wsg_bf[:, :], in_=wsg_in[:, :]).then_inc(s_c_sh, 16)
                    g.dma_start(out=wsd_bf[:, :], in_=wsd_in[:, :]).then_inc(s_c_sh, 16)

                @blk.scalar
                def _(s):
                    for ct in range(KT):
                        s.dma_start_transpose(
                            xlnT[:, ct * TC : (ct + 1) * TC],
                            xln_tok[:, ct * 128 : (ct + 1) * 128],
                        ).then_inc(s_tp, 16)

                @blk.sync
                def _(sy):
                    sy.wait_ge(s_c_cd, CAST_CD)
                    for k in range(KT):
                        sy.dma_start(
                            out=cwT[:, k * 1024 : (k + 1) * 1024],
                            in_=cw_bf[k * 128 : (k + 1) * 128, 0:1024],
                        ).then_inc(s_w, 16)
                        sy.dma_start(
                            out=dwT[:, k * 1024 : (k + 1) * 1024],
                            in_=dw_bf[k * 128 : (k + 1) * 128, 0:1024],
                        ).then_inc(s_w, 16)
                    for i in range(2 * GH):
                        if i == GH:
                            sy.wait_ge(s_mm, 8)
                            for k in range(KT):
                                sy.dma_start(
                                    out=cwT[:, k * 1024 : (k + 1) * 1024],
                                    in_=cw_bf[k * 128 : (k + 1) * 128, 1024:2048],
                                ).then_inc(s_w, 16)
                                sy.dma_start(
                                    out=dwT[:, k * 1024 : (k + 1) * 1024],
                                    in_=dw_bf[k * 128 : (k + 1) * 128, 1024:2048],
                                ).then_inc(s_w, 16)
                        g, tb = i // NB, i % NB
                        sy.wait_ge(s_ev, i + 1)
                        s_st.inc(
                            sy.dma_start(
                                out=xproj_d[g * 128 : (g + 1) * 128,
                                            tb * BLK : (tb + 1) * BLK],
                                in_=xpb[:, (i % 2) * BLK : (i % 2 + 1) * BLK],
                            ),
                            i,
                        )
                    s_st.wait_all(sy)

                @blk.tensor
                def _(t):
                    t.wait_ge(s_tp, 16 * KT)
                    for oh in range(2):
                        t.wait_ge(s_w, 32 * 16 * (oh + 1))
                        for oc8 in range(8):
                            g = oh * 8 + oc8
                            if g >= 1:
                                t.wait_ge(s_ev, NB * g)
                            for k in range(KT):
                                for tb in range(NB):
                                    t.matmul(
                                        psC[tb][:, :],
                                        cwT[:, k * 1024 + oc8 * 128 : k * 1024 + (oc8 + 1) * 128],
                                        xlnT[:, k * TC + tb * BLK : k * TC + (tb + 1) * BLK],
                                        start=(k == 0), stop=(k == KT - 1),
                                    )
                            for k in range(KT):
                                for tb in range(NB):
                                    inst = t.matmul(
                                        psD[tb][:, :],
                                        dwT[:, k * 1024 + oc8 * 128 : k * 1024 + (oc8 + 1) * 128],
                                        xlnT[:, k * TC + tb * BLK : k * TC + (tb + 1) * BLK],
                                        start=(k == 0), stop=(k == KT - 1),
                                    )
                            inst.then_inc(s_mm, 1)

                @blk.scalar
                def _(s):
                    for g in range(16):
                        oc = g
                        s.wait_ge(s_mm, g + 1)
                        for tb in range(NB):
                            if g >= 1:
                                s.wait_ge(s_ev, NB * g)
                            s.activation(
                                out=sdb[:, tb * BLK : (tb + 1) * BLK],
                                in_=psD[tb][:, :], func=AF.Sigmoid,
                                bias=db_s[:, oc : oc + 1], scale=1.0,
                            ).then_inc(s_sg, 1)

                @blk.vector
                def _(v):
                    for g in range(16):
                        oc = g
                        for tb in range(NB):
                            i = g * NB + tb
                            v.wait_ge(s_sg, i + 1)
                            if i >= 2:
                                s_st.wait(v, i - 2)
                            v.scalar_tensor_tensor(
                                xpb[:, (i % 2) * BLK : (i % 2 + 1) * BLK],
                                psC[tb][:, :], cb_s[:, oc : oc + 1],
                                sdb[:, tb * BLK : (tb + 1) * BLK],
                                ALU.add, ALU.mult,
                            ).then_inc(s_ev, 1)

        with contextlib.ExitStack() as sc:
            e3 = sc.enter_context
            gs = e3(nc.sbuf_tensor("gs", [128, KT * TC], BF16))
            xpl = e3(nc.sbuf_tensor("xpl", [128, 2 * TC], F32))
            aob = e3(nc.sbuf_tensor("aob", [128, 2 * TC], BF16))
            ztile = e3(nc.sbuf_tensor("ztile", [128, TC], F32))
            zeros = e3(nc.sbuf_tensor("zeros", [128, TC], F32))
            abc = e3(nc.sbuf_tensor("abc", [128, TC], F32))
            s_ld = PS(en, nc, "s3_ld")
            s_gx = en(nc.semaphore("s3_gx"))
            s_v = en(nc.semaphore("s3_v"))
            s_cc = en(nc.semaphore("s3_cc"))
            s_st = PS(en, nc, "s3_st")
            s_fx = en(nc.semaphore("s3_fx"))
            for _ct in range(KT):
                s_ld.reserve(_ct)
            for _ct in range(KT):
                s_st.reserve(_ct)

            with nc.Block() as blk:

                @blk.sync
                def _(sy):
                    for ct in range(KT):
                        if ct >= 2:
                            sy.wait_ge(s_v, 2 * (ct - 1))
                        s_ld.inc(
                            sy.dma_start(
                                out=xpl[:, (ct % 2) * TC : (ct % 2 + 1) * TC],
                                in_=xproj_d[ct * 128 : (ct + 1) * 128, :],
                            ),
                            ct,
                        )
                    sy.wait_ge(s_v, 2 * KT)
                    sy.dma_start(out=li_b[:, :], in_=lasts[:, :]).then_inc(s_gx, 16)
                    sy.wait_ge(s_cc, 1)
                    for r in range(8):
                        sy.dma_start(
                            out=lall[:, r * KT : (r + 1) * KT],
                            in_=lo_b[r * 128 : (r + 1) * 128, :],
                        ).then_inc(s_gx, 16)

                @blk.vector
                def _(v):
                    fx4 = 0
                    v.memset(zeros[:, :], 0.0).then_inc(s_fx, 1)
                    fx4 += 1
                    v.wait_ge(s_fx, fx4)
                    for ct in range(KT):
                        s_ld.wait(v, ct)
                        v.tensor_single_scalar(abc[:, :], zeros[:, :],
                                               adiag[:, ct : ct + 1],
                                               ALU.add).then_inc(s_fx, 1)
                        fx4 += 1
                        v.wait_ge(s_fx, fx4)
                        v.tensor_tensor_scan(
                            gs[:, ct * TC : (ct + 1) * TC], abc[:, :],
                            xpl[:, (ct % 2) * TC : (ct % 2 + 1) * TC],
                            0.0, ALU.mult, ALU.add,
                        ).then_inc(s_v, 1)
                        v.wait_ge(s_v, 2 * ct + 1)
                        v.tensor_copy(
                            lasts[:, ct : ct + 1],
                            gs[:, ct * TC + TC - 1 : ct * TC + TC],
                        ).then_inc(s_v, 1)

                @blk.gpsimd
                def _(g):
                    g.wait_ge(s_gx, 16)
                    g.collective_compute(
                        "AllGather", ALU.bypass,
                        replica_groups=[list(range(8))],
                        ins=[li_b.opt()], outs=[lo_b.opt()],
                    ).then_inc(s_cc, 1)

                @blk.vector
                def _(v):
                    fx6 = [KT + 1]

                    def edge(inst):
                        inst.then_inc(s_fx, 1)
                        fx6[0] += 1
                        v.wait_ge(s_fx, fx6[0])

                    v.wait_ge(s_gx, 16 * 9)
                    for s_ in range(3):
                        edge(v.tensor_scalar_mul(pred[:, s_ * KT : (s_ + 1) * KT],
                                                 lall[:, 0:KT],
                                                 selb[:, s_ * 8 : s_ * 8 + 1]))
                        for r in range(1, 8):
                            edge(v.scalar_tensor_tensor(
                                pred[:, s_ * KT : (s_ + 1) * KT],
                                lall[:, r * KT : (r + 1) * KT],
                                selb[:, s_ * 8 + r : s_ * 8 + r + 1],
                                pred[:, s_ * KT : (s_ + 1) * KT],
                                ALU.mult, ALU.add,
                            ))
                    edge(v.tensor_mul(gcar[:, :], a2048[:, :],
                                      pred[:, 2 * KT : 3 * KT]))
                    edge(v.tensor_add(gcar[:, :], gcar[:, :],
                                      pred[:, KT : 2 * KT]))
                    edge(v.tensor_mul(gcar[:, :], a2048[:, :], gcar[:, :]))
                    v.tensor_add(gcar[:, :], gcar[:, :],
                                 pred[:, 0:KT]).then_inc(s_v, 1)
                    v.wait_ge(s_v, 2 * KT + 1)
                    for ct in range(KT):
                        if ct >= 2:
                            s_st.wait(v, ct - 2)
                        edge(v.tensor_single_scalar(abc[:, :], zeros[:, :],
                                                    adiag[:, ct : ct + 1],
                                                    ALU.add))
                        edge(v.tensor_tensor_scan(
                            ztile[:, :], abc[:, :], zeros[:, :],
                            gcar[:, ct : ct + 1], ALU.mult, ALU.add,
                        ))
                        edge(v.tensor_add(ztile[:, :],
                                          gs[:, ct * TC : (ct + 1) * TC],
                                          ztile[:, :]))
                        v.tensor_scalar_mul(
                            aob[:, (ct % 2) * TC : (ct % 2 + 1) * TC],
                            ztile[:, :], ab[:, ct : ct + 1],
                        ).then_inc(s_v, 1)

                @blk.sync
                def _(sy):
                    for ct in range(KT):
                        sy.wait_ge(s_v, 2 * KT + 1 + ct + 1)
                        s_st.inc(
                            sy.dma_start(
                                out=ao_bf[ct * 128 : (ct + 1) * 128, :],
                                in_=aob[:, (ct % 2) * TC : (ct % 2 + 1) * TC],
                            ),
                            ct,
                        )
                    s_st.wait_all(sy)

        with contextlib.ExitStack() as sc:
            e4 = sc.enter_context
            aot = e4(nc.sbuf_tensor("aot", [128, 2 * D], BF16))
            dump = e4(nc.sbuf_tensor("dump4", [128, D], F32))
            xb2 = e4(nc.sbuf_tensor("xb2", [128, 2 * D], F32))
            x2b = e4(nc.sbuf_tensor("x2b", [128, 2 * D], F32))
            x3b = e4(nc.sbuf_tensor("x3b", [128, 2 * D], BF16))
            s_tp = PS(en, nc, "s4_tp")
            s_ld = PS(en, nc, "s4_ld")
            s_st = PS(en, nc, "s4_st", ways=4)
            for _j in range(TT):
                s_tp.reserve(_j)
            for _j in range(TT):
                s_ld.reserve(_j)
            for _j in range(TT):
                s_st.reserve((_j, 0))
                s_st.reserve((_j, 1))
            s_v = en(nc.semaphore("s4_v"))
            s_sc = en(nc.semaphore("s4_sc"))
            s_fx = en(nc.semaphore("s4_fx"))

            with nc.Block() as blk:

                @blk.sync
                def _(sy):
                    for j in range(TT):
                        if j >= 2:
                            sy.wait_ge(s_v, 8 * (j - 1))
                            s_st.wait(sy, (j - 2, 0))
                            s_st.wait(sy, (j - 2, 1))
                        s_ld.inc(
                            sy.dma_start(
                                out=xb2[:, (j % 2) * D : (j % 2 + 1) * D],
                                in_=x_in[j * 128 : (j + 1) * 128, :],
                            ),
                            j,
                        )
                        if j >= 1:
                            sy.wait_ge(s_v, 8 * (j - 1) + 4)
                            s_st.inc(
                                sy.dma_start(
                                    out=x2_d[(j - 1) * 128 : j * 128, :],
                                    in_=x2b[:, ((j - 1) % 2) * D : ((j - 1) % 2 + 1) * D],
                                ),
                                (j - 1, 0),
                            )
                            sy.wait_ge(s_v, 8 * j)
                            s_st.inc(
                                sy.dma_start(
                                    out=xln3_tok[(j - 1) * 128 : j * 128, :],
                                    in_=x3b[:, ((j - 1) % 2) * D : ((j - 1) % 2 + 1) * D],
                                ),
                                (j - 1, 1),
                            )
                    sy.wait_ge(s_v, 8 * (TT - 1) + 4)
                    s_st.inc(
                        sy.dma_start(
                            out=x2_d[(TT - 1) * 128 : TT * 128, :],
                            in_=x2b[:, ((TT - 1) % 2) * D : ((TT - 1) % 2 + 1) * D],
                        ),
                        (TT - 1, 0),
                    )
                    sy.wait_ge(s_v, 8 * TT)
                    s_st.inc(
                        sy.dma_start(
                            out=xln3_tok[(TT - 1) * 128 : TT * 128, :],
                            in_=x3b[:, ((TT - 1) % 2) * D : ((TT - 1) % 2 + 1) * D],
                        ),
                        (TT - 1, 1),
                    )
                    s_st.wait_all(sy)

                @blk.scalar
                def _(s):
                    s_tp.inc(
                        s.dma_start_transpose(
                            aot[:, 0:D], ao_bf[:, 0:128]
                        ),
                        0,
                    )
                    s_tp.inc(
                        s.dma_start_transpose(
                            aot[:, D : 2 * D], ao_bf[:, 128:256]
                        ),
                        1,
                    )
                    for j in range(TT):
                        c = j * 8
                        a_ = aot[:, (j % 2) * D : (j % 2 + 1) * D]
                        x2_ = x2b[:, (j % 2) * D : (j % 2 + 1) * D]
                        s_tp.wait(s, j)
                        s.activation(out=dump[:, :], in_=a_, func=AF.Square,
                                     accum_out=stt[:, c + 1 : c + 2]).then_inc(s_sc, 1)
                        s.wait_ge(s_sc, 4 * j + 1)
                        s.wait_ge(s_v, 8 * j + 2)
                        s.activation(out=stt[:, c + 4 : c + 5],
                                     in_=stt[:, c + 1 : c + 2], func=AF.Sqrt,
                                     scale=1.0 / D,
                                     bias=stt[:, c + 3 : c + 4]).then_inc(s_sc, 1)
                        s.wait_ge(s_v, 8 * j + 4)
                        s.activation(out=dump[:, :], in_=x2_, func=AF.Square,
                                     accum_out=stt[:, c + 6 : c + 7]).then_inc(s_sc, 1)
                        s.wait_ge(s_sc, 4 * j + 3)
                        s.wait_ge(s_v, 8 * j + 6)
                        s.activation(out=stt[:, c + 4 : c + 5],
                                     in_=stt[:, c + 6 : c + 7], func=AF.Sqrt,
                                     scale=1.0 / D,
                                     bias=stt[:, c + 3 : c + 4]).then_inc(s_sc, 1)
                        if j + 2 < TT:
                            s_tp.inc(
                                s.dma_start_transpose(
                                    aot[:, (j % 2) * D : (j % 2 + 1) * D],
                                    ao_bf[:, (j + 2) * 128 : (j + 3) * 128],
                                ),
                                j + 2,
                            )

                @blk.vector
                def _(v):
                    fx = 0
                    for j in range(TT):
                        c = j * 8
                        a_ = aot[:, (j % 2) * D : (j % 2 + 1) * D]
                        x_ = xb2[:, (j % 2) * D : (j % 2 + 1) * D]
                        x2_ = x2b[:, (j % 2) * D : (j % 2 + 1) * D]
                        m = stt[:, c + 2 : c + 3]
                        s_tp.wait(v, j)
                        v.tensor_reduce(stt[:, c : c + 1], a_, AX.X,
                                        ALU.add).then_inc(s_fx, 1)
                        fx += 1
                        v.wait_ge(s_fx, fx)
                        v.tensor_scalar(m, stt[:, c : c + 1], 1.0 / D, None,
                                        ALU.mult).then_inc(s_v, 1)
                        v.wait_ge(s_v, 8 * j + 1)
                        v.tensor_mul(stt[:, c + 3 : c + 4], m, m).then_inc(s_fx, 1)
                        fx += 1
                        v.wait_ge(s_fx, fx)
                        v.tensor_scalar(stt[:, c + 3 : c + 4],
                                        stt[:, c + 3 : c + 4], -1.0, EPS,
                                        ALU.mult, ALU.add).then_inc(s_v, 1)
                        v.wait_ge(s_sc, 4 * j + 2)
                        v.reciprocal(stt[:, c + 5 : c + 6],
                                     stt[:, c + 4 : c + 5]).then_inc(s_fx, 1)
                        fx += 1
                        v.wait_ge(s_fx, fx)
                        s_ld.wait(v, j)
                        v.tensor_scalar(x2_, a_, m, stt[:, c + 5 : c + 6],
                                        ALU.subtract, ALU.mult).then_inc(s_v, 1)
                        v.wait_ge(s_v, 8 * j + 3)
                        v.tensor_add(x2_, x2_, x_).then_inc(s_v, 1)
                        v.wait_ge(s_v, 8 * j + 4)
                        v.tensor_reduce(stt[:, c : c + 1], x2_, AX.X,
                                        ALU.add).then_inc(s_fx, 1)
                        fx += 1
                        v.wait_ge(s_fx, fx)
                        v.tensor_scalar(m, stt[:, c : c + 1], 1.0 / D, None,
                                        ALU.mult).then_inc(s_v, 1)
                        v.wait_ge(s_v, 8 * j + 5)
                        v.tensor_mul(stt[:, c + 3 : c + 4], m, m).then_inc(s_fx, 1)
                        fx += 1
                        v.wait_ge(s_fx, fx)
                        v.tensor_scalar(stt[:, c + 3 : c + 4],
                                        stt[:, c + 3 : c + 4], -1.0, EPS,
                                        ALU.mult, ALU.add).then_inc(s_v, 1)
                        v.wait_ge(s_sc, 4 * (j + 1))
                        v.reciprocal(stt[:, c + 5 : c + 6],
                                     stt[:, c + 4 : c + 5]).then_inc(s_fx, 1)
                        fx += 1
                        v.wait_ge(s_fx, fx)
                        v.tensor_scalar(x3b[:, (j % 2) * D : (j % 2 + 1) * D],
                                        x2_, m, stt[:, c + 5 : c + 6],
                                        ALU.subtract, ALU.mult).then_inc(s_v, 2)

        with contextlib.ExitStack() as sc:
            e5 = sc.enter_context
            x3T = e5(nc.sbuf_tensor("x3T", [128, KT * TC], BF16))
            wgf = e5(nc.sbuf_tensor("wgf", [128, KT * E], F32))
            wgs = e5(nc.sbuf_tensor("wgs", [128, KT * E], BF16))
            iden = e5(nc.sbuf_tensor("iden", [128, 128], F32))
            icol = e5(nc.sbuf_tensor("icol", [128, 128], F32))
            ipart = e5(nc.sbuf_tensor("ipart", [128, 1], F32))
            i8 = e5(nc.sbuf_tensor("i8", [128, E], F32))
            ohm = e5(nc.sbuf_tensor("ohm", [128, E * 128], BF16))
            pbc = e5(nc.sbuf_tensor("pbc", [128, 128], F32))
            lgt = e5(nc.sbuf_tensor("lgt", [128, E], F32))
            top8 = e5(nc.sbuf_tensor("top8", [128, 8], F32))
            idx8 = e5(nc.sbuf_tensor("idx8", [128, 8], U32))
            idxf = e5(nc.sbuf_tensor("idxf", [128, 8], F32))
            oh1 = e5(nc.sbuf_tensor("oh1", [128, E], F32))
            oh2 = e5(nc.sbuf_tensor("oh2", [128, E], F32))
            comb = e5(nc.sbuf_tensor("comb", [128, E], F32))
            comb_cm = e5(nc.sbuf_tensor("comb_cm", [128, TC], BF16))
            cbb = e5(nc.sbuf_tensor("cbb", [128, TC], BF16))
            sgb = e5(nc.sbuf_tensor("sgb", [128, TC], BF16))
            ucb = e5(nc.sbuf_tensor("ucb", [128, BLK], F32))
            actb = e5(nc.sbuf_tensor("actb", [128, 8 * TC], BF16))
            wgu_s = e5(nc.sbuf_tensor("wgu_s", [128, 4 * KT * 128], BF16))
            wdn_s = e5(nc.sbuf_tensor("wdn_s", [128, 4 * 8 * 128], BF16))
            racc = e5(nc.sbuf_tensor("racc", [128, KT * TC], BF16))
            rp_stack = contextlib.ExitStack()
            ps_r = rp_stack.enter_context(nc.psum_tensor("ps_r", [128, E], F32))
            ps_t = rp_stack.enter_context(nc.psum_tensor("ps_t", [128, 128], F32))
            s_tp = en(nc.semaphore("s5_tp"))
            s_wg = en(nc.semaphore("s5_wg"))
            s_w4 = PS(en, nc, "s5_w", ways=4)
            s_mm = en(nc.semaphore("s5_mm"))
            s_sil = en(nc.semaphore("s5_sil"))
            s_act = en(nc.semaphore("s5_act"))
            s_eo = en(nc.semaphore("s5_eo"))
            s_rt = en(nc.semaphore("s5_rt"))
            s_sg = en(nc.semaphore("s5_sg"))
            s_cb = en(nc.semaphore("s5_cb"))
            s_fx8 = en(nc.semaphore("s5_fx8"))
            s_ux = en(nc.semaphore("s5_ux"))
            uxv = [0]

            mmv = [0]
            silv = [0]
            actv = [0]
            eov = [0]
            cbv = [0]
            stage_id = [0]

            with nc.Block() as blk:

                @blk.scalar
                def _(s):
                    for ct in range(KT):
                        s.dma_start_transpose(
                            x3T[:, ct * TC : (ct + 1) * TC],
                            xln3_tok[:, ct * 128 : (ct + 1) * 128],
                        ).then_inc(s_tp, 16)

                @blk.sync
                def _(sy):
                    for k in range(KT):
                        sy.dma_start(out=wgf[:, k * E : (k + 1) * E],
                                     in_=wg_in[k * 128 : (k + 1) * 128, :]).then_inc(
                            s_wg, 16)

                @blk.gpsimd
                def _(g):
                    g.iota(i8[:, :], [[1, E]], base=0, channel_multiplier=0,
                           allow_small_or_imprecise_dtypes=True)
                    g.iota(icol[:, :], [[1, 128]], base=0, channel_multiplier=0,
                           allow_small_or_imprecise_dtypes=True)
                    g.iota(ipart[:, :], [[0, 1]], base=0, channel_multiplier=1,
                           allow_small_or_imprecise_dtypes=True).then_inc(s_rt, 1)

                @blk.vector
                def _(v):
                    v.wait_ge(s_rt, 1)
                    v.tensor_single_scalar(iden[:, :], icol[:, :], ipart[:, 0:1],
                                           ALU.is_equal)
                    v.tensor_scalar_mul(pbc[:, :], icol[:, :], 0.0).then_inc(s_fx8, 1)
                    v.wait_ge(s_fx8, 1)
                    v.tensor_scalar_add(pbc[:, :], pbc[:, :],
                                        ipart[:, 0:1]).then_inc(s_fx8, 1)
                    v.wait_ge(s_fx8, 2)
                    for e in range(E):
                        v.tensor_single_scalar(ohm[:, e * 128 : (e + 1) * 128],
                                               pbc[:, :], float(e), ALU.is_equal)
                    v.memset(racc[:, :], 0.0)
                    v.wait_ge(s_wg, 16 * KT)
                    v.tensor_copy(wgs[:, :], wgf[:, :]).then_inc(s_rt, 1)

            MASKS = 2

            with nc.Block() as blk:
                mm0 = mmv[0]

                @blk.tensor
                def _(t):
                    t.wait_ge(s_tp, 16 * KT)
                    t.wait_ge(s_rt, MASKS)
                    for j in range(TT):
                        if j >= 1:
                            t.wait_ge(s_rt, MASKS + 10 * (j - 1) + 1)
                        for k in range(KT):
                            inst = t.matmul(
                                ps_r[:, :],
                                x3T[:, k * TC + j * 128 : k * TC + (j + 1) * 128],
                                wgs[:, k * E : (k + 1) * E],
                                start=(k == 0), stop=(k == KT - 1),
                            )
                        inst.then_inc(s_mm, 1)
                        t.wait_ge(s_rt, MASKS + 10 * j + 9)
                        t.transpose(ps_t[0:8, 0:128], comb[:, :],
                                    iden[:, :]).then_inc(s_mm, 1)

                @blk.vector
                def _(v):
                    for j in range(TT):
                        base = MASKS + 10 * j
                        v.wait_ge(s_mm, mm0 + 2 * j + 1)
                        v.tensor_copy(lgt[:, :], ps_r[:, :]).then_inc(s_rt, 1)
                        if j >= 1:
                            v.wait_ge(s_sg, j)
                        v.wait_ge(s_rt, base + 1)
                        v.max(top8[:, :], lgt[:, :]).then_inc(s_rt, 1)
                        v.wait_ge(s_rt, base + 2)
                        v.max_index(idx8[:, :], top8[:, :], lgt[:, :]).then_inc(s_rt, 1)
                        v.wait_ge(s_rt, base + 3)
                        v.tensor_copy(idxf[:, :], idx8[:, :]).then_inc(s_rt, 1)
                        v.tensor_sub(stt[:, 0:1], top8[:, 0:1],
                                     top8[:, 1:2]).then_inc(s_rt, 1)
                        v.wait_ge(s_sg, j + 1)
                        v.wait_ge(s_rt, base + 4)
                        v.tensor_single_scalar(oh1[:, :], i8[:, :], idxf[:, 0:1],
                                               ALU.is_equal).then_inc(s_rt, 1)
                        v.tensor_single_scalar(oh2[:, :], i8[:, :], idxf[:, 1:2],
                                               ALU.is_equal).then_inc(s_rt, 1)
                        v.wait_ge(s_rt, base + 7)
                        v.tensor_scalar_mul(comb[:, :], oh1[:, :],
                                            stt[:, 2:3]).then_inc(s_rt, 1)
                        v.wait_ge(s_rt, base + 8)
                        v.scalar_tensor_tensor(comb[:, :], oh2[:, :], stt[:, 3:4],
                                               comb[:, :], ALU.mult,
                                               ALU.add).then_inc(s_rt, 1)
                        v.wait_ge(s_mm, mm0 + 2 * j + 2)
                        v.tensor_copy(comb_cm[0:8, j * 128 : (j + 1) * 128],
                                      ps_t[0:8, 0:128]).then_inc(s_rt, 1)

                @blk.scalar
                def _(s):
                    for j in range(TT):
                        base = MASKS + 10 * j
                        s.wait_ge(s_rt, base + 5)
                        s.activation(out=stt[:, 2:3], in_=stt[:, 0:1],
                                     func=AF.Sigmoid, scale=1.0)
                        s.activation(out=stt[:, 3:4], in_=stt[:, 0:1],
                                     func=AF.Sigmoid, scale=-1.0).then_inc(s_sg, 1)

                mmv[0] = mm0 + 2 * TT

            RT_DONE = MASKS + 10 * TT
            rp_stack.close()
            ps_q = [e5(nc.psum_tensor(f"ps_q{i}", [128, BLK], F32))
                    for i in range(2 * NB)]

            def gu_stage(is_shared, e_or_ic):
                with nc.Block() as blk:
                    mm0 = mmv[0]
                    sil0 = silv[0]
                    act0 = actv[0]
                    cb_need = cbv[0]
                    sid = stage_id[0]
                    stage_id[0] += 1
                    for _li in range(16):
                        s_w4.reserve((sid, _li))

                    @blk.sync
                    def _(sy):
                        if is_shared:
                            sy.wait_ge(s_c_sh, CAST_SH)
                        else:
                            sy.wait_ge(s_c_gu, CAST_GU)
                        if mm0:
                            sy.wait_ge(s_mm, mm0)
                        for fi in range(8):
                            for half in range(2):
                                li = fi * 2 + half
                                if li >= 4:
                                    sy.wait_ge(s_mm, mm0 + (li - 3))
                                if is_shared:
                                    ft = e_or_ic * 8 + fi if half == 0 else 32 + e_or_ic * 8 + fi
                                    src = wsg_bf.rearrange("(k p) f -> p k f", p=128)[
                                        :, :, ft * 128 : (ft + 1) * 128]
                                else:
                                    ft = fi if half == 0 else 8 + fi
                                    src = wgu_bf[e_or_ic].rearrange(
                                        "(k p) f -> p k f", p=128)[
                                        :, :, ft * 128 : (ft + 1) * 128]
                                s_w4.inc(
                                    sy.dma_start(
                                        out=wgu_s[:, (li % 4) * KT * 128 : (li % 4 + 1) * KT * 128],
                                        in_=src,
                                    ),
                                    (sid, li),
                                )

                    @blk.tensor
                    def _(t):
                        if not is_shared:
                            t.wait_ge(s_cb, cb_need)
                        for li in range(16):
                            s_w4.wait(t, (sid, li))
                            q = (li % 2) * NB
                            if li >= 2:
                                pl = li - 2
                                if pl % 2 == 0:
                                    t.wait_ge(s_sil, sil0 + (pl // 2) * NB + NB)
                                else:
                                    t.wait_ge(s_act, act0 + (pl // 2) * NB + NB)
                            for k in range(KT):
                                for tb in range(NB):
                                    inst = t.matmul(
                                        ps_q[q + tb][:, :],
                                        wgu_s[:, (li % 4) * KT * 128 + k * 128 : (li % 4) * KT * 128 + (k + 1) * 128],
                                        x3T[:, k * TC + tb * BLK : k * TC + (tb + 1) * BLK],
                                        start=(k == 0), stop=(k == KT - 1),
                                    )
                            inst.then_inc(s_mm, 1)

                    @blk.scalar
                    def _(s):
                        for fi in range(8):
                            li = fi * 2
                            q = (li % 2) * NB
                            s.wait_ge(s_mm, mm0 + li + 1)
                            for tb in range(NB):
                                if fi >= 1:
                                    s.wait_ge(s_act, act0 + (fi - 1) * NB + tb + 1)
                                s.activation(
                                    out=sgb[:, tb * BLK : (tb + 1) * BLK],
                                    in_=ps_q[q + tb][:, :], func=AF.Silu,
                                ).then_inc(s_sil, 1)

                    @blk.vector
                    def _(v):
                        for fi in range(8):
                            li = fi * 2 + 1
                            q = (li % 2) * NB
                            v.wait_ge(s_mm, mm0 + li + 1)
                            for tb in range(NB):
                                if is_shared:
                                    inst = v.tensor_copy(ucb[:, :], ps_q[q + tb][:, :])
                                else:
                                    inst = v.tensor_tensor(
                                        ucb[:, :], ps_q[q + tb][:, :],
                                        cbb[:, tb * BLK : (tb + 1) * BLK], ALU.mult)
                                inst.then_inc(s_ux, 1)
                                uxv[0] += 1
                                v.wait_ge(s_ux, uxv[0])
                                v.wait_ge(s_sil, sil0 + fi * NB + tb + 1)
                                v.tensor_tensor(
                                    actb[:, fi * TC + tb * BLK : fi * TC + (tb + 1) * BLK],
                                    sgb[:, tb * BLK : (tb + 1) * BLK], ucb[:, :],
                                    ALU.mult,
                                ).then_inc(s_act, 1)

                    mmv[0] = mm0 + 16
                    silv[0] = sil0 + 8 * NB
                    actv[0] = act0 + 8 * NB

            def down_stage(is_shared, e_or_ic):
                with nc.Block() as blk:
                    mm0 = mmv[0]
                    eo0 = eov[0]
                    sid = stage_id[0]
                    stage_id[0] += 1
                    for _ht in range(KT):
                        s_w4.reserve((sid, _ht))

                    @blk.sync
                    def _(sy):
                        if not is_shared:
                            sy.wait_ge(s_c_dn, CAST_DN)
                        sy.wait_ge(s_mm, mm0)
                        for ht in range(KT):
                            if ht >= 4:
                                sy.wait_ge(s_mm, mm0 + (ht - 3))
                            if is_shared:
                                src = wsd_bf[e_or_ic * IM : (e_or_ic + 1) * IM, :].rearrange(
                                    "(k p) h -> p k h", p=128)[:, :, ht * 128 : (ht + 1) * 128]
                            else:
                                src = wdn_bf[e_or_ic].rearrange("(k p) h -> p k h", p=128)[
                                    :, :, ht * 128 : (ht + 1) * 128]
                            s_w4.inc(
                                sy.dma_start(
                                    out=wdn_s[:, (ht % 4) * 8 * 128 : (ht % 4 + 1) * 8 * 128],
                                    in_=src,
                                ),
                                (sid, ht),
                            )

                    @blk.tensor
                    def _(t):
                        for ht in range(KT):
                            s_w4.wait(t, (sid, ht))
                            q = (ht % 2) * NB
                            if ht >= 2:
                                t.wait_ge(s_eo, eo0 + (ht - 1) * NB)
                            for ik in range(8):
                                for tb in range(NB):
                                    inst = t.matmul(
                                        ps_q[q + tb][:, :],
                                        wdn_s[:, (ht % 4) * 8 * 128 + ik * 128 : (ht % 4) * 8 * 128 + (ik + 1) * 128],
                                        actb[:, ik * TC + tb * BLK : ik * TC + (tb + 1) * BLK],
                                        start=(ik == 0), stop=(ik == 7),
                                    )
                            inst.then_inc(s_mm, 1)

                    @blk.vector
                    def _(v):
                        for ht in range(KT):
                            q = (ht % 2) * NB
                            v.wait_ge(s_mm, mm0 + ht + 1)
                            for tb in range(NB):
                                sl = racc[:, ht * TC + tb * BLK : ht * TC + (tb + 1) * BLK]
                                v.tensor_tensor(sl, sl, ps_q[q + tb][:, :],
                                                ALU.add).then_inc(s_eo, 1)

                    mmv[0] = mm0 + KT
                    eov[0] = eo0 + KT * NB

            for e in range(E):
                with nc.Block() as blk:
                    mm0 = mmv[0]
                    cb0 = cbv[0]

                    @blk.tensor
                    def _(t):
                        t.wait_ge(s_rt, RT_DONE)
                        for tb in range(NB):
                            if tb >= 1:
                                t.wait_ge(s_cb, cb0 + tb)
                            t.matmul(ps_q[0][:, :], ohm[0:8, e * 128 : (e + 1) * 128],
                                     comb_cm[0:8, tb * BLK : (tb + 1) * BLK],
                                     start=True, stop=True).then_inc(s_mm, 1)

                    @blk.vector
                    def _(v):
                        for tb in range(NB):
                            v.wait_ge(s_mm, mm0 + tb + 1)
                            v.tensor_copy(cbb[:, tb * BLK : (tb + 1) * BLK],
                                          ps_q[0][:, :]).then_inc(s_cb, 1)

                    mmv[0] = mm0 + NB
                    cbv[0] = cb0 + NB

                gu_stage(False, e)
                down_stage(False, e)

            for ic in range(4):
                gu_stage(True, ic)
                down_stage(True, ic)

            with nc.Block() as blk:

                @blk.sync
                def _(sy):
                    for ct in range(KT):
                        sy.dma_start(
                            out=ffn_bf[ct * 128 : (ct + 1) * 128, :],
                            in_=racc[:, ct * TC : (ct + 1) * TC],
                        ).then_inc(s_wg, 16)
                    sy.wait_ge(s_wg, 16 * KT + 16 * KT)

        with contextlib.ExitStack() as sc:
            e7 = sc.enter_context
            fnt = e7(nc.sbuf_tensor("fnt", [128, 2 * D], BF16))
            dump = e7(nc.sbuf_tensor("dump7", [128, D], F32))
            x2l = e7(nc.sbuf_tensor("x2l", [128, 2 * D], F32))
            yb = e7(nc.sbuf_tensor("yb", [128, 2 * D], F32))
            s_tp = PS(en, nc, "s7_tp")
            s_ld = PS(en, nc, "s7_ld")
            s_st = PS(en, nc, "s7_st")
            for _j in range(TT):
                s_tp.reserve(_j)
            for _j in range(TT):
                s_ld.reserve(_j)
            for _j in range(TT):
                s_st.reserve(_j)
            s_v = en(nc.semaphore("s7_v"))
            s_sc = en(nc.semaphore("s7_sc"))
            s_fx = en(nc.semaphore("s7_fx"))

            with nc.Block() as blk:

                @blk.sync
                def _(sy):
                    for j in range(TT):
                        if j >= 2:
                            sy.wait_ge(s_v, 4 * (j - 1))
                        s_ld.inc(
                            sy.dma_start(
                                out=x2l[:, (j % 2) * D : (j % 2 + 1) * D],
                                in_=x2_d[j * 128 : (j + 1) * 128, :],
                            ),
                            j,
                        )
                        if j >= 1:
                            sy.wait_ge(s_v, 4 * j)
                            s_st.inc(
                                sy.dma_start(
                                    out=y_out[(j - 1) * 128 : j * 128, :],
                                    in_=yb[:, ((j - 1) % 2) * D : ((j - 1) % 2 + 1) * D],
                                ),
                                j - 1,
                            )
                    sy.wait_ge(s_v, 4 * TT)
                    s_st.inc(
                        sy.dma_start(
                            out=y_out[(TT - 1) * 128 : TT * 128, :],
                            in_=yb[:, ((TT - 1) % 2) * D : ((TT - 1) % 2 + 1) * D],
                        ),
                        TT - 1,
                    )
                    s_st.wait_all(sy)

                @blk.scalar
                def _(s):
                    s_tp.inc(
                        s.dma_start_transpose(fnt[:, 0:D], ffn_bf[:, 0:128]), 0
                    )
                    s_tp.inc(
                        s.dma_start_transpose(fnt[:, D : 2 * D], ffn_bf[:, 128:256]), 1
                    )
                    for j in range(TT):
                        c = j * 8
                        f_ = fnt[:, (j % 2) * D : (j % 2 + 1) * D]
                        s_tp.wait(s, j)
                        s.activation(out=dump[:, :], in_=f_, func=AF.Square,
                                     accum_out=stt[:, c + 1 : c + 2]).then_inc(s_sc, 1)
                        s.wait_ge(s_sc, 2 * j + 1)
                        s.wait_ge(s_v, 4 * j + 2)
                        s.activation(out=stt[:, c + 4 : c + 5],
                                     in_=stt[:, c + 1 : c + 2], func=AF.Sqrt,
                                     scale=1.0 / D,
                                     bias=stt[:, c + 3 : c + 4]).then_inc(s_sc, 1)
                        if j + 2 < TT:
                            s.wait_ge(s_v, 4 * (j + 1))
                            s_tp.inc(
                                s.dma_start_transpose(
                                    fnt[:, (j % 2) * D : (j % 2 + 1) * D],
                                    ffn_bf[:, (j + 2) * 128 : (j + 3) * 128],
                                ),
                                j + 2,
                            )

                @blk.vector
                def _(v):
                    fx = 0
                    for j in range(TT):
                        c = j * 8
                        f_ = fnt[:, (j % 2) * D : (j % 2 + 1) * D]
                        x2_ = x2l[:, (j % 2) * D : (j % 2 + 1) * D]
                        y_ = yb[:, (j % 2) * D : (j % 2 + 1) * D]
                        m = stt[:, c + 2 : c + 3]
                        s_tp.wait(v, j)
                        v.tensor_reduce(stt[:, c : c + 1], f_, AX.X,
                                        ALU.add).then_inc(s_fx, 1)
                        fx += 1
                        v.wait_ge(s_fx, fx)
                        v.tensor_scalar(m, stt[:, c : c + 1], 1.0 / D, None,
                                        ALU.mult).then_inc(s_v, 1)
                        v.wait_ge(s_v, 4 * j + 1)
                        v.tensor_mul(stt[:, c + 3 : c + 4], m, m).then_inc(s_fx, 1)
                        fx += 1
                        v.wait_ge(s_fx, fx)
                        v.tensor_scalar(stt[:, c + 3 : c + 4],
                                        stt[:, c + 3 : c + 4], -1.0, EPS,
                                        ALU.mult, ALU.add).then_inc(s_v, 1)
                        v.wait_ge(s_sc, 2 * (j + 1))
                        v.reciprocal(stt[:, c + 5 : c + 6],
                                     stt[:, c + 4 : c + 5]).then_inc(s_fx, 1)
                        fx += 1
                        v.wait_ge(s_fx, fx)
                        s_ld.wait(v, j)
                        if j >= 2:
                            s_st.wait(v, j - 2)
                        v.tensor_scalar(y_, f_, m, stt[:, c + 5 : c + 6],
                                        ALU.subtract, ALU.mult).then_inc(s_v, 1)
                        v.wait_ge(s_v, 4 * j + 3)
                        v.tensor_add(y_, y_, x2_).then_inc(s_v, 1)


    if debug:
        with nc.semaphore("s_dbg") as s_dbg, nc.Block() as blk:

            @blk.gpsimd
            def _(g):
                n = 0
                for name, (o, s) in dbg.items():
                    g.dma_start(out=o[:, :], in_=s[:, :]).then_inc(s_dbg, 16)
                    n += 16
                g.wait_ge(s_dbg, n)

    return nc


_NC_CACHE = None


def kernel(**inputs):
    global _NC_CACHE
    x = np.ascontiguousarray(np.asarray(inputs["L0_attn_x"], np.float32))
    wnames = {
        "A": "L0_attn_A", "B": "L0_attn_B", "C_w": "L0_attn_C_w",
        "C_b": "L0_attn_C_b", "D_w": "L0_attn_D_w", "D_b": "L0_attn_D_b",
        "log_delta": "L0_attn_log_delta", "w_gate": "L0_ffn_w_gate",
        "w_gate_up": "L0_ffn_w_gate_up", "w_down": "L0_ffn_w_down",
        "w_sh_gate_up": "L0_ffn_w_sh_gate_up", "w_sh_down": "L0_ffn_w_sh_down",
    }
    weights = {k: np.ascontiguousarray(np.asarray(inputs[v], np.float32))
               for k, v in wnames.items()}
    weights["C_w"] = np.ascontiguousarray(weights["C_w"].T)
    weights["D_w"] = np.ascontiguousarray(weights["D_w"].T)

    if _NC_CACHE is None:
        _NC_CACHE = build_nc()
    nc = _NC_CACHE

    in_maps = []
    for c in range(8):
        b, q = c // 4, c % 4
        shard = np.ascontiguousarray(x[b, q * TC : (q + 1) * TC, :])
        sel = np.zeros((3, 8), np.float32)
        for s_ in range(3):
            r = q - 1 - s_
            if r >= 0:
                sel[s_, 4 * b + r] = 1.0
        m = {"x": shard, "sel": sel}
        m.update(weights)
        in_maps.append(m)

    res = run_bass_kernel_spmd(nc, in_maps, core_ids=list(range(8)))
    out = np.empty((2, 8192, D), np.float32)
    for c in range(8):
        b, q = c // 4, c % 4
        out[b, q * TC : (q + 1) * TC, :] = res.results[c]["y"]
    return out


if __name__ == "__main__":
    nc = build_nc()
    print("built ok")


# version 09
# speedup vs baseline: 1.2346x; 1.2010x over previous
import contextlib
import numpy as np
import concourse.bass as bass
import concourse.mybir as mybir
from concourse.bass_utils import run_bass_kernel_spmd

F32 = mybir.dt.float32
BF16 = mybir.dt.bfloat16
U32 = mybir.dt.uint32
AF = mybir.ActivationFunctionType
ALU = mybir.AluOpType
AX = mybir.AxisListType

D = 2048
TC = 2048
KT = 16
TT = 16
NB = 4
BLK = 512
E = 8
IM = 1024
ISH = 4096
EPS = 1e-5


class PS:

    def __init__(self, en, nc, name, ways=2):
        self.s = [en(nc.semaphore(f"{name}_{i}")) for i in range(ways)]
        self.c = [0] * ways
        self.ways = ways
        self.n = 0
        self.mark = {}

    def reserve(self, key):
        assert key not in self.mark
        w = self.n % self.ways
        self.n += 1
        self.c[w] += 16
        self.mark[key] = (w, self.c[w])

    def inc(self, inst, key):
        w, _ = self.mark[key]
        inst.then_inc(self.s[w], 16)

    def wait(self, eng, key):
        w, v = self.mark[key]
        eng.wait_ge(self.s[w], v)

    def wait_all(self, eng):
        for w in range(self.ways):
            if self.c[w]:
                eng.wait_ge(self.s[w], self.c[w])


def build_nc(tc=2048, blk=512, debug=False):
    global TC, TT, NB, BLK
    TC, BLK = tc, blk
    TT, NB = TC // 128, TC // BLK

    nc = bass.Bass("TRN2", target_bir_lowering=False)

    x_in = nc.dram_tensor("x", [TC, D], F32, kind="ExternalInput").ap()
    sel_in = nc.dram_tensor("sel", [3, 8], F32, kind="ExternalInput").ap()
    A_in = nc.dram_tensor("A", [D, 64], F32, kind="ExternalInput").ap()
    B_in = nc.dram_tensor("B", [D, 64], F32, kind="ExternalInput").ap()
    Cw_in = nc.dram_tensor("C_w", [D, D], F32, kind="ExternalInput").ap()
    Cb_in = nc.dram_tensor("C_b", [D], F32, kind="ExternalInput").ap()
    Dw_in = nc.dram_tensor("D_w", [D, D], F32, kind="ExternalInput").ap()
    Db_in = nc.dram_tensor("D_b", [D], F32, kind="ExternalInput").ap()
    ld_in = nc.dram_tensor("log_delta", [D], F32, kind="ExternalInput").ap()
    wg_in = nc.dram_tensor("w_gate", [D, E], F32, kind="ExternalInput").ap()
    wgu_in = nc.dram_tensor("w_gate_up", [E, D, 2 * IM], F32, kind="ExternalInput").ap()
    wdn_in = nc.dram_tensor("w_down", [E, IM, D], F32, kind="ExternalInput").ap()
    wsg_in = nc.dram_tensor("w_sh_gate_up", [D, 2 * ISH], F32, kind="ExternalInput").ap()
    wsd_in = nc.dram_tensor("w_sh_down", [ISH, D], F32, kind="ExternalInput").ap()
    y_out = nc.dram_tensor("y", [TC, D], F32, kind="ExternalOutput").ap()

    xln_tok = nc.dram_tensor("xln_tok", [TC, D], BF16).ap()
    xproj_d = nc.dram_tensor("xproj_d", [D, TC], F32).ap()
    ao_bf = nc.dram_tensor("ao_bf", [D, TC], BF16).ap()
    x2_d = nc.dram_tensor("x2_d", [TC, D], F32).ap()
    xln3_tok = nc.dram_tensor("xln3_tok", [TC, D], BF16).ap()
    ffn_bf = nc.dram_tensor("ffn_bf", [D, TC], BF16).ap()
    li_b = nc.dram_tensor("lasts_in", [128, KT], F32).ap()
    lo_b = nc.dram_tensor("lasts_out", [8 * 128, KT], F32).ap()
    if debug:
        dbg = {
            "d_xln": (nc.dram_tensor("d_xln", [TC, D], F32, kind="ExternalOutput").ap(), xln_tok),
            "d_xp": (nc.dram_tensor("d_xp", [D, TC], F32, kind="ExternalOutput").ap(), xproj_d),
            "d_ao": (nc.dram_tensor("d_ao", [D, TC], F32, kind="ExternalOutput").ap(), ao_bf),
            "d_x2": (nc.dram_tensor("d_x2", [TC, D], F32, kind="ExternalOutput").ap(), x2_d),
            "d_x3": (nc.dram_tensor("d_x3", [TC, D], F32, kind="ExternalOutput").ap(), xln3_tok),
            "d_ffn": (nc.dram_tensor("d_ffn", [D, TC], F32, kind="ExternalOutput").ap(), ffn_bf),
        }

    stk = contextlib.ExitStack()
    with stk:
        en = stk.enter_context

        adiag = en(nc.sbuf_tensor("adiag", [128, KT], F32))
        a2048 = en(nc.sbuf_tensor("a2048", [128, KT], F32))
        asq = en(nc.sbuf_tensor("asq", [128, KT], F32))
        ab = en(nc.sbuf_tensor("ab", [128, KT], F32))
        cb_s = en(nc.sbuf_tensor("cb_s", [128, KT], F32))
        db_s = en(nc.sbuf_tensor("db_s", [128, KT], F32))
        ldl = en(nc.sbuf_tensor("ldl", [128, KT], F32))
        abt = en(nc.sbuf_tensor("abt", [128, 2 * 64], F32))
        bbt = en(nc.sbuf_tensor("bbt", [128, 2 * 64], F32))
        lasts = en(nc.sbuf_tensor("lasts", [128, KT], F32))
        lall = en(nc.sbuf_tensor("lall", [128, 8 * KT], F32))
        gcar = en(nc.sbuf_tensor("gcar", [128, KT], F32))
        pred = en(nc.sbuf_tensor("pred", [128, 3 * KT], F32))
        selr = en(nc.sbuf_tensor("selr", [128, 24], F32))
        selb = en(nc.sbuf_tensor("selb", [128, 24], F32))
        onesr = en(nc.sbuf_tensor("onesr", [128, 128], F32))
        stt = en(nc.sbuf_tensor("stt", [128, 16 * 8], F32))

        s_p0 = en(nc.semaphore("s_p0"))
        s_ab0 = en(nc.semaphore("s_ab0"))
        s_ab1 = en(nc.semaphore("s_ab1"))
        s_abp = [s_ab0, s_ab1]
        s_pre = en(nc.semaphore("s_pre"))
        s_sel = en(nc.semaphore("s_sel"))
        s_fx0 = en(nc.semaphore("s_fx0"))

        P0_PARAMS = 49 * 16

        def p0_bodies(blk):

            @blk.sync
            def _(sy):
                for j in range(KT):
                    sl = slice(j * 128, (j + 1) * 128)
                    o1 = lambda t: t.rearrange("(p o) -> p o", o=1)
                    sy.dma_start(out=ldl[:, j : j + 1], in_=o1(ld_in[sl])).then_inc(s_p0, 16)
                    sy.dma_start(out=cb_s[:, j : j + 1], in_=o1(Cb_in[sl])).then_inc(s_p0, 16)
                    sy.dma_start(out=db_s[:, j : j + 1], in_=o1(Db_in[sl])).then_inc(s_p0, 16)
                sy.dma_start(
                    out=selr[0:1, 0:24],
                    in_=sel_in.rearrange("a b -> (a b)").rearrange("(o f) -> o f", o=1),
                ).then_inc(s_p0, 16)
                for j in range(KT):
                    if j >= 2:
                        sy.wait_ge(s_pre, 3 + j - 1)
                    p = j % 2
                    sy.dma_start(out=abt[:, p * 64 : (p + 1) * 64], in_=A_in[j * 128 : (j + 1) * 128, :]).then_inc(s_abp[p], 16)
                    sy.dma_start(out=bbt[:, p * 64 : (p + 1) * 64], in_=B_in[j * 128 : (j + 1) * 128, :]).then_inc(s_abp[p], 16)

            @blk.scalar
            def _(s):
                s.wait_ge(s_p0, P0_PARAMS)
                s.activation(out=adiag[:, :], in_=ldl[:, :], func=AF.Sigmoid,
                             scale=-1.0).then_inc(s_pre, 1)

            @blk.vector
            def _(v):
                fx0 = [0]

                def edge(inst):
                    inst.then_inc(s_fx0, 1)
                    fx0[0] += 1
                    v.wait_ge(s_fx0, fx0[0])

                v.wait_ge(s_pre, 1)
                edge(v.tensor_mul(asq[:, :], adiag[:, :], adiag[:, :]))
                cur, oth = asq, a2048
                for _ in range(TC.bit_length() - 2):
                    edge(v.tensor_mul(oth[:, :], cur[:, :], cur[:, :]))
                    cur, oth = oth, cur
                if cur is not a2048:
                    edge(v.tensor_copy(a2048[:, :], cur[:, :]))
                v.memset(onesr[0:1, :], 1.0).then_inc(s_pre, 2)
                for j in range(KT):
                    p = j % 2
                    v.wait_ge(s_abp[p], 32 * (j // 2 + 1))
                    edge(v.tensor_mul(bbt[:, p * 64 : (p + 1) * 64],
                                      abt[:, p * 64 : (p + 1) * 64],
                                      bbt[:, p * 64 : (p + 1) * 64]))
                    v.tensor_reduce(ab[:, j : j + 1],
                                    bbt[:, p * 64 : (p + 1) * 64], AX.X,
                                    ALU.add).then_inc(s_pre, 1)

            @blk.tensor
            def _(t):
                t.wait_ge(s_pre, 3)
                t.wait_ge(s_p0, P0_PARAMS)
                t.matmul(pss[:, 0:24], onesr[0:1, :], selr[0:1, 0:24],
                         start=True, stop=True).then_inc(s_sel, 1)

        with contextlib.ExitStack() as sc:
            e1 = sc.enter_context
            xb = e1(nc.sbuf_tensor("xb", [128, 2 * D], F32))
            xlnb = e1(nc.sbuf_tensor("xlnb", [128, 2 * D], BF16))
            dump = e1(nc.sbuf_tensor("dump1", [128, D], F32))
            pss = e1(nc.psum_tensor("pss", [128, 32], F32))
            s_ld = PS(en, nc, "s1_ld")
            s_st = PS(en, nc, "s1_st")
            for _j in range(TT):
                s_ld.reserve(_j)
            for _j in range(TT):
                s_st.reserve(_j)
            s_v = en(nc.semaphore("s1_v"))
            s_sc = en(nc.semaphore("s1_sc"))
            s_fx = en(nc.semaphore("s1_fx"))

            with nc.Block() as blk:
                p0_bodies(blk)

                @blk.sync
                def _(sy):
                    for j in range(TT):
                        if j >= 2:
                            sy.wait_ge(s_v, 4 * (j - 1))
                            sy.wait_ge(s_sc, 2 * (j - 1))
                            s_st.wait(sy, j - 2)
                        s_ld.inc(
                            sy.dma_start(
                                out=xb[:, (j % 2) * D : (j % 2 + 1) * D],
                                in_=x_in[j * 128 : (j + 1) * 128, :],
                            ),
                            j,
                        )
                        if j >= 1:
                            sy.wait_ge(s_v, 4 * j)
                            s_st.inc(
                                sy.dma_start(
                                    out=xln_tok[(j - 1) * 128 : j * 128, :],
                                    in_=xlnb[:, ((j - 1) % 2) * D : ((j - 1) % 2 + 1) * D],
                                ),
                                j - 1,
                            )
                    sy.wait_ge(s_v, 4 * TT)
                    s_st.inc(
                        sy.dma_start(
                            out=xln_tok[(TT - 1) * 128 : TT * 128, :],
                            in_=xlnb[:, ((TT - 1) % 2) * D : ((TT - 1) % 2 + 1) * D],
                        ),
                        TT - 1,
                    )
                    s_st.wait_all(sy)

                @blk.scalar
                def _(s):
                    for j in range(TT):
                        xj = xb[:, (j % 2) * D : (j % 2 + 1) * D]
                        c = j * 8
                        s_ld.wait(s, j)
                        s.activation(out=dump[:, :], in_=xj, func=AF.Square,
                                     accum_out=stt[:, c + 1 : c + 2]).then_inc(s_sc, 1)
                        s.wait_ge(s_sc, 2 * j + 1)
                        s.wait_ge(s_v, 4 * j + 2)
                        s.activation(out=stt[:, c + 4 : c + 5],
                                     in_=stt[:, c + 1 : c + 2], func=AF.Sqrt,
                                     scale=1.0 / D,
                                     bias=stt[:, c + 3 : c + 4]).then_inc(s_sc, 1)

                @blk.vector
                def _(v):
                    fx = 0
                    for j in range(TT):
                        xj = xb[:, (j % 2) * D : (j % 2 + 1) * D]
                        c = j * 8
                        m = stt[:, c + 2 : c + 3]
                        s_ld.wait(v, j)
                        v.tensor_reduce(stt[:, c : c + 1], xj, AX.X,
                                        ALU.add).then_inc(s_fx, 1)
                        fx += 1
                        v.wait_ge(s_fx, fx)
                        v.tensor_scalar(m, stt[:, c : c + 1], 1.0 / D, None,
                                        ALU.mult).then_inc(s_v, 1)
                        v.wait_ge(s_v, 4 * j + 1)
                        v.tensor_mul(stt[:, c + 3 : c + 4], m, m).then_inc(s_fx, 1)
                        fx += 1
                        v.wait_ge(s_fx, fx)
                        v.tensor_scalar(stt[:, c + 3 : c + 4],
                                        stt[:, c + 3 : c + 4], -1.0, EPS,
                                        ALU.mult, ALU.add).then_inc(s_v, 1)
                        v.wait_ge(s_sc, 2 * (j + 1))
                        v.reciprocal(stt[:, c + 5 : c + 6],
                                     stt[:, c + 4 : c + 5]).then_inc(s_fx, 1)
                        fx += 1
                        v.wait_ge(s_fx, fx)
                        v.tensor_scalar(xlnb[:, (j % 2) * D : (j % 2 + 1) * D],
                                        xj, m, stt[:, c + 5 : c + 6],
                                        ALU.subtract, ALU.mult).then_inc(s_v, 2)

        
                @blk.vector
                def _(v):
                    v.wait_ge(s_sel, 1)
                    v.tensor_copy(selb[:, :], pss[:, 0:24]).then_inc(s_sel, 1)

        with contextlib.ExitStack() as sc:
            e2 = sc.enter_context
            xlnT = e2(nc.sbuf_tensor("xlnT", [128, KT * TC], BF16))
            cwT = e2(nc.sbuf_tensor("cwT", [128, KT * 1024], BF16))
            dwT = e2(nc.sbuf_tensor("dwT", [128, KT * 1024], BF16))
            sdb = e2(nc.sbuf_tensor("sdb", [128, NB * BLK], F32))
            xpb = e2(nc.sbuf_tensor("xpb", [128, 2 * BLK], F32))
            psC = [e2(nc.psum_tensor(f"psC{i}", [128, BLK], F32)) for i in range(NB)]
            psD = [e2(nc.psum_tensor(f"psD{i}", [128, BLK], F32)) for i in range(NB)]
            s_tp = en(nc.semaphore("s2_tp"))
            s_w = en(nc.semaphore("s2_w"))
            s_mm = en(nc.semaphore("s2_mm"))
            s_sg = en(nc.semaphore("s2_sg"))
            s_ev = en(nc.semaphore("s2_ev"))
            s_st = PS(en, nc, "s2_st")
            GH = NB * 8
            for _i in range(2 * GH):
                s_st.reserve(_i)

            with nc.Block() as blk:

                @blk.gpsimd
                def _(g):
                    for k in range(KT):
                        g.dma_start(
                            out=cwT[:, k * 1024 : (k + 1) * 1024],
                            in_=Cw_in[k * 128 : (k + 1) * 128, 0:1024],
                        ).then_inc(s_w, 16)
                        g.dma_start(
                            out=dwT[:, k * 1024 : (k + 1) * 1024],
                            in_=Dw_in[k * 128 : (k + 1) * 128, 0:1024],
                        ).then_inc(s_w, 16)
                    g.wait_ge(s_mm, 8)
                    for k in range(KT):
                        g.dma_start(
                            out=cwT[:, k * 1024 : (k + 1) * 1024],
                            in_=Cw_in[k * 128 : (k + 1) * 128, 1024:2048],
                        ).then_inc(s_w, 16)
                        g.dma_start(
                            out=dwT[:, k * 1024 : (k + 1) * 1024],
                            in_=Dw_in[k * 128 : (k + 1) * 128, 1024:2048],
                        ).then_inc(s_w, 16)

                @blk.scalar
                def _(s):
                    for ct in range(KT):
                        s.dma_start_transpose(
                            xlnT[:, ct * TC : (ct + 1) * TC],
                            xln_tok[:, ct * 128 : (ct + 1) * 128],
                        ).then_inc(s_tp, 16)

                @blk.sync
                def _(sy):
                    for i in range(2 * GH):
                        g, tb = i // NB, i % NB
                        sy.wait_ge(s_ev, i + 1)
                        s_st.inc(
                            sy.dma_start(
                                out=xproj_d[g * 128 : (g + 1) * 128,
                                            tb * BLK : (tb + 1) * BLK],
                                in_=xpb[:, (i % 2) * BLK : (i % 2 + 1) * BLK],
                            ),
                            i,
                        )
                    s_st.wait_all(sy)

                @blk.tensor
                def _(t):
                    t.wait_ge(s_tp, 16 * KT)
                    for oh in range(2):
                        t.wait_ge(s_w, 32 * 16 * (oh + 1))
                        for oc8 in range(8):
                            g = oh * 8 + oc8
                            if g >= 1:
                                t.wait_ge(s_ev, NB * g)
                            for k in range(KT):
                                for tb in range(NB):
                                    t.matmul(
                                        psC[tb][:, :],
                                        cwT[:, k * 1024 + oc8 * 128 : k * 1024 + (oc8 + 1) * 128],
                                        xlnT[:, k * TC + tb * BLK : k * TC + (tb + 1) * BLK],
                                        start=(k == 0), stop=(k == KT - 1),
                                    )
                            for k in range(KT):
                                for tb in range(NB):
                                    inst = t.matmul(
                                        psD[tb][:, :],
                                        dwT[:, k * 1024 + oc8 * 128 : k * 1024 + (oc8 + 1) * 128],
                                        xlnT[:, k * TC + tb * BLK : k * TC + (tb + 1) * BLK],
                                        start=(k == 0), stop=(k == KT - 1),
                                    )
                            inst.then_inc(s_mm, 1)

                @blk.scalar
                def _(s):
                    for g in range(16):
                        oc = g
                        s.wait_ge(s_mm, g + 1)
                        for tb in range(NB):
                            if g >= 1:
                                s.wait_ge(s_ev, NB * g)
                            s.activation(
                                out=sdb[:, tb * BLK : (tb + 1) * BLK],
                                in_=psD[tb][:, :], func=AF.Sigmoid,
                                bias=db_s[:, oc : oc + 1], scale=1.0,
                            ).then_inc(s_sg, 1)

                @blk.vector
                def _(v):
                    for g in range(16):
                        oc = g
                        for tb in range(NB):
                            i = g * NB + tb
                            v.wait_ge(s_sg, i + 1)
                            if i >= 2:
                                s_st.wait(v, i - 2)
                            v.scalar_tensor_tensor(
                                xpb[:, (i % 2) * BLK : (i % 2 + 1) * BLK],
                                psC[tb][:, :], cb_s[:, oc : oc + 1],
                                sdb[:, tb * BLK : (tb + 1) * BLK],
                                ALU.add, ALU.mult,
                            ).then_inc(s_ev, 1)

        with contextlib.ExitStack() as sc:
            e3 = sc.enter_context
            gs = e3(nc.sbuf_tensor("gs", [128, KT * TC], BF16))
            xpl = e3(nc.sbuf_tensor("xpl", [128, 2 * TC], F32))
            aob = e3(nc.sbuf_tensor("aob", [128, 2 * TC], BF16))
            ztile = e3(nc.sbuf_tensor("ztile", [128, TC], F32))
            zeros = e3(nc.sbuf_tensor("zeros", [128, TC], F32))
            abc = e3(nc.sbuf_tensor("abc", [128, TC], F32))
            s_ld = PS(en, nc, "s3_ld")
            s_gx = en(nc.semaphore("s3_gx"))
            s_v = en(nc.semaphore("s3_v"))
            s_cc = en(nc.semaphore("s3_cc"))
            s_st = PS(en, nc, "s3_st")
            s_fx = en(nc.semaphore("s3_fx"))
            for _ct in range(KT):
                s_ld.reserve(_ct)
            for _ct in range(KT):
                s_st.reserve(_ct)

            with nc.Block() as blk:

                @blk.sync
                def _(sy):
                    for ct in range(KT):
                        if ct >= 2:
                            sy.wait_ge(s_v, 2 * (ct - 1))
                        s_ld.inc(
                            sy.dma_start(
                                out=xpl[:, (ct % 2) * TC : (ct % 2 + 1) * TC],
                                in_=xproj_d[ct * 128 : (ct + 1) * 128, :],
                            ),
                            ct,
                        )
                    sy.wait_ge(s_v, 2 * KT)
                    sy.dma_start(out=li_b[:, :], in_=lasts[:, :]).then_inc(s_gx, 16)
                    sy.wait_ge(s_cc, 1)
                    for r in range(8):
                        sy.dma_start(
                            out=lall[:, r * KT : (r + 1) * KT],
                            in_=lo_b[r * 128 : (r + 1) * 128, :],
                        ).then_inc(s_gx, 16)

                @blk.vector
                def _(v):
                    fx4 = 0
                    v.memset(zeros[:, :], 0.0).then_inc(s_fx, 1)
                    fx4 += 1
                    v.wait_ge(s_fx, fx4)
                    for ct in range(KT):
                        s_ld.wait(v, ct)
                        v.tensor_single_scalar(abc[:, :], zeros[:, :],
                                               adiag[:, ct : ct + 1],
                                               ALU.add).then_inc(s_fx, 1)
                        fx4 += 1
                        v.wait_ge(s_fx, fx4)
                        v.tensor_tensor_scan(
                            gs[:, ct * TC : (ct + 1) * TC], abc[:, :],
                            xpl[:, (ct % 2) * TC : (ct % 2 + 1) * TC],
                            0.0, ALU.mult, ALU.add,
                        ).then_inc(s_v, 1)
                        v.wait_ge(s_v, 2 * ct + 1)
                        v.tensor_copy(
                            lasts[:, ct : ct + 1],
                            gs[:, ct * TC + TC - 1 : ct * TC + TC],
                        ).then_inc(s_v, 1)

                @blk.gpsimd
                def _(g):
                    g.wait_ge(s_gx, 16)
                    g.collective_compute(
                        "AllGather", ALU.bypass,
                        replica_groups=[list(range(8))],
                        ins=[li_b.opt()], outs=[lo_b.opt()],
                    ).then_inc(s_cc, 1)

                @blk.vector
                def _(v):
                    fx6 = [KT + 1]

                    def edge(inst):
                        inst.then_inc(s_fx, 1)
                        fx6[0] += 1
                        v.wait_ge(s_fx, fx6[0])

                    v.wait_ge(s_gx, 16 * 9)
                    for s_ in range(3):
                        edge(v.tensor_scalar_mul(pred[:, s_ * KT : (s_ + 1) * KT],
                                                 lall[:, 0:KT],
                                                 selb[:, s_ * 8 : s_ * 8 + 1]))
                        for r in range(1, 8):
                            edge(v.scalar_tensor_tensor(
                                pred[:, s_ * KT : (s_ + 1) * KT],
                                lall[:, r * KT : (r + 1) * KT],
                                selb[:, s_ * 8 + r : s_ * 8 + r + 1],
                                pred[:, s_ * KT : (s_ + 1) * KT],
                                ALU.mult, ALU.add,
                            ))
                    edge(v.tensor_mul(gcar[:, :], a2048[:, :],
                                      pred[:, 2 * KT : 3 * KT]))
                    edge(v.tensor_add(gcar[:, :], gcar[:, :],
                                      pred[:, KT : 2 * KT]))
                    edge(v.tensor_mul(gcar[:, :], a2048[:, :], gcar[:, :]))
                    v.tensor_add(gcar[:, :], gcar[:, :],
                                 pred[:, 0:KT]).then_inc(s_v, 1)
                    v.wait_ge(s_v, 2 * KT + 1)
                    for ct in range(KT):
                        if ct >= 2:
                            s_st.wait(v, ct - 2)
                        edge(v.tensor_single_scalar(abc[:, :], zeros[:, :],
                                                    adiag[:, ct : ct + 1],
                                                    ALU.add))
                        edge(v.tensor_tensor_scan(
                            ztile[:, :], abc[:, :], zeros[:, :],
                            gcar[:, ct : ct + 1], ALU.mult, ALU.add,
                        ))
                        edge(v.tensor_add(ztile[:, :],
                                          gs[:, ct * TC : (ct + 1) * TC],
                                          ztile[:, :]))
                        v.tensor_scalar_mul(
                            aob[:, (ct % 2) * TC : (ct % 2 + 1) * TC],
                            ztile[:, :], ab[:, ct : ct + 1],
                        ).then_inc(s_v, 1)

                @blk.sync
                def _(sy):
                    for ct in range(KT):
                        sy.wait_ge(s_v, 2 * KT + 1 + ct + 1)
                        s_st.inc(
                            sy.dma_start(
                                out=ao_bf[ct * 128 : (ct + 1) * 128, :],
                                in_=aob[:, (ct % 2) * TC : (ct % 2 + 1) * TC],
                            ),
                            ct,
                        )
                    s_st.wait_all(sy)

        with contextlib.ExitStack() as sc:
            e4 = sc.enter_context
            aot = e4(nc.sbuf_tensor("aot", [128, 2 * D], BF16))
            dump = e4(nc.sbuf_tensor("dump4", [128, D], F32))
            xb2 = e4(nc.sbuf_tensor("xb2", [128, 2 * D], F32))
            x2b = e4(nc.sbuf_tensor("x2b", [128, 2 * D], F32))
            x3b = e4(nc.sbuf_tensor("x3b", [128, 2 * D], BF16))
            s_tp = PS(en, nc, "s4_tp")
            s_ld = PS(en, nc, "s4_ld")
            s_st = PS(en, nc, "s4_st", ways=4)
            for _j in range(TT):
                s_tp.reserve(_j)
            for _j in range(TT):
                s_ld.reserve(_j)
            for _j in range(TT):
                s_st.reserve((_j, 0))
                s_st.reserve((_j, 1))
            s_v = en(nc.semaphore("s4_v"))
            s_sc = en(nc.semaphore("s4_sc"))
            s_fx = en(nc.semaphore("s4_fx"))

            with nc.Block() as blk:

                @blk.sync
                def _(sy):
                    for j in range(TT):
                        if j >= 2:
                            sy.wait_ge(s_v, 8 * (j - 1))
                            s_st.wait(sy, (j - 2, 0))
                            s_st.wait(sy, (j - 2, 1))
                        s_ld.inc(
                            sy.dma_start(
                                out=xb2[:, (j % 2) * D : (j % 2 + 1) * D],
                                in_=x_in[j * 128 : (j + 1) * 128, :],
                            ),
                            j,
                        )
                        if j >= 1:
                            sy.wait_ge(s_v, 8 * (j - 1) + 4)
                            s_st.inc(
                                sy.dma_start(
                                    out=x2_d[(j - 1) * 128 : j * 128, :],
                                    in_=x2b[:, ((j - 1) % 2) * D : ((j - 1) % 2 + 1) * D],
                                ),
                                (j - 1, 0),
                            )
                            sy.wait_ge(s_v, 8 * j)
                            s_st.inc(
                                sy.dma_start(
                                    out=xln3_tok[(j - 1) * 128 : j * 128, :],
                                    in_=x3b[:, ((j - 1) % 2) * D : ((j - 1) % 2 + 1) * D],
                                ),
                                (j - 1, 1),
                            )
                    sy.wait_ge(s_v, 8 * (TT - 1) + 4)
                    s_st.inc(
                        sy.dma_start(
                            out=x2_d[(TT - 1) * 128 : TT * 128, :],
                            in_=x2b[:, ((TT - 1) % 2) * D : ((TT - 1) % 2 + 1) * D],
                        ),
                        (TT - 1, 0),
                    )
                    sy.wait_ge(s_v, 8 * TT)
                    s_st.inc(
                        sy.dma_start(
                            out=xln3_tok[(TT - 1) * 128 : TT * 128, :],
                            in_=x3b[:, ((TT - 1) % 2) * D : ((TT - 1) % 2 + 1) * D],
                        ),
                        (TT - 1, 1),
                    )
                    s_st.wait_all(sy)

                @blk.scalar
                def _(s):
                    s_tp.inc(
                        s.dma_start_transpose(
                            aot[:, 0:D], ao_bf[:, 0:128]
                        ),
                        0,
                    )
                    s_tp.inc(
                        s.dma_start_transpose(
                            aot[:, D : 2 * D], ao_bf[:, 128:256]
                        ),
                        1,
                    )
                    for j in range(TT):
                        c = j * 8
                        a_ = aot[:, (j % 2) * D : (j % 2 + 1) * D]
                        x2_ = x2b[:, (j % 2) * D : (j % 2 + 1) * D]
                        s_tp.wait(s, j)
                        s.activation(out=dump[:, :], in_=a_, func=AF.Square,
                                     accum_out=stt[:, c + 1 : c + 2]).then_inc(s_sc, 1)
                        s.wait_ge(s_sc, 4 * j + 1)
                        s.wait_ge(s_v, 8 * j + 2)
                        s.activation(out=stt[:, c + 4 : c + 5],
                                     in_=stt[:, c + 1 : c + 2], func=AF.Sqrt,
                                     scale=1.0 / D,
                                     bias=stt[:, c + 3 : c + 4]).then_inc(s_sc, 1)
                        s.wait_ge(s_v, 8 * j + 4)
                        s.activation(out=dump[:, :], in_=x2_, func=AF.Square,
                                     accum_out=stt[:, c + 6 : c + 7]).then_inc(s_sc, 1)
                        s.wait_ge(s_sc, 4 * j + 3)
                        s.wait_ge(s_v, 8 * j + 6)
                        s.activation(out=stt[:, c + 4 : c + 5],
                                     in_=stt[:, c + 6 : c + 7], func=AF.Sqrt,
                                     scale=1.0 / D,
                                     bias=stt[:, c + 3 : c + 4]).then_inc(s_sc, 1)
                        if j + 2 < TT:
                            s_tp.inc(
                                s.dma_start_transpose(
                                    aot[:, (j % 2) * D : (j % 2 + 1) * D],
                                    ao_bf[:, (j + 2) * 128 : (j + 3) * 128],
                                ),
                                j + 2,
                            )

                @blk.vector
                def _(v):
                    fx = 0
                    for j in range(TT):
                        c = j * 8
                        a_ = aot[:, (j % 2) * D : (j % 2 + 1) * D]
                        x_ = xb2[:, (j % 2) * D : (j % 2 + 1) * D]
                        x2_ = x2b[:, (j % 2) * D : (j % 2 + 1) * D]
                        m = stt[:, c + 2 : c + 3]
                        s_tp.wait(v, j)
                        v.tensor_reduce(stt[:, c : c + 1], a_, AX.X,
                                        ALU.add).then_inc(s_fx, 1)
                        fx += 1
                        v.wait_ge(s_fx, fx)
                        v.tensor_scalar(m, stt[:, c : c + 1], 1.0 / D, None,
                                        ALU.mult).then_inc(s_v, 1)
                        v.wait_ge(s_v, 8 * j + 1)
                        v.tensor_mul(stt[:, c + 3 : c + 4], m, m).then_inc(s_fx, 1)
                        fx += 1
                        v.wait_ge(s_fx, fx)
                        v.tensor_scalar(stt[:, c + 3 : c + 4],
                                        stt[:, c + 3 : c + 4], -1.0, EPS,
                                        ALU.mult, ALU.add).then_inc(s_v, 1)
                        v.wait_ge(s_sc, 4 * j + 2)
                        v.reciprocal(stt[:, c + 5 : c + 6],
                                     stt[:, c + 4 : c + 5]).then_inc(s_fx, 1)
                        fx += 1
                        v.wait_ge(s_fx, fx)
                        s_ld.wait(v, j)
                        v.tensor_scalar(x2_, a_, m, stt[:, c + 5 : c + 6],
                                        ALU.subtract, ALU.mult).then_inc(s_v, 1)
                        v.wait_ge(s_v, 8 * j + 3)
                        v.tensor_add(x2_, x2_, x_).then_inc(s_v, 1)
                        v.wait_ge(s_v, 8 * j + 4)
                        v.tensor_reduce(stt[:, c : c + 1], x2_, AX.X,
                                        ALU.add).then_inc(s_fx, 1)
                        fx += 1
                        v.wait_ge(s_fx, fx)
                        v.tensor_scalar(m, stt[:, c : c + 1], 1.0 / D, None,
                                        ALU.mult).then_inc(s_v, 1)
                        v.wait_ge(s_v, 8 * j + 5)
                        v.tensor_mul(stt[:, c + 3 : c + 4], m, m).then_inc(s_fx, 1)
                        fx += 1
                        v.wait_ge(s_fx, fx)
                        v.tensor_scalar(stt[:, c + 3 : c + 4],
                                        stt[:, c + 3 : c + 4], -1.0, EPS,
                                        ALU.mult, ALU.add).then_inc(s_v, 1)
                        v.wait_ge(s_sc, 4 * (j + 1))
                        v.reciprocal(stt[:, c + 5 : c + 6],
                                     stt[:, c + 4 : c + 5]).then_inc(s_fx, 1)
                        fx += 1
                        v.wait_ge(s_fx, fx)
                        v.tensor_scalar(x3b[:, (j % 2) * D : (j % 2 + 1) * D],
                                        x2_, m, stt[:, c + 5 : c + 6],
                                        ALU.subtract, ALU.mult).then_inc(s_v, 2)

        with contextlib.ExitStack() as sc:
            e5 = sc.enter_context
            x3T = e5(nc.sbuf_tensor("x3T", [128, KT * TC], BF16))
            wgf = e5(nc.sbuf_tensor("wgf", [128, KT * E], F32))
            wgs = e5(nc.sbuf_tensor("wgs", [128, KT * E], BF16))
            iden = e5(nc.sbuf_tensor("iden", [128, 128], F32))
            icol = e5(nc.sbuf_tensor("icol", [128, 128], F32))
            ipart = e5(nc.sbuf_tensor("ipart", [128, 1], F32))
            i8 = e5(nc.sbuf_tensor("i8", [128, E], F32))
            ohm = e5(nc.sbuf_tensor("ohm", [128, E * 128], BF16))
            pbc = e5(nc.sbuf_tensor("pbc", [128, 128], F32))
            lgt = e5(nc.sbuf_tensor("lgt", [128, E], F32))
            top8 = e5(nc.sbuf_tensor("top8", [128, 8], F32))
            idx8 = e5(nc.sbuf_tensor("idx8", [128, 8], U32))
            idxf = e5(nc.sbuf_tensor("idxf", [128, 8], F32))
            oh1 = e5(nc.sbuf_tensor("oh1", [128, E], F32))
            oh2 = e5(nc.sbuf_tensor("oh2", [128, E], F32))
            comb = e5(nc.sbuf_tensor("comb", [128, E], F32))
            comb_cm = e5(nc.sbuf_tensor("comb_cm", [128, TC], BF16))
            cbb = e5(nc.sbuf_tensor("cbb", [128, TC], BF16))
            sgb = e5(nc.sbuf_tensor("sgb", [128, TC], BF16))
            ucb = e5(nc.sbuf_tensor("ucb", [128, BLK], F32))
            actb = e5(nc.sbuf_tensor("actb", [128, 8 * TC], BF16))
            wgu_s = e5(nc.sbuf_tensor("wgu_s", [128, 4 * KT * 128], BF16))
            wdn_s = e5(nc.sbuf_tensor("wdn_s", [128, 4 * 8 * 128], BF16))
            racc = e5(nc.sbuf_tensor("racc", [128, KT * TC], BF16))
            rp_stack = contextlib.ExitStack()
            ps_r = rp_stack.enter_context(nc.psum_tensor("ps_r", [128, E], F32))
            ps_t = rp_stack.enter_context(nc.psum_tensor("ps_t", [128, 128], F32))
            s_tp = en(nc.semaphore("s5_tp"))
            s_wg = en(nc.semaphore("s5_wg"))
            s_w4 = PS(en, nc, "s5_w", ways=4)
            s_mm = en(nc.semaphore("s5_mm"))
            s_sil = en(nc.semaphore("s5_sil"))
            s_act = en(nc.semaphore("s5_act"))
            s_eo = en(nc.semaphore("s5_eo"))
            s_rt = en(nc.semaphore("s5_rt"))
            s_sg = en(nc.semaphore("s5_sg"))
            s_cb = en(nc.semaphore("s5_cb"))
            s_fx8 = en(nc.semaphore("s5_fx8"))
            s_ux = en(nc.semaphore("s5_ux"))
            uxv = [0]

            mmv = [0]
            silv = [0]
            actv = [0]
            eov = [0]
            cbv = [0]
            stage_id = [0]

            with nc.Block() as blk:

                @blk.scalar
                def _(s):
                    for ct in range(KT):
                        s.dma_start_transpose(
                            x3T[:, ct * TC : (ct + 1) * TC],
                            xln3_tok[:, ct * 128 : (ct + 1) * 128],
                        ).then_inc(s_tp, 16)

                @blk.sync
                def _(sy):
                    for k in range(KT):
                        sy.dma_start(out=wgf[:, k * E : (k + 1) * E],
                                     in_=wg_in[k * 128 : (k + 1) * 128, :]).then_inc(
                            s_wg, 16)

                @blk.gpsimd
                def _(g):
                    g.iota(i8[:, :], [[1, E]], base=0, channel_multiplier=0,
                           allow_small_or_imprecise_dtypes=True)
                    g.iota(icol[:, :], [[1, 128]], base=0, channel_multiplier=0,
                           allow_small_or_imprecise_dtypes=True)
                    g.iota(ipart[:, :], [[0, 1]], base=0, channel_multiplier=1,
                           allow_small_or_imprecise_dtypes=True).then_inc(s_rt, 1)

                @blk.vector
                def _(v):
                    v.wait_ge(s_rt, 1)
                    v.tensor_single_scalar(iden[:, :], icol[:, :], ipart[:, 0:1],
                                           ALU.is_equal)
                    v.tensor_scalar_mul(pbc[:, :], icol[:, :], 0.0).then_inc(s_fx8, 1)
                    v.wait_ge(s_fx8, 1)
                    v.tensor_scalar_add(pbc[:, :], pbc[:, :],
                                        ipart[:, 0:1]).then_inc(s_fx8, 1)
                    v.wait_ge(s_fx8, 2)
                    for e in range(E):
                        v.tensor_single_scalar(ohm[:, e * 128 : (e + 1) * 128],
                                               pbc[:, :], float(e), ALU.is_equal)
                    v.memset(racc[:, :], 0.0)
                    v.wait_ge(s_wg, 16 * KT)
                    v.tensor_copy(wgs[:, :], wgf[:, :]).then_inc(s_rt, 1)

            MASKS = 2

            with nc.Block() as blk:
                mm0 = mmv[0]

                @blk.tensor
                def _(t):
                    t.wait_ge(s_tp, 16 * KT)
                    t.wait_ge(s_rt, MASKS)
                    for j in range(TT):
                        if j >= 1:
                            t.wait_ge(s_rt, MASKS + 10 * (j - 1) + 1)
                        for k in range(KT):
                            inst = t.matmul(
                                ps_r[:, :],
                                x3T[:, k * TC + j * 128 : k * TC + (j + 1) * 128],
                                wgs[:, k * E : (k + 1) * E],
                                start=(k == 0), stop=(k == KT - 1),
                            )
                        inst.then_inc(s_mm, 1)
                        t.wait_ge(s_rt, MASKS + 10 * j + 9)
                        t.transpose(ps_t[0:8, 0:128], comb[:, :],
                                    iden[:, :]).then_inc(s_mm, 1)

                @blk.vector
                def _(v):
                    for j in range(TT):
                        base = MASKS + 10 * j
                        v.wait_ge(s_mm, mm0 + 2 * j + 1)
                        v.tensor_copy(lgt[:, :], ps_r[:, :]).then_inc(s_rt, 1)
                        if j >= 1:
                            v.wait_ge(s_sg, j)
                        v.wait_ge(s_rt, base + 1)
                        v.max(top8[:, :], lgt[:, :]).then_inc(s_rt, 1)
                        v.wait_ge(s_rt, base + 2)
                        v.max_index(idx8[:, :], top8[:, :], lgt[:, :]).then_inc(s_rt, 1)
                        v.wait_ge(s_rt, base + 3)
                        v.tensor_copy(idxf[:, :], idx8[:, :]).then_inc(s_rt, 1)
                        v.tensor_sub(stt[:, 0:1], top8[:, 0:1],
                                     top8[:, 1:2]).then_inc(s_rt, 1)
                        v.wait_ge(s_sg, j + 1)
                        v.wait_ge(s_rt, base + 4)
                        v.tensor_single_scalar(oh1[:, :], i8[:, :], idxf[:, 0:1],
                                               ALU.is_equal).then_inc(s_rt, 1)
                        v.tensor_single_scalar(oh2[:, :], i8[:, :], idxf[:, 1:2],
                                               ALU.is_equal).then_inc(s_rt, 1)
                        v.wait_ge(s_rt, base + 7)
                        v.tensor_scalar_mul(comb[:, :], oh1[:, :],
                                            stt[:, 2:3]).then_inc(s_rt, 1)
                        v.wait_ge(s_rt, base + 8)
                        v.scalar_tensor_tensor(comb[:, :], oh2[:, :], stt[:, 3:4],
                                               comb[:, :], ALU.mult,
                                               ALU.add).then_inc(s_rt, 1)
                        v.wait_ge(s_mm, mm0 + 2 * j + 2)
                        v.tensor_copy(comb_cm[0:8, j * 128 : (j + 1) * 128],
                                      ps_t[0:8, 0:128]).then_inc(s_rt, 1)

                @blk.scalar
                def _(s):
                    for j in range(TT):
                        base = MASKS + 10 * j
                        s.wait_ge(s_rt, base + 5)
                        s.activation(out=stt[:, 2:3], in_=stt[:, 0:1],
                                     func=AF.Sigmoid, scale=1.0)
                        s.activation(out=stt[:, 3:4], in_=stt[:, 0:1],
                                     func=AF.Sigmoid, scale=-1.0).then_inc(s_sg, 1)

                mmv[0] = mm0 + 2 * TT

            RT_DONE = MASKS + 10 * TT
            rp_stack.close()
            ps_q = [e5(nc.psum_tensor(f"ps_q{i}", [128, BLK], F32))
                    for i in range(2 * NB)]

            def gu_stage(is_shared, e_or_ic):
                with nc.Block() as blk:
                    mm0 = mmv[0]
                    sil0 = silv[0]
                    act0 = actv[0]
                    cb_need = cbv[0]
                    sid = stage_id[0]
                    stage_id[0] += 1
                    for _li in range(16):
                        s_w4.reserve((sid, _li))

                    @blk.gpsimd
                    def _(g):
                        if mm0:
                            g.wait_ge(s_mm, mm0)
                        for fi in range(8):
                            for half in range(2):
                                li = fi * 2 + half
                                if li >= 4:
                                    g.wait_ge(s_mm, mm0 + (li - 3))
                                if is_shared:
                                    ft = e_or_ic * 8 + fi if half == 0 else 32 + e_or_ic * 8 + fi
                                    src = wsg_in.rearrange("(k p) f -> p k f", p=128)[
                                        :, :, ft * 128 : (ft + 1) * 128]
                                else:
                                    ft = fi if half == 0 else 8 + fi
                                    src = wgu_in[e_or_ic].rearrange(
                                        "(k p) f -> p k f", p=128)[
                                        :, :, ft * 128 : (ft + 1) * 128]
                                s_w4.inc(
                                    g.dma_start(
                                        out=wgu_s[:, (li % 4) * KT * 128 : (li % 4 + 1) * KT * 128],
                                        in_=src,
                                    ),
                                    (sid, li),
                                )

                    @blk.tensor
                    def _(t):
                        if not is_shared:
                            t.wait_ge(s_cb, cb_need)
                        for li in range(16):
                            s_w4.wait(t, (sid, li))
                            q = (li % 2) * NB
                            if li >= 2:
                                pl = li - 2
                                if pl % 2 == 0:
                                    t.wait_ge(s_sil, sil0 + (pl // 2) * NB + NB)
                                else:
                                    t.wait_ge(s_act, act0 + (pl // 2) * NB + NB)
                            for k in range(KT):
                                for tb in range(NB):
                                    inst = t.matmul(
                                        ps_q[q + tb][:, :],
                                        wgu_s[:, (li % 4) * KT * 128 + k * 128 : (li % 4) * KT * 128 + (k + 1) * 128],
                                        x3T[:, k * TC + tb * BLK : k * TC + (tb + 1) * BLK],
                                        start=(k == 0), stop=(k == KT - 1),
                                    )
                            inst.then_inc(s_mm, 1)

                    @blk.scalar
                    def _(s):
                        for fi in range(8):
                            li = fi * 2
                            q = (li % 2) * NB
                            s.wait_ge(s_mm, mm0 + li + 1)
                            for tb in range(NB):
                                if fi >= 1:
                                    s.wait_ge(s_act, act0 + (fi - 1) * NB + tb + 1)
                                s.activation(
                                    out=sgb[:, tb * BLK : (tb + 1) * BLK],
                                    in_=ps_q[q + tb][:, :], func=AF.Silu,
                                ).then_inc(s_sil, 1)

                    @blk.vector
                    def _(v):
                        for fi in range(8):
                            li = fi * 2 + 1
                            q = (li % 2) * NB
                            v.wait_ge(s_mm, mm0 + li + 1)
                            for tb in range(NB):
                                if is_shared:
                                    inst = v.tensor_copy(ucb[:, :], ps_q[q + tb][:, :])
                                else:
                                    inst = v.tensor_tensor(
                                        ucb[:, :], ps_q[q + tb][:, :],
                                        cbb[:, tb * BLK : (tb + 1) * BLK], ALU.mult)
                                inst.then_inc(s_ux, 1)
                                uxv[0] += 1
                                v.wait_ge(s_ux, uxv[0])
                                v.wait_ge(s_sil, sil0 + fi * NB + tb + 1)
                                v.tensor_tensor(
                                    actb[:, fi * TC + tb * BLK : fi * TC + (tb + 1) * BLK],
                                    sgb[:, tb * BLK : (tb + 1) * BLK], ucb[:, :],
                                    ALU.mult,
                                ).then_inc(s_act, 1)

                    mmv[0] = mm0 + 16
                    silv[0] = sil0 + 8 * NB
                    actv[0] = act0 + 8 * NB

            def down_stage(is_shared, e_or_ic):
                with nc.Block() as blk:
                    mm0 = mmv[0]
                    eo0 = eov[0]
                    sid = stage_id[0]
                    stage_id[0] += 1
                    for _ht in range(KT):
                        s_w4.reserve((sid, _ht))

                    @blk.gpsimd
                    def _(g):
                        g.wait_ge(s_mm, mm0)
                        for ht in range(KT):
                            if ht >= 4:
                                g.wait_ge(s_mm, mm0 + (ht - 3))
                            if is_shared:
                                src = wsd_in[e_or_ic * IM : (e_or_ic + 1) * IM, :].rearrange(
                                    "(k p) h -> p k h", p=128)[:, :, ht * 128 : (ht + 1) * 128]
                            else:
                                src = wdn_in[e_or_ic].rearrange("(k p) h -> p k h", p=128)[
                                    :, :, ht * 128 : (ht + 1) * 128]
                            s_w4.inc(
                                g.dma_start(
                                    out=wdn_s[:, (ht % 4) * 8 * 128 : (ht % 4 + 1) * 8 * 128],
                                    in_=src,
                                ),
                                (sid, ht),
                            )

                    @blk.tensor
                    def _(t):
                        for ht in range(KT):
                            s_w4.wait(t, (sid, ht))
                            q = (ht % 2) * NB
                            if ht >= 2:
                                t.wait_ge(s_eo, eo0 + (ht - 1) * NB)
                            for ik in range(8):
                                for tb in range(NB):
                                    inst = t.matmul(
                                        ps_q[q + tb][:, :],
                                        wdn_s[:, (ht % 4) * 8 * 128 + ik * 128 : (ht % 4) * 8 * 128 + (ik + 1) * 128],
                                        actb[:, ik * TC + tb * BLK : ik * TC + (tb + 1) * BLK],
                                        start=(ik == 0), stop=(ik == 7),
                                    )
                            inst.then_inc(s_mm, 1)

                    @blk.vector
                    def _(v):
                        for ht in range(KT):
                            q = (ht % 2) * NB
                            v.wait_ge(s_mm, mm0 + ht + 1)
                            for tb in range(NB):
                                sl = racc[:, ht * TC + tb * BLK : ht * TC + (tb + 1) * BLK]
                                v.tensor_tensor(sl, sl, ps_q[q + tb][:, :],
                                                ALU.add).then_inc(s_eo, 1)

                    mmv[0] = mm0 + KT
                    eov[0] = eo0 + KT * NB

            for e in range(E):
                with nc.Block() as blk:
                    mm0 = mmv[0]
                    cb0 = cbv[0]

                    @blk.tensor
                    def _(t):
                        t.wait_ge(s_rt, RT_DONE)
                        for tb in range(NB):
                            if tb >= 1:
                                t.wait_ge(s_cb, cb0 + tb)
                            t.matmul(ps_q[0][:, :], ohm[0:8, e * 128 : (e + 1) * 128],
                                     comb_cm[0:8, tb * BLK : (tb + 1) * BLK],
                                     start=True, stop=True).then_inc(s_mm, 1)

                    @blk.vector
                    def _(v):
                        for tb in range(NB):
                            v.wait_ge(s_mm, mm0 + tb + 1)
                            v.tensor_copy(cbb[:, tb * BLK : (tb + 1) * BLK],
                                          ps_q[0][:, :]).then_inc(s_cb, 1)

                    mmv[0] = mm0 + NB
                    cbv[0] = cb0 + NB

                gu_stage(False, e)
                down_stage(False, e)

            for ic in range(4):
                gu_stage(True, ic)
                down_stage(True, ic)

            with nc.Block() as blk:

                @blk.sync
                def _(sy):
                    for ct in range(KT):
                        sy.dma_start(
                            out=ffn_bf[ct * 128 : (ct + 1) * 128, :],
                            in_=racc[:, ct * TC : (ct + 1) * TC],
                        ).then_inc(s_wg, 16)
                    sy.wait_ge(s_wg, 16 * KT + 16 * KT)

        with contextlib.ExitStack() as sc:
            e7 = sc.enter_context
            fnt = e7(nc.sbuf_tensor("fnt", [128, 2 * D], BF16))
            dump = e7(nc.sbuf_tensor("dump7", [128, D], F32))
            x2l = e7(nc.sbuf_tensor("x2l", [128, 2 * D], F32))
            yb = e7(nc.sbuf_tensor("yb", [128, 2 * D], F32))
            s_tp = PS(en, nc, "s7_tp")
            s_ld = PS(en, nc, "s7_ld")
            s_st = PS(en, nc, "s7_st")
            for _j in range(TT):
                s_tp.reserve(_j)
            for _j in range(TT):
                s_ld.reserve(_j)
            for _j in range(TT):
                s_st.reserve(_j)
            s_v = en(nc.semaphore("s7_v"))
            s_sc = en(nc.semaphore("s7_sc"))
            s_fx = en(nc.semaphore("s7_fx"))

            with nc.Block() as blk:

                @blk.sync
                def _(sy):
                    for j in range(TT):
                        if j >= 2:
                            sy.wait_ge(s_v, 4 * (j - 1))
                        s_ld.inc(
                            sy.dma_start(
                                out=x2l[:, (j % 2) * D : (j % 2 + 1) * D],
                                in_=x2_d[j * 128 : (j + 1) * 128, :],
                            ),
                            j,
                        )
                        if j >= 1:
                            sy.wait_ge(s_v, 4 * j)
                            s_st.inc(
                                sy.dma_start(
                                    out=y_out[(j - 1) * 128 : j * 128, :],
                                    in_=yb[:, ((j - 1) % 2) * D : ((j - 1) % 2 + 1) * D],
                                ),
                                j - 1,
                            )
                    sy.wait_ge(s_v, 4 * TT)
                    s_st.inc(
                        sy.dma_start(
                            out=y_out[(TT - 1) * 128 : TT * 128, :],
                            in_=yb[:, ((TT - 1) % 2) * D : ((TT - 1) % 2 + 1) * D],
                        ),
                        TT - 1,
                    )
                    s_st.wait_all(sy)

                @blk.scalar
                def _(s):
                    s_tp.inc(
                        s.dma_start_transpose(fnt[:, 0:D], ffn_bf[:, 0:128]), 0
                    )
                    s_tp.inc(
                        s.dma_start_transpose(fnt[:, D : 2 * D], ffn_bf[:, 128:256]), 1
                    )
                    for j in range(TT):
                        c = j * 8
                        f_ = fnt[:, (j % 2) * D : (j % 2 + 1) * D]
                        s_tp.wait(s, j)
                        s.activation(out=dump[:, :], in_=f_, func=AF.Square,
                                     accum_out=stt[:, c + 1 : c + 2]).then_inc(s_sc, 1)
                        s.wait_ge(s_sc, 2 * j + 1)
                        s.wait_ge(s_v, 4 * j + 2)
                        s.activation(out=stt[:, c + 4 : c + 5],
                                     in_=stt[:, c + 1 : c + 2], func=AF.Sqrt,
                                     scale=1.0 / D,
                                     bias=stt[:, c + 3 : c + 4]).then_inc(s_sc, 1)
                        if j + 2 < TT:
                            s.wait_ge(s_v, 4 * (j + 1))
                            s_tp.inc(
                                s.dma_start_transpose(
                                    fnt[:, (j % 2) * D : (j % 2 + 1) * D],
                                    ffn_bf[:, (j + 2) * 128 : (j + 3) * 128],
                                ),
                                j + 2,
                            )

                @blk.vector
                def _(v):
                    fx = 0
                    for j in range(TT):
                        c = j * 8
                        f_ = fnt[:, (j % 2) * D : (j % 2 + 1) * D]
                        x2_ = x2l[:, (j % 2) * D : (j % 2 + 1) * D]
                        y_ = yb[:, (j % 2) * D : (j % 2 + 1) * D]
                        m = stt[:, c + 2 : c + 3]
                        s_tp.wait(v, j)
                        v.tensor_reduce(stt[:, c : c + 1], f_, AX.X,
                                        ALU.add).then_inc(s_fx, 1)
                        fx += 1
                        v.wait_ge(s_fx, fx)
                        v.tensor_scalar(m, stt[:, c : c + 1], 1.0 / D, None,
                                        ALU.mult).then_inc(s_v, 1)
                        v.wait_ge(s_v, 4 * j + 1)
                        v.tensor_mul(stt[:, c + 3 : c + 4], m, m).then_inc(s_fx, 1)
                        fx += 1
                        v.wait_ge(s_fx, fx)
                        v.tensor_scalar(stt[:, c + 3 : c + 4],
                                        stt[:, c + 3 : c + 4], -1.0, EPS,
                                        ALU.mult, ALU.add).then_inc(s_v, 1)
                        v.wait_ge(s_sc, 2 * (j + 1))
                        v.reciprocal(stt[:, c + 5 : c + 6],
                                     stt[:, c + 4 : c + 5]).then_inc(s_fx, 1)
                        fx += 1
                        v.wait_ge(s_fx, fx)
                        s_ld.wait(v, j)
                        if j >= 2:
                            s_st.wait(v, j - 2)
                        v.tensor_scalar(y_, f_, m, stt[:, c + 5 : c + 6],
                                        ALU.subtract, ALU.mult).then_inc(s_v, 1)
                        v.wait_ge(s_v, 4 * j + 3)
                        v.tensor_add(y_, y_, x2_).then_inc(s_v, 1)


    if debug:
        with nc.semaphore("s_dbg") as s_dbg, nc.Block() as blk:

            @blk.gpsimd
            def _(g):
                n = 0
                for name, (o, s) in dbg.items():
                    g.dma_start(out=o[:, :], in_=s[:, :]).then_inc(s_dbg, 16)
                    n += 16
                g.wait_ge(s_dbg, n)

    return nc


_NC_CACHE = None


def kernel(**inputs):
    global _NC_CACHE
    x = np.ascontiguousarray(np.asarray(inputs["L0_attn_x"], np.float32))
    wnames = {
        "A": "L0_attn_A", "B": "L0_attn_B", "C_w": "L0_attn_C_w",
        "C_b": "L0_attn_C_b", "D_w": "L0_attn_D_w", "D_b": "L0_attn_D_b",
        "log_delta": "L0_attn_log_delta", "w_gate": "L0_ffn_w_gate",
        "w_gate_up": "L0_ffn_w_gate_up", "w_down": "L0_ffn_w_down",
        "w_sh_gate_up": "L0_ffn_w_sh_gate_up", "w_sh_down": "L0_ffn_w_sh_down",
    }
    weights = {k: np.ascontiguousarray(np.asarray(inputs[v], np.float32))
               for k, v in wnames.items()}
    weights["C_w"] = np.ascontiguousarray(weights["C_w"].T)
    weights["D_w"] = np.ascontiguousarray(weights["D_w"].T)

    if _NC_CACHE is None:
        _NC_CACHE = build_nc()
    nc = _NC_CACHE

    in_maps = []
    for c in range(8):
        b, q = c // 4, c % 4
        shard = np.ascontiguousarray(x[b, q * TC : (q + 1) * TC, :])
        sel = np.zeros((3, 8), np.float32)
        for s_ in range(3):
            r = q - 1 - s_
            if r >= 0:
                sel[s_, 4 * b + r] = 1.0
        m = {"x": shard, "sel": sel}
        m.update(weights)
        in_maps.append(m)

    res = run_bass_kernel_spmd(nc, in_maps, core_ids=list(range(8)))
    out = np.empty((2, 8192, D), np.float32)
    for c in range(8):
        b, q = c // 4, c % 4
        out[b, q * TC : (q + 1) * TC, :] = res.results[c]["y"]
    return out


if __name__ == "__main__":
    nc = build_nc()
    print("built ok")


# version 11 (final)
# speedup vs baseline: 1.2843x; 1.0403x over previous
import contextlib
import numpy as np
import concourse.bass as bass
import concourse.mybir as mybir
from concourse.bass_utils import run_bass_kernel_spmd

F32 = mybir.dt.float32
BF16 = mybir.dt.bfloat16
U32 = mybir.dt.uint32
AF = mybir.ActivationFunctionType
ALU = mybir.AluOpType
AX = mybir.AxisListType

D = 2048
TC = 2048
KT = 16
TT = 16
NB = 4
BLK = 512
E = 8
IM = 1024
ISH = 4096
EPS = 1e-5


class PS:

    def __init__(self, en, nc, name, ways=2):
        self.s = [en(nc.semaphore(f"{name}_{i}")) for i in range(ways)]
        self.c = [0] * ways
        self.ways = ways
        self.n = 0
        self.mark = {}

    def reserve(self, key):
        assert key not in self.mark
        w = self.n % self.ways
        self.n += 1
        self.c[w] += 16
        self.mark[key] = (w, self.c[w])

    def inc(self, inst, key):
        w, _ = self.mark[key]
        inst.then_inc(self.s[w], 16)

    def wait(self, eng, key):
        w, v = self.mark[key]
        eng.wait_ge(self.s[w], v)

    def wait_all(self, eng):
        for w in range(self.ways):
            if self.c[w]:
                eng.wait_ge(self.s[w], self.c[w])


def build_nc(tc=2048, blk=512, debug=False):
    global TC, TT, NB, BLK
    TC, BLK = tc, blk
    TT, NB = TC // 128, TC // BLK

    nc = bass.Bass("TRN2", target_bir_lowering=False)

    x_in = nc.dram_tensor("x", [TC, D], F32, kind="ExternalInput").ap()
    sel_in = nc.dram_tensor("sel", [3, 8], F32, kind="ExternalInput").ap()
    A_in = nc.dram_tensor("A", [D, 64], F32, kind="ExternalInput").ap()
    B_in = nc.dram_tensor("B", [D, 64], F32, kind="ExternalInput").ap()
    Cw_in = nc.dram_tensor("C_w", [D, D], F32, kind="ExternalInput").ap()
    Cb_in = nc.dram_tensor("C_b", [D], F32, kind="ExternalInput").ap()
    Dw_in = nc.dram_tensor("D_w", [D, D], F32, kind="ExternalInput").ap()
    Db_in = nc.dram_tensor("D_b", [D], F32, kind="ExternalInput").ap()
    ld_in = nc.dram_tensor("log_delta", [D], F32, kind="ExternalInput").ap()
    wg_in = nc.dram_tensor("w_gate", [D, E], F32, kind="ExternalInput").ap()
    wgu_in = nc.dram_tensor("w_gate_up", [E, D, 2 * IM], F32, kind="ExternalInput").ap()
    wdn_in = nc.dram_tensor("w_down", [E, IM, D], F32, kind="ExternalInput").ap()
    wsg_in = nc.dram_tensor("w_sh_gate_up", [D, 2 * ISH], F32, kind="ExternalInput").ap()
    wsd_in = nc.dram_tensor("w_sh_down", [ISH, D], F32, kind="ExternalInput").ap()
    y_out = nc.dram_tensor("y", [TC, D], F32, kind="ExternalOutput").ap()

    xln_tok = nc.dram_tensor("xln_tok", [TC, D], BF16).ap()
    xproj_d = nc.dram_tensor("xproj_d", [D, TC], F32).ap()
    ao_bf = nc.dram_tensor("ao_bf", [D, TC], BF16).ap()
    x2_d = nc.dram_tensor("x2_d", [TC, D], F32).ap()
    xln3_tok = nc.dram_tensor("xln3_tok", [TC, D], BF16).ap()
    ffn_bf = nc.dram_tensor("ffn_bf", [D, TC], BF16).ap()
    li_b = nc.dram_tensor("lasts_in", [128, KT], F32).ap()
    lo_b = nc.dram_tensor("lasts_out", [8 * 128, KT], F32).ap()
    if debug:
        dbg = {
            "d_xln": (nc.dram_tensor("d_xln", [TC, D], F32, kind="ExternalOutput").ap(), xln_tok),
            "d_xp": (nc.dram_tensor("d_xp", [D, TC], F32, kind="ExternalOutput").ap(), xproj_d),
            "d_ao": (nc.dram_tensor("d_ao", [D, TC], F32, kind="ExternalOutput").ap(), ao_bf),
            "d_x2": (nc.dram_tensor("d_x2", [TC, D], F32, kind="ExternalOutput").ap(), x2_d),
            "d_x3": (nc.dram_tensor("d_x3", [TC, D], F32, kind="ExternalOutput").ap(), xln3_tok),
            "d_ffn": (nc.dram_tensor("d_ffn", [D, TC], F32, kind="ExternalOutput").ap(), ffn_bf),
        }

    stk = contextlib.ExitStack()
    with stk:
        en = stk.enter_context

        adiag = en(nc.sbuf_tensor("adiag", [128, KT], F32))
        a2048 = en(nc.sbuf_tensor("a2048", [128, KT], F32))
        asq = en(nc.sbuf_tensor("asq", [128, KT], F32))
        ab = en(nc.sbuf_tensor("ab", [128, KT], F32))
        cb_s = en(nc.sbuf_tensor("cb_s", [128, KT], F32))
        db_s = en(nc.sbuf_tensor("db_s", [128, KT], F32))
        ldl = en(nc.sbuf_tensor("ldl", [128, KT], F32))
        abt = en(nc.sbuf_tensor("abt", [128, 2 * 64], F32))
        bbt = en(nc.sbuf_tensor("bbt", [128, 2 * 64], F32))
        lasts = en(nc.sbuf_tensor("lasts", [128, KT], F32))
        lall = en(nc.sbuf_tensor("lall", [128, 8 * KT], F32))
        gcar = en(nc.sbuf_tensor("gcar", [128, KT], F32))
        pred = en(nc.sbuf_tensor("pred", [128, 3 * KT], F32))
        selr = en(nc.sbuf_tensor("selr", [128, 24], F32))
        selb = en(nc.sbuf_tensor("selb", [128, 24], F32))
        onesr = en(nc.sbuf_tensor("onesr", [128, 128], F32))
        stt = en(nc.sbuf_tensor("stt", [128, 16 * 8], F32))

        s_p0 = en(nc.semaphore("s_p0"))
        s_ab0 = en(nc.semaphore("s_ab0"))
        s_ab1 = en(nc.semaphore("s_ab1"))
        s_abp = [s_ab0, s_ab1]
        s_pre = en(nc.semaphore("s_pre"))
        s_sel = en(nc.semaphore("s_sel"))
        s_fx0 = en(nc.semaphore("s_fx0"))

        P0_PARAMS = 49 * 16

        def p0_bodies(blk):

            @blk.sync
            def _(sy):
                for j in range(KT):
                    sl = slice(j * 128, (j + 1) * 128)
                    o1 = lambda t: t.rearrange("(p o) -> p o", o=1)
                    sy.dma_start(out=ldl[:, j : j + 1], in_=o1(ld_in[sl])).then_inc(s_p0, 16)
                    sy.dma_start(out=cb_s[:, j : j + 1], in_=o1(Cb_in[sl])).then_inc(s_p0, 16)
                    sy.dma_start(out=db_s[:, j : j + 1], in_=o1(Db_in[sl])).then_inc(s_p0, 16)
                sy.dma_start(
                    out=selr[0:1, 0:24],
                    in_=sel_in.rearrange("a b -> (a b)").rearrange("(o f) -> o f", o=1),
                ).then_inc(s_p0, 16)
                for j in range(KT):
                    if j >= 2:
                        sy.wait_ge(s_pre, 3 + j - 1)
                    p = j % 2
                    sy.dma_start(out=abt[:, p * 64 : (p + 1) * 64], in_=A_in[j * 128 : (j + 1) * 128, :]).then_inc(s_abp[p], 16)
                    sy.dma_start(out=bbt[:, p * 64 : (p + 1) * 64], in_=B_in[j * 128 : (j + 1) * 128, :]).then_inc(s_abp[p], 16)

            @blk.scalar
            def _(s):
                s.wait_ge(s_p0, P0_PARAMS)
                s.activation(out=adiag[:, :], in_=ldl[:, :], func=AF.Sigmoid,
                             scale=-1.0).then_inc(s_pre, 1)

            @blk.vector
            def _(v):
                fx0 = [0]

                def edge(inst):
                    inst.then_inc(s_fx0, 1)
                    fx0[0] += 1
                    v.wait_ge(s_fx0, fx0[0])

                v.wait_ge(s_pre, 1)
                edge(v.tensor_mul(asq[:, :], adiag[:, :], adiag[:, :]))
                cur, oth = asq, a2048
                for _ in range(TC.bit_length() - 2):
                    edge(v.tensor_mul(oth[:, :], cur[:, :], cur[:, :]))
                    cur, oth = oth, cur
                if cur is not a2048:
                    edge(v.tensor_copy(a2048[:, :], cur[:, :]))
                v.memset(onesr[0:1, :], 1.0).then_inc(s_pre, 2)
                for j in range(KT):
                    p = j % 2
                    v.wait_ge(s_abp[p], 32 * (j // 2 + 1))
                    edge(v.tensor_mul(bbt[:, p * 64 : (p + 1) * 64],
                                      abt[:, p * 64 : (p + 1) * 64],
                                      bbt[:, p * 64 : (p + 1) * 64]))
                    v.tensor_reduce(ab[:, j : j + 1],
                                    bbt[:, p * 64 : (p + 1) * 64], AX.X,
                                    ALU.add).then_inc(s_pre, 1)

            @blk.tensor
            def _(t):
                t.wait_ge(s_pre, 3)
                t.wait_ge(s_p0, P0_PARAMS)
                t.matmul(pss[:, 0:24], onesr[0:1, :], selr[0:1, 0:24],
                         start=True, stop=True).then_inc(s_sel, 1)

        with contextlib.ExitStack() as sc:
            e1 = sc.enter_context
            xb = e1(nc.sbuf_tensor("xb", [128, 2 * D], F32))
            xlnb = e1(nc.sbuf_tensor("xlnb", [128, 2 * D], BF16))
            dump = e1(nc.sbuf_tensor("dump1", [128, D], F32))
            pss = e1(nc.psum_tensor("pss", [128, 32], F32))
            s_ld = PS(en, nc, "s1_ld")
            s_st = PS(en, nc, "s1_st")
            for _j in range(TT):
                s_ld.reserve(_j)
            for _j in range(TT):
                s_st.reserve(_j)
            s_v = en(nc.semaphore("s1_v"))
            s_sc = en(nc.semaphore("s1_sc"))
            s_fx = en(nc.semaphore("s1_fx"))

            with nc.Block() as blk:
                p0_bodies(blk)

                @blk.sync
                def _(sy):
                    for j in range(TT):
                        if j >= 2:
                            sy.wait_ge(s_v, 4 * (j - 1))
                            sy.wait_ge(s_sc, 2 * (j - 1))
                            s_st.wait(sy, j - 2)
                        s_ld.inc(
                            sy.dma_start(
                                out=xb[:, (j % 2) * D : (j % 2 + 1) * D],
                                in_=x_in[j * 128 : (j + 1) * 128, :],
                            ),
                            j,
                        )
                        if j >= 1:
                            sy.wait_ge(s_v, 4 * j)
                            s_st.inc(
                                sy.dma_start(
                                    out=xln_tok[(j - 1) * 128 : j * 128, :],
                                    in_=xlnb[:, ((j - 1) % 2) * D : ((j - 1) % 2 + 1) * D],
                                ),
                                j - 1,
                            )
                    sy.wait_ge(s_v, 4 * TT)
                    s_st.inc(
                        sy.dma_start(
                            out=xln_tok[(TT - 1) * 128 : TT * 128, :],
                            in_=xlnb[:, ((TT - 1) % 2) * D : ((TT - 1) % 2 + 1) * D],
                        ),
                        TT - 1,
                    )
                    s_st.wait_all(sy)

                @blk.scalar
                def _(s):
                    for j in range(TT):
                        xj = xb[:, (j % 2) * D : (j % 2 + 1) * D]
                        c = j * 8
                        s_ld.wait(s, j)
                        s.activation(out=dump[:, :], in_=xj, func=AF.Square,
                                     accum_out=stt[:, c + 1 : c + 2]).then_inc(s_sc, 1)
                        s.wait_ge(s_sc, 2 * j + 1)
                        s.wait_ge(s_v, 4 * j + 2)
                        s.activation(out=stt[:, c + 4 : c + 5],
                                     in_=stt[:, c + 1 : c + 2], func=AF.Sqrt,
                                     scale=1.0 / D,
                                     bias=stt[:, c + 3 : c + 4]).then_inc(s_sc, 1)

                @blk.vector
                def _(v):
                    fx = 0
                    for j in range(TT):
                        xj = xb[:, (j % 2) * D : (j % 2 + 1) * D]
                        c = j * 8
                        m = stt[:, c + 2 : c + 3]
                        s_ld.wait(v, j)
                        v.tensor_reduce(stt[:, c : c + 1], xj, AX.X,
                                        ALU.add).then_inc(s_fx, 1)
                        fx += 1
                        v.wait_ge(s_fx, fx)
                        v.tensor_scalar(m, stt[:, c : c + 1], 1.0 / D, None,
                                        ALU.mult).then_inc(s_v, 1)
                        v.wait_ge(s_v, 4 * j + 1)
                        v.tensor_mul(stt[:, c + 3 : c + 4], m, m).then_inc(s_fx, 1)
                        fx += 1
                        v.wait_ge(s_fx, fx)
                        v.tensor_scalar(stt[:, c + 3 : c + 4],
                                        stt[:, c + 3 : c + 4], -1.0, EPS,
                                        ALU.mult, ALU.add).then_inc(s_v, 1)
                        v.wait_ge(s_sc, 2 * (j + 1))
                        v.reciprocal(stt[:, c + 5 : c + 6],
                                     stt[:, c + 4 : c + 5]).then_inc(s_fx, 1)
                        fx += 1
                        v.wait_ge(s_fx, fx)
                        v.tensor_scalar(xlnb[:, (j % 2) * D : (j % 2 + 1) * D],
                                        xj, m, stt[:, c + 5 : c + 6],
                                        ALU.subtract, ALU.mult).then_inc(s_v, 2)

        
                @blk.vector
                def _(v):
                    v.wait_ge(s_sel, 1)
                    v.tensor_copy(selb[:, :], pss[:, 0:24]).then_inc(s_sel, 1)

        with contextlib.ExitStack() as sc:
            e2 = sc.enter_context
            xpsb = e2(nc.sbuf_tensor("xpsb", [128, KT * TC], BF16))
            xlnT = e2(nc.sbuf_tensor("xlnT", [128, KT * TC], BF16))
            cwT = e2(nc.sbuf_tensor("cwT", [128, KT * 512], BF16))
            dwT = e2(nc.sbuf_tensor("dwT", [128, KT * 512], BF16))
            sdb = e2(nc.sbuf_tensor("sdb", [128, NB * BLK], F32))
            gs = xpsb
            aob = e2(nc.sbuf_tensor("aob", [128, 2 * TC], BF16))
            ztile = e2(nc.sbuf_tensor("ztile", [128, TC], F32))
            zcol = e2(nc.sbuf_tensor("zcol", [128, 1], F32))
            psC = [e2(nc.psum_tensor(f"psC{i}", [128, BLK], F32)) for i in range(NB)]
            psD = [e2(nc.psum_tensor(f"psD{i}", [128, BLK], F32)) for i in range(NB)]
            s_tp = en(nc.semaphore("s2_tp"))
            s_w = en(nc.semaphore("s2_w"))
            s_mm = en(nc.semaphore("s2_mm"))
            s_sg = en(nc.semaphore("s2_sg"))
            s_ev = en(nc.semaphore("s2_ev"))
            s_gx = en(nc.semaphore("s3_gx"))
            s_v = en(nc.semaphore("s3_v"))
            s_cc = en(nc.semaphore("s3_cc"))
            s_fx = en(nc.semaphore("s3_fx"))
            s_st = PS(en, nc, "s3_st")
            for _ct in range(KT):
                s_st.reserve(_ct)
            NG = 16
            QG = 4

            with nc.Block() as blk:

                @blk.gpsimd
                def _(g):
                    for qh in range(4):
                        if qh >= 1:
                            g.wait_ge(s_mm, QG * qh)
                        for k in range(KT):
                            g.dma_start(
                                out=cwT[:, k * 512 : (k + 1) * 512],
                                in_=Cw_in[k * 128 : (k + 1) * 128,
                                          qh * 512 : (qh + 1) * 512],
                            ).then_inc(s_w, 16)
                            g.dma_start(
                                out=dwT[:, k * 512 : (k + 1) * 512],
                                in_=Dw_in[k * 128 : (k + 1) * 128,
                                          qh * 512 : (qh + 1) * 512],
                            ).then_inc(s_w, 16)
                    g.wait_ge(s_gx, 16)
                    g.collective_compute(
                        "AllGather", ALU.bypass,
                        replica_groups=[list(range(8))],
                        ins=[li_b.opt()], outs=[lo_b.opt()],
                    ).then_inc(s_cc, 1)

                @blk.scalar
                def _(s):
                    for ct in range(KT):
                        s.dma_start_transpose(
                            xlnT[:, ct * TC : (ct + 1) * TC],
                            xln_tok[:, ct * 128 : (ct + 1) * 128],
                        ).then_inc(s_tp, 16)
                    for g_ in range(NG):
                        oc = g_
                        s.wait_ge(s_mm, g_ + 1)
                        for tb in range(NB):
                            if g_ >= 1:
                                s.wait_ge(s_ev, NB * g_)
                            s.activation(
                                out=sdb[:, tb * BLK : (tb + 1) * BLK],
                                in_=psD[tb][:, :], func=AF.Sigmoid,
                                bias=db_s[:, oc : oc + 1], scale=1.0,
                            ).then_inc(s_sg, 1)

                @blk.tensor
                def _(t):
                    t.wait_ge(s_tp, 16 * KT)
                    for qh in range(4):
                        t.wait_ge(s_w, 32 * 16 * (qh + 1))
                        for j4 in range(QG):
                            g_ = qh * QG + j4
                            if g_ >= 1:
                                t.wait_ge(s_ev, NB * g_)
                            for k in range(KT):
                                for tb in range(NB):
                                    t.matmul(
                                        psC[tb][:, :],
                                        cwT[:, k * 512 + j4 * 128 : k * 512 + (j4 + 1) * 128],
                                        xlnT[:, k * TC + tb * BLK : k * TC + (tb + 1) * BLK],
                                        start=(k == 0), stop=(k == KT - 1),
                                    )
                            for k in range(KT):
                                for tb in range(NB):
                                    inst = t.matmul(
                                        psD[tb][:, :],
                                        dwT[:, k * 512 + j4 * 128 : k * 512 + (j4 + 1) * 128],
                                        xlnT[:, k * TC + tb * BLK : k * TC + (tb + 1) * BLK],
                                        start=(k == 0), stop=(k == KT - 1),
                                    )
                            inst.then_inc(s_mm, 1)

                @blk.vector
                def _(v):
                    fx = 0
                    v.memset(zcol[:, :], 0.0)
                    for g_ in range(NG):
                        oc = g_
                        for tb in range(NB):
                            i = g_ * NB + tb
                            v.wait_ge(s_sg, i + 1)
                            v.scalar_tensor_tensor(
                                xpsb[:, g_ * TC + tb * BLK : g_ * TC + (tb + 1) * BLK],
                                psC[tb][:, :], cb_s[:, oc : oc + 1],
                                sdb[:, tb * BLK : (tb + 1) * BLK],
                                ALU.add, ALU.mult,
                            ).then_inc(s_ev, 1)
                        v.wait_ge(s_ev, NB * (g_ + 1))
                        v.tensor_tensor_scan(
                            gs[:, g_ * TC : (g_ + 1) * TC],
                            adiag[:, g_ : g_ + 1].broadcast_to([128, TC]),
                            xpsb[:, g_ * TC : (g_ + 1) * TC],
                            0.0, ALU.mult, ALU.add,
                        ).then_inc(s_v, 1)
                        v.wait_ge(s_v, 2 * g_ + 1)
                        v.tensor_copy(
                            lasts[:, g_ : g_ + 1],
                            gs[:, g_ * TC + TC - 1 : g_ * TC + TC],
                        ).then_inc(s_v, 1)

                @blk.vector
                def _(v):
                    fx6 = [0]

                    def edge(inst):
                        inst.then_inc(s_fx, 1)
                        fx6[0] += 1
                        v.wait_ge(s_fx, fx6[0])

                    v.wait_ge(s_gx, 16 * 9)
                    for s_ in range(3):
                        edge(v.tensor_scalar_mul(pred[:, s_ * KT : (s_ + 1) * KT],
                                                 lall[:, 0:KT],
                                                 selb[:, s_ * 8 : s_ * 8 + 1]))
                        for r in range(1, 8):
                            edge(v.scalar_tensor_tensor(
                                pred[:, s_ * KT : (s_ + 1) * KT],
                                lall[:, r * KT : (r + 1) * KT],
                                selb[:, s_ * 8 + r : s_ * 8 + r + 1],
                                pred[:, s_ * KT : (s_ + 1) * KT],
                                ALU.mult, ALU.add,
                            ))
                    edge(v.tensor_mul(gcar[:, :], a2048[:, :],
                                      pred[:, 2 * KT : 3 * KT]))
                    edge(v.tensor_add(gcar[:, :], gcar[:, :],
                                      pred[:, KT : 2 * KT]))
                    edge(v.tensor_mul(gcar[:, :], a2048[:, :], gcar[:, :]))
                    v.tensor_add(gcar[:, :], gcar[:, :],
                                 pred[:, 0:KT]).then_inc(s_v, 1)
                    v.wait_ge(s_v, 2 * KT + 1)
                    for ct in range(KT):
                        if ct >= 2:
                            s_st.wait(v, ct - 2)
                        edge(v.tensor_tensor_scan(
                            ztile[:, :],
                            adiag[:, ct : ct + 1].broadcast_to([128, TC]),
                            zcol[:, 0:1].broadcast_to([128, TC]),
                            gcar[:, ct : ct + 1], ALU.mult, ALU.add,
                        ))
                        edge(v.tensor_add(ztile[:, :],
                                          gs[:, ct * TC : (ct + 1) * TC],
                                          ztile[:, :]))
                        v.tensor_scalar_mul(
                            aob[:, (ct % 2) * TC : (ct % 2 + 1) * TC],
                            ztile[:, :], ab[:, ct : ct + 1],
                        ).then_inc(s_v, 1)

                @blk.sync
                def _(sy):
                    sy.wait_ge(s_v, 2 * KT)
                    sy.dma_start(out=li_b[:, :], in_=lasts[:, :]).then_inc(s_gx, 16)
                    sy.wait_ge(s_cc, 1)
                    for r in range(8):
                        sy.dma_start(
                            out=lall[:, r * KT : (r + 1) * KT],
                            in_=lo_b[r * 128 : (r + 1) * 128, :],
                        ).then_inc(s_gx, 16)
                    for ct in range(KT):
                        sy.wait_ge(s_v, 2 * KT + 1 + ct + 1)
                        s_st.inc(
                            sy.dma_start(
                                out=ao_bf[ct * 128 : (ct + 1) * 128, :],
                                in_=aob[:, (ct % 2) * TC : (ct % 2 + 1) * TC],
                            ),
                            ct,
                        )
                    s_st.wait_all(sy)

        with contextlib.ExitStack() as sc:
            e4 = sc.enter_context
            aot = e4(nc.sbuf_tensor("aot", [128, 2 * D], BF16))
            dump = e4(nc.sbuf_tensor("dump4", [128, D], F32))
            xb2 = e4(nc.sbuf_tensor("xb2", [128, 2 * D], F32))
            x2b = e4(nc.sbuf_tensor("x2b", [128, 2 * D], F32))
            x3b = e4(nc.sbuf_tensor("x3b", [128, 2 * D], BF16))
            s_tp = PS(en, nc, "s4_tp")
            s_ld = PS(en, nc, "s4_ld")
            s_st = PS(en, nc, "s4_st", ways=4)
            for _j in range(TT):
                s_tp.reserve(_j)
            for _j in range(TT):
                s_ld.reserve(_j)
            for _j in range(TT):
                s_st.reserve((_j, 0))
                s_st.reserve((_j, 1))
            s_v = en(nc.semaphore("s4_v"))
            s_sc = en(nc.semaphore("s4_sc"))
            s_fx = en(nc.semaphore("s4_fx"))

            with nc.Block() as blk:

                @blk.sync
                def _(sy):
                    for j in range(TT):
                        if j >= 2:
                            sy.wait_ge(s_v, 8 * (j - 1))
                            s_st.wait(sy, (j - 2, 0))
                            s_st.wait(sy, (j - 2, 1))
                        s_ld.inc(
                            sy.dma_start(
                                out=xb2[:, (j % 2) * D : (j % 2 + 1) * D],
                                in_=x_in[j * 128 : (j + 1) * 128, :],
                            ),
                            j,
                        )
                        if j >= 1:
                            sy.wait_ge(s_v, 8 * (j - 1) + 4)
                            s_st.inc(
                                sy.dma_start(
                                    out=x2_d[(j - 1) * 128 : j * 128, :],
                                    in_=x2b[:, ((j - 1) % 2) * D : ((j - 1) % 2 + 1) * D],
                                ),
                                (j - 1, 0),
                            )
                            sy.wait_ge(s_v, 8 * j)
                            s_st.inc(
                                sy.dma_start(
                                    out=xln3_tok[(j - 1) * 128 : j * 128, :],
                                    in_=x3b[:, ((j - 1) % 2) * D : ((j - 1) % 2 + 1) * D],
                                ),
                                (j - 1, 1),
                            )
                    sy.wait_ge(s_v, 8 * (TT - 1) + 4)
                    s_st.inc(
                        sy.dma_start(
                            out=x2_d[(TT - 1) * 128 : TT * 128, :],
                            in_=x2b[:, ((TT - 1) % 2) * D : ((TT - 1) % 2 + 1) * D],
                        ),
                        (TT - 1, 0),
                    )
                    sy.wait_ge(s_v, 8 * TT)
                    s_st.inc(
                        sy.dma_start(
                            out=xln3_tok[(TT - 1) * 128 : TT * 128, :],
                            in_=x3b[:, ((TT - 1) % 2) * D : ((TT - 1) % 2 + 1) * D],
                        ),
                        (TT - 1, 1),
                    )
                    s_st.wait_all(sy)

                @blk.scalar
                def _(s):
                    s_tp.inc(
                        s.dma_start_transpose(
                            aot[:, 0:D], ao_bf[:, 0:128]
                        ),
                        0,
                    )
                    s_tp.inc(
                        s.dma_start_transpose(
                            aot[:, D : 2 * D], ao_bf[:, 128:256]
                        ),
                        1,
                    )
                    for j in range(TT):
                        c = j * 8
                        a_ = aot[:, (j % 2) * D : (j % 2 + 1) * D]
                        x2_ = x2b[:, (j % 2) * D : (j % 2 + 1) * D]
                        s_tp.wait(s, j)
                        s.activation(out=dump[:, :], in_=a_, func=AF.Square,
                                     accum_out=stt[:, c + 1 : c + 2]).then_inc(s_sc, 1)
                        s.wait_ge(s_sc, 4 * j + 1)
                        s.wait_ge(s_v, 8 * j + 2)
                        s.activation(out=stt[:, c + 4 : c + 5],
                                     in_=stt[:, c + 1 : c + 2], func=AF.Sqrt,
                                     scale=1.0 / D,
                                     bias=stt[:, c + 3 : c + 4]).then_inc(s_sc, 1)
                        s.wait_ge(s_v, 8 * j + 4)
                        s.activation(out=dump[:, :], in_=x2_, func=AF.Square,
                                     accum_out=stt[:, c + 6 : c + 7]).then_inc(s_sc, 1)
                        s.wait_ge(s_sc, 4 * j + 3)
                        s.wait_ge(s_v, 8 * j + 6)
                        s.activation(out=stt[:, c + 4 : c + 5],
                                     in_=stt[:, c + 6 : c + 7], func=AF.Sqrt,
                                     scale=1.0 / D,
                                     bias=stt[:, c + 3 : c + 4]).then_inc(s_sc, 1)
                        if j + 2 < TT:
                            s_tp.inc(
                                s.dma_start_transpose(
                                    aot[:, (j % 2) * D : (j % 2 + 1) * D],
                                    ao_bf[:, (j + 2) * 128 : (j + 3) * 128],
                                ),
                                j + 2,
                            )

                @blk.vector
                def _(v):
                    fx = 0
                    for j in range(TT):
                        c = j * 8
                        a_ = aot[:, (j % 2) * D : (j % 2 + 1) * D]
                        x_ = xb2[:, (j % 2) * D : (j % 2 + 1) * D]
                        x2_ = x2b[:, (j % 2) * D : (j % 2 + 1) * D]
                        m = stt[:, c + 2 : c + 3]
                        s_tp.wait(v, j)
                        v.tensor_reduce(stt[:, c : c + 1], a_, AX.X,
                                        ALU.add).then_inc(s_fx, 1)
                        fx += 1
                        v.wait_ge(s_fx, fx)
                        v.tensor_scalar(m, stt[:, c : c + 1], 1.0 / D, None,
                                        ALU.mult).then_inc(s_v, 1)
                        v.wait_ge(s_v, 8 * j + 1)
                        v.tensor_mul(stt[:, c + 3 : c + 4], m, m).then_inc(s_fx, 1)
                        fx += 1
                        v.wait_ge(s_fx, fx)
                        v.tensor_scalar(stt[:, c + 3 : c + 4],
                                        stt[:, c + 3 : c + 4], -1.0, EPS,
                                        ALU.mult, ALU.add).then_inc(s_v, 1)
                        v.wait_ge(s_sc, 4 * j + 2)
                        v.reciprocal(stt[:, c + 5 : c + 6],
                                     stt[:, c + 4 : c + 5]).then_inc(s_fx, 1)
                        fx += 1
                        v.wait_ge(s_fx, fx)
                        s_ld.wait(v, j)
                        v.tensor_scalar(x2_, a_, m, stt[:, c + 5 : c + 6],
                                        ALU.subtract, ALU.mult).then_inc(s_v, 1)
                        v.wait_ge(s_v, 8 * j + 3)
                        v.tensor_add(x2_, x2_, x_).then_inc(s_v, 1)
                        v.wait_ge(s_v, 8 * j + 4)
                        v.tensor_reduce(stt[:, c : c + 1], x2_, AX.X,
                                        ALU.add).then_inc(s_fx, 1)
                        fx += 1
                        v.wait_ge(s_fx, fx)
                        v.tensor_scalar(m, stt[:, c : c + 1], 1.0 / D, None,
                                        ALU.mult).then_inc(s_v, 1)
                        v.wait_ge(s_v, 8 * j + 5)
                        v.tensor_mul(stt[:, c + 3 : c + 4], m, m).then_inc(s_fx, 1)
                        fx += 1
                        v.wait_ge(s_fx, fx)
                        v.tensor_scalar(stt[:, c + 3 : c + 4],
                                        stt[:, c + 3 : c + 4], -1.0, EPS,
                                        ALU.mult, ALU.add).then_inc(s_v, 1)
                        v.wait_ge(s_sc, 4 * (j + 1))
                        v.reciprocal(stt[:, c + 5 : c + 6],
                                     stt[:, c + 4 : c + 5]).then_inc(s_fx, 1)
                        fx += 1
                        v.wait_ge(s_fx, fx)
                        v.tensor_scalar(x3b[:, (j % 2) * D : (j % 2 + 1) * D],
                                        x2_, m, stt[:, c + 5 : c + 6],
                                        ALU.subtract, ALU.mult).then_inc(s_v, 2)

        with contextlib.ExitStack() as sc:
            e5 = sc.enter_context
            x3T = e5(nc.sbuf_tensor("x3T", [128, KT * TC], BF16))
            wgf = e5(nc.sbuf_tensor("wgf", [128, KT * E], F32))
            wgs = e5(nc.sbuf_tensor("wgs", [128, KT * E], BF16))
            iden = e5(nc.sbuf_tensor("iden", [128, 128], F32))
            icol = e5(nc.sbuf_tensor("icol", [128, 128], F32))
            ipart = e5(nc.sbuf_tensor("ipart", [128, 1], F32))
            i8 = e5(nc.sbuf_tensor("i8", [128, E], F32))
            ohm = e5(nc.sbuf_tensor("ohm", [128, E * 128], BF16))
            pbc = e5(nc.sbuf_tensor("pbc", [128, 128], F32))
            lgt = e5(nc.sbuf_tensor("lgt", [128, E], F32))
            top8 = e5(nc.sbuf_tensor("top8", [128, 8], F32))
            idx8 = e5(nc.sbuf_tensor("idx8", [128, 8], U32))
            idxf = e5(nc.sbuf_tensor("idxf", [128, 8], F32))
            oh1 = e5(nc.sbuf_tensor("oh1", [128, E], F32))
            oh2 = e5(nc.sbuf_tensor("oh2", [128, E], F32))
            comb = e5(nc.sbuf_tensor("comb", [128, E], F32))
            comb_cm = e5(nc.sbuf_tensor("comb_cm", [128, TC], BF16))
            cbb = e5(nc.sbuf_tensor("cbb", [128, TC], BF16))
            sgb = e5(nc.sbuf_tensor("sgb", [128, TC], BF16))
            ucb = e5(nc.sbuf_tensor("ucb", [128, BLK], F32))
            actb = e5(nc.sbuf_tensor("actb", [128, 8 * TC], BF16))
            wgu_s = e5(nc.sbuf_tensor("wgu_s", [128, 4 * KT * 128], BF16))
            wdn_s = e5(nc.sbuf_tensor("wdn_s", [128, 4 * 8 * 128], BF16))
            racc = e5(nc.sbuf_tensor("racc", [128, KT * TC], BF16))
            rp_stack = contextlib.ExitStack()
            ps_r = rp_stack.enter_context(nc.psum_tensor("ps_r", [128, E], F32))
            ps_t = rp_stack.enter_context(nc.psum_tensor("ps_t", [128, 128], F32))
            s_tp = en(nc.semaphore("s5_tp"))
            s_wg = en(nc.semaphore("s5_wg"))
            s_w4 = PS(en, nc, "s5_w", ways=4)
            s_mm = en(nc.semaphore("s5_mm"))
            s_sil = en(nc.semaphore("s5_sil"))
            s_act = en(nc.semaphore("s5_act"))
            s_eo = en(nc.semaphore("s5_eo"))
            s_rt = en(nc.semaphore("s5_rt"))
            s_sg = en(nc.semaphore("s5_sg"))
            s_cb = en(nc.semaphore("s5_cb"))
            s_fx8 = en(nc.semaphore("s5_fx8"))
            s_ux = en(nc.semaphore("s5_ux"))
            uxv = [0]

            mmv = [0]
            silv = [0]
            actv = [0]
            eov = [0]
            cbv = [0]
            stage_id = [0]

            with nc.Block() as blk:

                @blk.scalar
                def _(s):
                    for ct in range(KT):
                        s.dma_start_transpose(
                            x3T[:, ct * TC : (ct + 1) * TC],
                            xln3_tok[:, ct * 128 : (ct + 1) * 128],
                        ).then_inc(s_tp, 16)

                @blk.sync
                def _(sy):
                    for k in range(KT):
                        sy.dma_start(out=wgf[:, k * E : (k + 1) * E],
                                     in_=wg_in[k * 128 : (k + 1) * 128, :]).then_inc(
                            s_wg, 16)

                @blk.gpsimd
                def _(g):
                    g.iota(i8[:, :], [[1, E]], base=0, channel_multiplier=0,
                           allow_small_or_imprecise_dtypes=True)
                    g.iota(icol[:, :], [[1, 128]], base=0, channel_multiplier=0,
                           allow_small_or_imprecise_dtypes=True)
                    g.iota(ipart[:, :], [[0, 1]], base=0, channel_multiplier=1,
                           allow_small_or_imprecise_dtypes=True).then_inc(s_rt, 1)

                @blk.vector
                def _(v):
                    v.wait_ge(s_rt, 1)
                    v.tensor_single_scalar(iden[:, :], icol[:, :], ipart[:, 0:1],
                                           ALU.is_equal)
                    v.tensor_scalar_mul(pbc[:, :], icol[:, :], 0.0).then_inc(s_fx8, 1)
                    v.wait_ge(s_fx8, 1)
                    v.tensor_scalar_add(pbc[:, :], pbc[:, :],
                                        ipart[:, 0:1]).then_inc(s_fx8, 1)
                    v.wait_ge(s_fx8, 2)
                    for e in range(E):
                        v.tensor_single_scalar(ohm[:, e * 128 : (e + 1) * 128],
                                               pbc[:, :], float(e), ALU.is_equal)
                    v.memset(racc[:, :], 0.0)
                    v.wait_ge(s_wg, 16 * KT)
                    v.tensor_copy(wgs[:, :], wgf[:, :]).then_inc(s_rt, 1)

            MASKS = 2

            with nc.Block() as blk:
                mm0 = mmv[0]

                @blk.tensor
                def _(t):
                    t.wait_ge(s_tp, 16 * KT)
                    t.wait_ge(s_rt, MASKS)
                    for j in range(TT):
                        if j >= 1:
                            t.wait_ge(s_rt, MASKS + 10 * (j - 1) + 1)
                        for k in range(KT):
                            inst = t.matmul(
                                ps_r[:, :],
                                x3T[:, k * TC + j * 128 : k * TC + (j + 1) * 128],
                                wgs[:, k * E : (k + 1) * E],
                                start=(k == 0), stop=(k == KT - 1),
                            )
                        inst.then_inc(s_mm, 1)
                        t.wait_ge(s_rt, MASKS + 10 * j + 9)
                        t.transpose(ps_t[0:8, 0:128], comb[:, :],
                                    iden[:, :]).then_inc(s_mm, 1)

                @blk.vector
                def _(v):
                    for j in range(TT):
                        base = MASKS + 10 * j
                        v.wait_ge(s_mm, mm0 + 2 * j + 1)
                        v.tensor_copy(lgt[:, :], ps_r[:, :]).then_inc(s_rt, 1)
                        if j >= 1:
                            v.wait_ge(s_sg, j)
                        v.wait_ge(s_rt, base + 1)
                        v.max(top8[:, :], lgt[:, :]).then_inc(s_rt, 1)
                        v.wait_ge(s_rt, base + 2)
                        v.max_index(idx8[:, :], top8[:, :], lgt[:, :]).then_inc(s_rt, 1)
                        v.wait_ge(s_rt, base + 3)
                        v.tensor_copy(idxf[:, :], idx8[:, :]).then_inc(s_rt, 1)
                        v.tensor_sub(stt[:, 0:1], top8[:, 0:1],
                                     top8[:, 1:2]).then_inc(s_rt, 1)
                        v.wait_ge(s_sg, j + 1)
                        v.wait_ge(s_rt, base + 4)
                        v.tensor_single_scalar(oh1[:, :], i8[:, :], idxf[:, 0:1],
                                               ALU.is_equal).then_inc(s_rt, 1)
                        v.tensor_single_scalar(oh2[:, :], i8[:, :], idxf[:, 1:2],
                                               ALU.is_equal).then_inc(s_rt, 1)
                        v.wait_ge(s_rt, base + 7)
                        v.tensor_scalar_mul(comb[:, :], oh1[:, :],
                                            stt[:, 2:3]).then_inc(s_rt, 1)
                        v.wait_ge(s_rt, base + 8)
                        v.scalar_tensor_tensor(comb[:, :], oh2[:, :], stt[:, 3:4],
                                               comb[:, :], ALU.mult,
                                               ALU.add).then_inc(s_rt, 1)
                        v.wait_ge(s_mm, mm0 + 2 * j + 2)
                        v.tensor_copy(comb_cm[0:8, j * 128 : (j + 1) * 128],
                                      ps_t[0:8, 0:128]).then_inc(s_rt, 1)

                @blk.scalar
                def _(s):
                    for j in range(TT):
                        base = MASKS + 10 * j
                        s.wait_ge(s_rt, base + 5)
                        s.activation(out=stt[:, 2:3], in_=stt[:, 0:1],
                                     func=AF.Sigmoid, scale=1.0)
                        s.activation(out=stt[:, 3:4], in_=stt[:, 0:1],
                                     func=AF.Sigmoid, scale=-1.0).then_inc(s_sg, 1)

                mmv[0] = mm0 + 2 * TT

            RT_DONE = MASKS + 10 * TT
            rp_stack.close()
            ps_q = [e5(nc.psum_tensor(f"ps_q{i}", [128, BLK], F32))
                    for i in range(2 * NB)]

            def gu_stage(blk, is_shared, e_or_ic):
                if True:
                    mm0 = mmv[0]
                    sil0 = silv[0]
                    act0 = actv[0]
                    cb_need = cbv[0]
                    sid = stage_id[0]
                    stage_id[0] += 1
                    for _li in range(16):
                        s_w4.reserve((sid, _li))

                    @blk.gpsimd
                    def _(g):
                        if mm0:
                            g.wait_ge(s_mm, mm0)
                        for fi in range(8):
                            for half in range(2):
                                li = fi * 2 + half
                                if li >= 4:
                                    g.wait_ge(s_mm, mm0 + (li - 3))
                                if is_shared:
                                    ft = e_or_ic * 8 + fi if half == 0 else 32 + e_or_ic * 8 + fi
                                    src = wsg_in.rearrange("(k p) f -> p k f", p=128)[
                                        :, :, ft * 128 : (ft + 1) * 128]
                                else:
                                    ft = fi if half == 0 else 8 + fi
                                    src = wgu_in[e_or_ic].rearrange(
                                        "(k p) f -> p k f", p=128)[
                                        :, :, ft * 128 : (ft + 1) * 128]
                                s_w4.inc(
                                    g.dma_start(
                                        out=wgu_s[:, (li % 4) * KT * 128 : (li % 4 + 1) * KT * 128],
                                        in_=src,
                                    ),
                                    (sid, li),
                                )

                    @blk.tensor
                    def _(t):
                        if not is_shared:
                            t.wait_ge(s_cb, cb_need)
                        if eov[0]:
                            t.wait_ge(s_eo, eov[0])
                        for li in range(16):
                            s_w4.wait(t, (sid, li))
                            q = (li % 2) * NB
                            if li >= 2:
                                pl = li - 2
                                if pl % 2 == 0:
                                    t.wait_ge(s_sil, sil0 + (pl // 2) * NB + NB)
                                else:
                                    t.wait_ge(s_act, act0 + (pl // 2) * NB + NB)
                            for k in range(KT):
                                for tb in range(NB):
                                    inst = t.matmul(
                                        ps_q[q + tb][:, :],
                                        wgu_s[:, (li % 4) * KT * 128 + k * 128 : (li % 4) * KT * 128 + (k + 1) * 128],
                                        x3T[:, k * TC + tb * BLK : k * TC + (tb + 1) * BLK],
                                        start=(k == 0), stop=(k == KT - 1),
                                    )
                            inst.then_inc(s_mm, 1)

                    @blk.scalar
                    def _(s):
                        for fi in range(8):
                            li = fi * 2
                            q = (li % 2) * NB
                            s.wait_ge(s_mm, mm0 + li + 1)
                            for tb in range(NB):
                                if fi >= 1:
                                    s.wait_ge(s_act, act0 + (fi - 1) * NB + tb + 1)
                                s.activation(
                                    out=sgb[:, tb * BLK : (tb + 1) * BLK],
                                    in_=ps_q[q + tb][:, :], func=AF.Silu,
                                ).then_inc(s_sil, 1)

                    @blk.vector
                    def _(v):
                        for fi in range(8):
                            li = fi * 2 + 1
                            q = (li % 2) * NB
                            v.wait_ge(s_mm, mm0 + li + 1)
                            for tb in range(NB):
                                if is_shared:
                                    inst = v.tensor_copy(ucb[:, :], ps_q[q + tb][:, :])
                                else:
                                    inst = v.tensor_tensor(
                                        ucb[:, :], ps_q[q + tb][:, :],
                                        cbb[:, tb * BLK : (tb + 1) * BLK], ALU.mult)
                                inst.then_inc(s_ux, 1)
                                uxv[0] += 1
                                v.wait_ge(s_ux, uxv[0])
                                v.wait_ge(s_sil, sil0 + fi * NB + tb + 1)
                                v.tensor_tensor(
                                    actb[:, fi * TC + tb * BLK : fi * TC + (tb + 1) * BLK],
                                    sgb[:, tb * BLK : (tb + 1) * BLK], ucb[:, :],
                                    ALU.mult,
                                ).then_inc(s_act, 1)

                    mmv[0] = mm0 + 16
                    silv[0] = sil0 + 8 * NB
                    actv[0] = act0 + 8 * NB

            def down_stage(blk, is_shared, e_or_ic):
                if True:
                    mm0 = mmv[0]
                    eo0 = eov[0]
                    sid = stage_id[0]
                    stage_id[0] += 1
                    for _ht in range(KT):
                        s_w4.reserve((sid, _ht))

                    @blk.gpsimd
                    def _(g):
                        g.wait_ge(s_mm, mm0)
                        for ht in range(KT):
                            if ht >= 4:
                                g.wait_ge(s_mm, mm0 + (ht - 3))
                            if is_shared:
                                src = wsd_in[e_or_ic * IM : (e_or_ic + 1) * IM, :].rearrange(
                                    "(k p) h -> p k h", p=128)[:, :, ht * 128 : (ht + 1) * 128]
                            else:
                                src = wdn_in[e_or_ic].rearrange("(k p) h -> p k h", p=128)[
                                    :, :, ht * 128 : (ht + 1) * 128]
                            s_w4.inc(
                                g.dma_start(
                                    out=wdn_s[:, (ht % 4) * 8 * 128 : (ht % 4 + 1) * 8 * 128],
                                    in_=src,
                                ),
                                (sid, ht),
                            )

                    @blk.tensor
                    def _(t):
                        t.wait_ge(s_sil, silv[0])
                        t.wait_ge(s_act, actv[0])
                        for ht in range(KT):
                            s_w4.wait(t, (sid, ht))
                            q = (ht % 2) * NB
                            if ht >= 2:
                                t.wait_ge(s_eo, eo0 + (ht - 1) * NB)
                            for ik in range(8):
                                for tb in range(NB):
                                    inst = t.matmul(
                                        ps_q[q + tb][:, :],
                                        wdn_s[:, (ht % 4) * 8 * 128 + ik * 128 : (ht % 4) * 8 * 128 + (ik + 1) * 128],
                                        actb[:, ik * TC + tb * BLK : ik * TC + (tb + 1) * BLK],
                                        start=(ik == 0), stop=(ik == 7),
                                    )
                            inst.then_inc(s_mm, 1)

                    @blk.vector
                    def _(v):
                        for ht in range(KT):
                            q = (ht % 2) * NB
                            v.wait_ge(s_mm, mm0 + ht + 1)
                            for tb in range(NB):
                                sl = racc[:, ht * TC + tb * BLK : ht * TC + (tb + 1) * BLK]
                                v.tensor_tensor(sl, sl, ps_q[q + tb][:, :],
                                                ALU.add).then_inc(s_eo, 1)

                    mmv[0] = mm0 + KT
                    eov[0] = eo0 + KT * NB

            with nc.Block() as ffn_blk:
                for e in range(E):
                    mm0 = mmv[0]
                    cb0 = cbv[0]
                    eo_now = eov[0]

                    @ffn_blk.tensor
                    def _(t, e=e, mm0=mm0, cb0=cb0, eo_now=eo_now):
                        t.wait_ge(s_rt, RT_DONE)
                        if eo_now:
                            t.wait_ge(s_eo, eo_now)
                        for tb in range(NB):
                            if tb >= 1:
                                t.wait_ge(s_cb, cb0 + tb)
                            t.matmul(ps_q[0][:, :], ohm[0:8, e * 128 : (e + 1) * 128],
                                     comb_cm[0:8, tb * BLK : (tb + 1) * BLK],
                                     start=True, stop=True).then_inc(s_mm, 1)

                    @ffn_blk.vector
                    def _(v, mm0=mm0):
                        for tb in range(NB):
                            v.wait_ge(s_mm, mm0 + tb + 1)
                            v.tensor_copy(cbb[:, tb * BLK : (tb + 1) * BLK],
                                          ps_q[0][:, :]).then_inc(s_cb, 1)

                    mmv[0] = mm0 + NB
                    cbv[0] = cb0 + NB

                    gu_stage(ffn_blk, False, e)
                    down_stage(ffn_blk, False, e)

                for ic in range(4):
                    gu_stage(ffn_blk, True, ic)
                    down_stage(ffn_blk, True, ic)

            with nc.Block() as blk:

                @blk.sync
                def _(sy):
                    for ct in range(KT):
                        sy.dma_start(
                            out=ffn_bf[ct * 128 : (ct + 1) * 128, :],
                            in_=racc[:, ct * TC : (ct + 1) * TC],
                        ).then_inc(s_wg, 16)
                    sy.wait_ge(s_wg, 16 * KT + 16 * KT)

        with contextlib.ExitStack() as sc:
            e7 = sc.enter_context
            fnt = e7(nc.sbuf_tensor("fnt", [128, 2 * D], BF16))
            dump = e7(nc.sbuf_tensor("dump7", [128, D], F32))
            x2l = e7(nc.sbuf_tensor("x2l", [128, 2 * D], F32))
            yb = e7(nc.sbuf_tensor("yb", [128, 2 * D], F32))
            s_tp = PS(en, nc, "s7_tp")
            s_ld = PS(en, nc, "s7_ld")
            s_st = PS(en, nc, "s7_st")
            for _j in range(TT):
                s_tp.reserve(_j)
            for _j in range(TT):
                s_ld.reserve(_j)
            for _j in range(TT):
                s_st.reserve(_j)
            s_v = en(nc.semaphore("s7_v"))
            s_sc = en(nc.semaphore("s7_sc"))
            s_fx = en(nc.semaphore("s7_fx"))

            with nc.Block() as blk:

                @blk.sync
                def _(sy):
                    for j in range(TT):
                        if j >= 2:
                            sy.wait_ge(s_v, 4 * (j - 1))
                        s_ld.inc(
                            sy.dma_start(
                                out=x2l[:, (j % 2) * D : (j % 2 + 1) * D],
                                in_=x2_d[j * 128 : (j + 1) * 128, :],
                            ),
                            j,
                        )
                        if j >= 1:
                            sy.wait_ge(s_v, 4 * j)
                            s_st.inc(
                                sy.dma_start(
                                    out=y_out[(j - 1) * 128 : j * 128, :],
                                    in_=yb[:, ((j - 1) % 2) * D : ((j - 1) % 2 + 1) * D],
                                ),
                                j - 1,
                            )
                    sy.wait_ge(s_v, 4 * TT)
                    s_st.inc(
                        sy.dma_start(
                            out=y_out[(TT - 1) * 128 : TT * 128, :],
                            in_=yb[:, ((TT - 1) % 2) * D : ((TT - 1) % 2 + 1) * D],
                        ),
                        TT - 1,
                    )
                    s_st.wait_all(sy)

                @blk.scalar
                def _(s):
                    s_tp.inc(
                        s.dma_start_transpose(fnt[:, 0:D], ffn_bf[:, 0:128]), 0
                    )
                    s_tp.inc(
                        s.dma_start_transpose(fnt[:, D : 2 * D], ffn_bf[:, 128:256]), 1
                    )
                    for j in range(TT):
                        c = j * 8
                        f_ = fnt[:, (j % 2) * D : (j % 2 + 1) * D]
                        s_tp.wait(s, j)
                        s.activation(out=dump[:, :], in_=f_, func=AF.Square,
                                     accum_out=stt[:, c + 1 : c + 2]).then_inc(s_sc, 1)
                        s.wait_ge(s_sc, 2 * j + 1)
                        s.wait_ge(s_v, 4 * j + 2)
                        s.activation(out=stt[:, c + 4 : c + 5],
                                     in_=stt[:, c + 1 : c + 2], func=AF.Sqrt,
                                     scale=1.0 / D,
                                     bias=stt[:, c + 3 : c + 4]).then_inc(s_sc, 1)
                        if j + 2 < TT:
                            s.wait_ge(s_v, 4 * (j + 1))
                            s_tp.inc(
                                s.dma_start_transpose(
                                    fnt[:, (j % 2) * D : (j % 2 + 1) * D],
                                    ffn_bf[:, (j + 2) * 128 : (j + 3) * 128],
                                ),
                                j + 2,
                            )

                @blk.vector
                def _(v):
                    fx = 0
                    for j in range(TT):
                        c = j * 8
                        f_ = fnt[:, (j % 2) * D : (j % 2 + 1) * D]
                        x2_ = x2l[:, (j % 2) * D : (j % 2 + 1) * D]
                        y_ = yb[:, (j % 2) * D : (j % 2 + 1) * D]
                        m = stt[:, c + 2 : c + 3]
                        s_tp.wait(v, j)
                        v.tensor_reduce(stt[:, c : c + 1], f_, AX.X,
                                        ALU.add).then_inc(s_fx, 1)
                        fx += 1
                        v.wait_ge(s_fx, fx)
                        v.tensor_scalar(m, stt[:, c : c + 1], 1.0 / D, None,
                                        ALU.mult).then_inc(s_v, 1)
                        v.wait_ge(s_v, 4 * j + 1)
                        v.tensor_mul(stt[:, c + 3 : c + 4], m, m).then_inc(s_fx, 1)
                        fx += 1
                        v.wait_ge(s_fx, fx)
                        v.tensor_scalar(stt[:, c + 3 : c + 4],
                                        stt[:, c + 3 : c + 4], -1.0, EPS,
                                        ALU.mult, ALU.add).then_inc(s_v, 1)
                        v.wait_ge(s_sc, 2 * (j + 1))
                        v.reciprocal(stt[:, c + 5 : c + 6],
                                     stt[:, c + 4 : c + 5]).then_inc(s_fx, 1)
                        fx += 1
                        v.wait_ge(s_fx, fx)
                        s_ld.wait(v, j)
                        if j >= 2:
                            s_st.wait(v, j - 2)
                        v.tensor_scalar(y_, f_, m, stt[:, c + 5 : c + 6],
                                        ALU.subtract, ALU.mult).then_inc(s_v, 1)
                        v.wait_ge(s_v, 4 * j + 3)
                        v.tensor_add(y_, y_, x2_).then_inc(s_v, 1)


    if debug:
        with nc.semaphore("s_dbg") as s_dbg, nc.Block() as blk:

            @blk.gpsimd
            def _(g):
                n = 0
                for name, (o, s) in dbg.items():
                    g.dma_start(out=o[:, :], in_=s[:, :]).then_inc(s_dbg, 16)
                    n += 16
                g.wait_ge(s_dbg, n)

    return nc


_NC_CACHE = None


def kernel(**inputs):
    global _NC_CACHE
    x = np.ascontiguousarray(np.asarray(inputs["L0_attn_x"], np.float32))
    wnames = {
        "A": "L0_attn_A", "B": "L0_attn_B", "C_w": "L0_attn_C_w",
        "C_b": "L0_attn_C_b", "D_w": "L0_attn_D_w", "D_b": "L0_attn_D_b",
        "log_delta": "L0_attn_log_delta", "w_gate": "L0_ffn_w_gate",
        "w_gate_up": "L0_ffn_w_gate_up", "w_down": "L0_ffn_w_down",
        "w_sh_gate_up": "L0_ffn_w_sh_gate_up", "w_sh_down": "L0_ffn_w_sh_down",
    }
    weights = {k: np.ascontiguousarray(np.asarray(inputs[v], np.float32))
               for k, v in wnames.items()}
    weights["C_w"] = np.ascontiguousarray(weights["C_w"].T)
    weights["D_w"] = np.ascontiguousarray(weights["D_w"].T)

    if _NC_CACHE is None:
        _NC_CACHE = build_nc()
    nc = _NC_CACHE

    in_maps = []
    for c in range(8):
        b, q = c // 4, c % 4
        shard = np.ascontiguousarray(x[b, q * TC : (q + 1) * TC, :])
        sel = np.zeros((3, 8), np.float32)
        for s_ in range(3):
            r = q - 1 - s_
            if r >= 0:
                sel[s_, 4 * b + r] = 1.0
        m = {"x": shard, "sel": sel}
        m.update(weights)
        in_maps.append(m)

    res = run_bass_kernel_spmd(nc, in_maps, core_ids=list(range(8)))
    out = np.empty((2, 8192, D), np.float32)
    for c in range(8):
        b, q = c // 4, c % 4
        out[b, q * TC : (q + 1) * TC, :] = res.results[c]["y"]
    return out


if __name__ == "__main__":
    nc = build_nc()
    print("built ok")
